# version 16
# baseline (speedup 1.0000x reference)
from contextlib import ExitStack

import numpy as np
import ml_dtypes

import concourse.bass as bass
import concourse.mybir as mybir
from concourse.bass_utils import run_bass_kernel_spmd
from concourse.replica_groups import maybe_share_collective_output_space

F32 = mybir.dt.float32
BF16 = mybir.dt.bfloat16
FP8 = mybir.dt.float8e4
AF = mybir.ActivationFunctionType
ALU = mybir.AluOpType
AX = mybir.AxisListType

HID = 2048
HK, HV, DK, DV, KC = 8, 16, 128, 128, 4
KEY, VAL, CONVD = HK * DK, HV * DV, 4096
TOK, HALO, TT, NTT = 1024, 8, 128, 8
C, NCH = 128, 8
NIN = 49
NI = 48
LN_EPS, L2_EPS, RMS_EPS = 1e-5, 1e-6, 1e-6

ENGS = ("sync", "gpsimd", "tensor", "vector", "scalar")
BSEM = {"sync": "Bs", "gpsimd": "Bg", "tensor": "Bt", "vector": "Bv",
        "scalar": "Ba"}


class KB:
    ESEM = {"vector": "dv", "scalar": "ac"}

    def __init__(self, nc):
        self.nc = nc
        self.streams = {e: [] for e in ENGS}
        self.cnt = {}
        self.hwm = {}
        self.sems = {}
        self.psum_free = {}
        self.wr_hist = {e: [] for e in ENGS}

    @staticmethod
    def _ap_names(vals):
        out = set()
        for v in vals:
            t = getattr(v, "tensor", None)
            if t is not None:
                lo = v.offset
                hi = lo + sum((n - 1) * st for st, n in v.ap) + 1
                out.add((t.name, lo, hi))
        return out

    @staticmethod
    def _overlaps(s1, s2):
        for (n1, a1, b1) in s1:
            for (n2, a2, b2) in s2:
                if n1 == n2 and a1 < b2 and a2 < b1:
                    return True
        return False

    def _hazard_guard(self, eng, meth, a, k):
        if eng not in self.ESEM:
            return
        writes = self._ap_names(a[:1])
        if "accum_out" in k and k["accum_out"] is not None:
            writes |= self._ap_names([k["accum_out"]])
        reads = self._ap_names(a[1:]) | self._ap_names(
            v for kk, v in k.items() if kk != "accum_out")
        hist = self.wr_hist[eng]
        need = False
        for dist in (1, 2):
            if len(hist) >= dist:
                idx, wset, rset = hist[-dist]
                if (self._overlaps(reads, wset) or
                        self._overlaps(writes, wset) or
                        self._overlaps(writes, rset)):
                    need = True
                    writer_idx = idx
                    break
        if need:
            ent = self.streams[eng][writer_idx]
            tail = self.streams[eng][writer_idx + 1:]
            if ent[4]:
                sem = ent[4][-1][0]
            else:
                sem = "hz" + eng[0]
                ent[4].append((sem, 1))
                self.cnt[sem] += 1
            after = sum(n for it in tail if it[0] == "op"
                        for (sm, n) in it[4] if sm == sem)
            self.wait(eng, (sem, self.cnt[sem] - after))
            sp = getattr(self, "spacer", None)
            if sp and eng in sp:
                meth_s, args_s = sp[eng]
                for _ in range(2):
                    self.streams[eng].append(["op", meth_s, args_s, {}, []])
            self.wr_hist[eng] = []
        self.wr_hist[eng].append((len(self.streams[eng]), writes, reads))
        if len(self.wr_hist[eng]) > 2:
            self.wr_hist[eng] = self.wr_hist[eng][-2:]

    def ckpt(self, name):
        if not hasattr(self, "ckpts"):
            self.ckpts = {}
        self.ckpts[name] = {e: len(s) for e, s in self.streams.items()}

    def cut_at(self, name):
        marks = self.ckpts[name]
        for e in ENGS:
            self.streams[e] = self.streams[e][:marks[e]]

    def new_sem(self, stack, name):
        self.sems[name] = stack.enter_context(self.nc.semaphore(name))
        self.cnt[name] = 0

    def do(self, eng, meth, *a, **k):
        self._hazard_guard(eng, meth, a, k)
        self.streams[eng].append(["op", meth, a, k, []])

    def mark(self, eng, sem, n=1):
        assert self.streams[eng] and self.streams[eng][-1][0] == "op", \
            f"mark on empty/wait stream {eng}"
        self.streams[eng][-1][4].append((sem, n))
        self.cnt[sem] += n
        return (sem, self.cnt[sem])

    def tok(self, sem):
        return (sem, self.cnt[sem])

    def wait_p16(self, eng, token):
        if token is None:
            return
        self.wait(eng, (token[0], token[1] + 16))

    def wait(self, eng, token):
        if token is None:
            return
        sem, v = token
        if v <= 0 or self.hwm.get((eng, sem), 0) >= v:
            return
        self.hwm[(eng, sem)] = v
        self.streams[eng].append(["wait", sem, v])

    def barrier(self):
        for e in ("sync", "gpsimd"):
            st = self.streams[e]
            idx = len(st) - 1
            while idx >= 0 and st[idx][0] != "op":
                idx -= 1
            if idx >= 0 and st[idx][1] == "dma_start" and st[idx][4]:
                sem = st[idx][4][-1][0]
                self.streams[e].append(
                    ["op", "dma_start", st[idx][2], dict(st[idx][3]), []])
                self.mark(e, sem, 16)
        toks = []
        for e in ENGS:
            idx = len(self.streams[e]) - 1
            while idx >= 0 and self.streams[e][idx][0] != "op":
                idx -= 1
            if idx < 0:
                toks.append(self.tok(BSEM[e]))
                continue
            ent = self.streams[e][idx]
            if ent[4]:
                toks.append((ent[4][-1][0], self.cnt[ent[4][-1][0]]))
            else:
                n = 16 if e in ("sync", "gpsimd") else 1
                ent[4].append((BSEM[e], n))
                self.cnt[BSEM[e]] += n
                toks.append((BSEM[e], self.cnt[BSEM[e]]))
        for e in ENGS:
            for t in toks:
                self.wait(e, t)

    def psum_wait(self, bank):
        for t in self.psum_free.get(bank, []):
            self.wait("tensor", t)

    def psum_consumed(self, bank, token, add=False):
        if add:
            self.psum_free.setdefault(bank, []).append(token)
        else:
            self.psum_free[bank] = [token]

    def emit(self, block):
        def mk(stream):
            def body(e):
                for it in stream:
                    if it[0] == "wait":
                        e.wait_ge(self.sems[it[1]], it[2])
                    else:
                        _, meth, a, k, incs = it
                        inst = getattr(e, meth)(*a, **k)
                        for (s, n) in incs:
                            inst = inst.then_inc(self.sems[s], n)
            return body
        for e in ENGS:
            getattr(block, e)(mk(self.streams[e]))


def build_nc(debug=False, cut=None, kb_sink=None, debug2=False):
    nc = bass.Bass("TRN2", target_bir_lowering=False)
    dbg_list = []

    def din(name, shape, dt=F32):
        return nc.dram_tensor(name, list(shape), dt, kind="ExternalInput").ap()

    x_main = din("x_main", (TOK, HID))
    x_halo = din("x_halo", (HALO, HID))
    flag = din("flag", (1, 1))
    sel8 = din("sel8", (1, 8))
    w_in = din("w_in", (NIN, 2, 128, 1024))
    b_in = din("b_in", (128, NIN))
    conv_w = din("conv_w", (128, 32 * KC))
    conv_b = din("conv_b", (128, 32))
    small = din("small", (16, 8))
    norm_w = din("norm_w", (128, 1))
    wout_b = din("wout_b", (16, 2, 128, 1024))
    bout_c = din("bout_c", (128, 16))
    gate_w = din("gate_w", (NI, 2, 128, 1024))
    up_w = din("up_w", (NI, 2, 128, 1024))
    down_in = din("down_in", (NI, 2, 128, 1024))
    g_scale = din("g_scale", (128, NI))
    u_scale = din("u_scale", (128, NI))
    d_scale = din("d_scale", (128, 16))
    i128b = din("i128b", (128, 128), BF16)
    i16f = din("i16f", (16, 16))
    selhb = din("selhb", (16, 16 * 128), BF16)
    selhf = din("selhf", (16, 16 * 128))
    ones16f = din("ones16f", (16, 128))
    onesb = din("onesb", (128, 1), BF16)
    onesrb = din("onesrb", (1, 128), BF16)
    onesrf = din("onesrf", (1, 128))
    mask_lo = din("mask_lo", (128, 128), BF16)
    mask_ue = din("mask_ue", (128, 128), BF16)
    rmask = din("rmask", (16, TOK))
    e127 = din("e127", (128, 1))
    csts = din("csts", (1, 16))
    cstc = din("cstc", (128, 16))

    out_x = nc.dram_tensor("out_x", [TOK, HID], F32, kind="ExternalOutput").ap()

    def dscr(name, shape, dt=F32, **kw):
        return nc.dram_tensor(name, list(shape), dt, kind="Internal", **kw).ap()

    pan_hbm = dscr("pan_hbm", (NCH, 128, HV * 256), BF16)
    gt_hbm = dscr("gt_hbm", (NCH, 128, HV * 128), BF16)
    kt_hbm = dscr("kt_hbm", (NCH, 128, HK * 128), BF16)
    z_hbm = dscr("z_hbm", (HV, 128, TOK), BF16)
    x1_hbm = dscr("x1_hbm", (TOK, HID), F32)
    ln3_hbm = dscr("ln3_hbm", (2, 16, 128, 512), BF16)
    gw_bf = dscr("gw_bf", (NI, 128, 2048), BF16)
    uw_bf = dscr("uw_bf", (NI, 128, 2048), BF16)
    dn_bf = dscr("dn_bf", (NI, 128, 2048), BF16)
    f8mid = dscr("f8mid", (3, 2, 128, 1024), FP8)
    rg = [[0, 1, 2, 3, 4, 5, 6, 7]]
    aspace = "Local"
    cc_in = dscr("cc_in", (128, HV * DV), BF16)
    cc_out = dscr("cc_out", (8 * 128, HV * DV), BF16, addr_space=aspace)

    with ExitStack() as st:
        E = st.enter_context
        kb = KB(nc)
        for s in ("wdma", "gdma", "pe", "dv", "ac",
                  "Bs", "Bg", "Bt", "Bv", "Ba", "hzv", "hzs"):
            kb.new_sem(st, s)

        pb = [E(nc.psum_tensor(f"pb{i}", [128, 512], F32)) for i in range(8)]

        def sb(n, sh, dt):
            return E(nc.sbuf_tensor(n, list(sh), dt))

        s_state = [sb(f"s_state{i}", (128, HV * DV), BF16) for i in range(2)]
        s_cst = sb("s_cst", (1, 16), F32)
        s_cstc = sb("s_cstc", (128, 16), F32)
        s_small = sb("s_small", (16, 8), F32)
        s_bias = sb("s_bias", (128, NIN), F32)
        s_convw = sb("s_convw", (128, 32 * KC), F32)
        s_convb = sb("s_convb", (128, 32), F32)
        s_normw = sb("s_normw", (128, 1), F32)
        s_boutc = sb("s_boutc", (128, 16), F32)
        s_gsc = sb("s_gsc", (128, NI), F32)
        s_usc = sb("s_usc", (128, NI), F32)
        s_dsc = sb("s_dsc", (128, 16), F32)
        s_i128b = sb("s_i128b", (128, 128), BF16)
        s_i16f = sb("s_i16f", (16, 16), F32)
        s_ones16f = sb("s_ones16f", (16, 128), F32)
        s_onesb = sb("s_onesb", (128, 1), BF16)
        s_onesrb = sb("s_onesrb", (1, 128), BF16)
        s_onesrf = sb("s_onesrf", (1, 128), F32)
        s_mlo = sb("s_mlo", (128, 128), BF16)
        s_mue = sb("s_mue", (128, 128), BF16)
        s_rmask = sb("s_rmask", (16, TOK), F32)
        s_e127 = sb("s_e127", (128, 1), F32)
        s_flag = sb("s_flag", (1, 1), F32)
        s_sel8 = sb("s_sel8", (1, 8), F32)
        s_flagcol = sb("s_flagcol", (128, 1), F32)
        s_selcol = sb("s_selcol", (128, 8), F32)
        s_negea = sb("s_negea", (16, 1), F32)
        s_beta = sb("s_beta", (16, TOK), F32)
        s_g = sb("s_g", (16, TOK), F32)
        s_c = sb("s_c", (16, TOK), F32)
        s_arow = sb("s_arow", (16, TOK), BF16)
        s_cT = sb("s_cT", (128, NCH * 16), F32)
        s_bT = sb("s_bT", (128, NCH * 16), F32)
        s_bTn = sb("s_bTn", (128, NCH * 16), F32)
        s_bA = sb("s_bA", (128, NCH * 16), F32)
        s_kbar = sb("s_kbar", (128, NCH * 16), F32)
        s_acb = sb("s_acb", (128, NCH * 16), F32)
        s_dg16 = sb("s_dg16", (16, 16), F32)
        s_sqd = sb("s_sqd", (128, 1024), BF16)
        wf = [sb(f"wf{i}", (128, 1024), F32) for i in range(2)]
        wb = [sb(f"wb{i}", (128, 2048), BF16) for i in range(2)]

        w8e = {"vector": sb("w8v", (128, 1024), FP8),
               "scalar": sb("w8s", (128, 1024), FP8)}
        s_rows = sb("s_rows", (1, TOK), F32)
        s_rowb = sb("s_rowb", (1, TOK), BF16)
        s_spc = sb("s_spc", (1, 8), F32)
        s_col = sb("s_col", (128, 4), F32)
        s_edf = sb("s_edf", (128, 128), F32)

        wf_free = [None, None]
        wb_free = [None, None]

        kb.spacer = {"vector": ("memset", (s_spc[0:1, 0:2], 0.0)),
                     "scalar": ("copy", (s_spc[0:1, 2:3], s_spc[0:1, 3:4]))}

        def dbg(name, ap, shape, dt):
            if not debug:
                return
            d = nc.dram_tensor("dbg_" + name, list(shape), dt,
                               kind="ExternalOutput").ap()
            kb.barrier()
            kb.do("gpsimd", "dma_start", d, ap)
            kb.mark("gpsimd", "gdma", 16)
            kb.barrier()

        def snap(name, ap, shape, dt):
            if not debug2:
                return
            d = nc.dram_tensor("snap_" + name, list(shape), dt,
                               kind="ExternalOutput").ap()
            kb.do("gpsimd", "dma_start", d, ap)
            kb.mark("gpsimd", "gdma", 16)

        lw_half = [0]

        def load_wblk(src_ap, widx, cvt_fp8, parity):
            eng = "vector" if parity == 0 else "scalar"
            sem = "dv" if parity == 0 else "ac"
            meth = "tensor_copy" if eng == "vector" else "copy"
            for hh in range(2):
                wfi = lw_half[0] & 1
                lw_half[0] += 1
                kb.wait("sync", wf_free[wfi])
                kb.do("sync", "dma_start", wf[wfi][:], src_ap[hh])
                kb.mark("sync", "wdma", 16)
                tw = kb.tok("wdma")
                kb.wait_p16(eng, tw)
                if hh == 0:
                    kb.wait(eng, wb_free[widx])
                if cvt_fp8:
                    w8 = w8e[eng]
                    kb.do(eng, meth, w8[:], wf[wfi][:])
                    kb.do(eng, meth,
                          wb[widx][:, hh * 1024:(hh + 1) * 1024], w8[:])
                else:
                    kb.do(eng, meth,
                          wb[widx][:, hh * 1024:(hh + 1) * 1024], wf[wfi][:])
                t = kb.mark(eng, sem)
                wf_free[wfi] = t
            return t

        conv_seq = []
        for ib in range(NI):
            conv_seq.append((gate_w[ib],
                             gw_bf[ib].rearrange("p (h m) -> h p m", h=2)))
            conv_seq.append((up_w[ib],
                             uw_bf[ib].rearrange("p (h m) -> h p m", h=2)))
            conv_seq.append((down_in[ib],
                             dn_bf[ib].rearrange("p (h m) -> h p m", h=2)))
        conv_state = {"i": 0}

        conv_emitted = {"d1": [], "d2": 0}

        def conv_pump(nblk=1):
            for _ in range(nblk):
                i = conv_state["i"]
                n = len(conv_seq)
                if i >= n + 1:
                    return
                conv_state["i"] = i + 1
                if i < n:
                    src, dst = conv_seq[i]
                    if i >= 3:
                        kb.wait("gpsimd", conv_emitted["d1"][i - 1])
                    kb.do("gpsimd", "dma_start", f8mid[i % 3], src)
                    kb.mark("gpsimd", "gdma", 16)
                    conv_emitted["d1"].append(kb.tok("gdma"))
                j = conv_emitted["d2"]
                if j < n - 1 and j == i - 1:
                    kb.wait("gpsimd", conv_emitted["d1"][j + 1])
                    kb.do("gpsimd", "dma_start", conv_seq[j][1],
                          f8mid[j % 3])
                    kb.mark("gpsimd", "gdma", 16)
                    conv_emitted["d2"] = j + 1
                if i == n:
                    kb.do("gpsimd", "dma_start", f8mid[(n - 1) % 3],
                          conv_seq[n - 1][0])
                    kb.mark("gpsimd", "gdma", 16)
                    kb.wait("gpsimd", kb.tok("gdma"))
                    kb.do("gpsimd", "dma_start", conv_seq[n - 1][1],
                          f8mid[(n - 1) % 3])
                    kb.mark("gpsimd", "gdma", 16)
                    conv_emitted["d2"] = n

        ln_state = {"prev": None}

        def ln_apply(xb, nrows, out_ap, stat,
                     add_ap=None, out_f32=None):
            kb.wait("scalar", ln_state["prev"])
            kb.do("vector", "reduce_sum", stat[0:nrows, 0:1],
                  xb[0:nrows, :], AX.X)
            kb.do("scalar", "activation", s_sqd[0:nrows, :],
                  xb[0:nrows, 0:1024], AF.Square,
                  accum_out=stat[0:nrows, 1:2])
            kb.do("scalar", "activation", s_sqd[0:nrows, :],
                  xb[0:nrows, 1024:2048], AF.Square,
                  accum_out=stat[0:nrows, 7:8])
            ta = kb.mark("scalar", "ac")
            kb.do("vector", "tensor_scalar_mul", stat[0:nrows, 2:3],
                  stat[0:nrows, 0:1], 1.0 / HID)
            kb.wait("vector", ta)
            kb.do("vector", "tensor_tensor", stat[0:nrows, 1:2],
                  stat[0:nrows, 1:2], stat[0:nrows, 7:8], ALU.add)
            kb.do("vector", "tensor_scalar_mul", stat[0:nrows, 3:4],
                  stat[0:nrows, 1:2], 1.0 / HID)
            kb.do("vector", "tensor_tensor", stat[0:nrows, 4:5],
                  stat[0:nrows, 2:3], stat[0:nrows, 2:3], ALU.mult)
            kb.do("vector", "tensor_tensor", stat[0:nrows, 4:5],
                  stat[0:nrows, 3:4], stat[0:nrows, 4:5], ALU.subtract)
            tv = kb.mark("vector", "dv")
            kb.wait("scalar", tv)
            kb.do("scalar", "activation", stat[0:nrows, 5:6],
                  stat[0:nrows, 4:5], AF.Sqrt, bias=s_cstc[0:nrows, 0:1])
            ta = kb.mark("scalar", "ac")
            kb.wait("vector", ta)
            kb.do("vector", "reciprocal", stat[0:nrows, 6:7],
                  stat[0:nrows, 5:6])
            if out_ap is not None:
                kb.do("vector", "tensor_scalar", out_ap,
                      xb[0:nrows, :], stat[0:nrows, 2:3], stat[0:nrows, 6:7],
                      ALU.subtract, ALU.mult)
            if add_ap is not None:
                kb.do("vector", "tensor_scalar", out_f32,
                      xb[0:nrows, :], stat[0:nrows, 2:3], stat[0:nrows, 6:7],
                      ALU.subtract, ALU.mult)
                kb.do("vector", "tensor_tensor", out_f32, out_f32, add_ap,
                      ALU.add)
            t = kb.mark("vector", "dv")
            ln_state["prev"] = t
            return t

        for dst, src in (
            (s_cst[:], csts), (s_cstc[:], cstc),
            (s_small[:], small), (s_bias[:], b_in),
            (s_convw[:], conv_w), (s_convb[:], conv_b), (s_normw[:], norm_w),
            (s_boutc[:], bout_c), (s_gsc[:], g_scale), (s_usc[:], u_scale),
            (s_dsc[:], d_scale), (s_i128b[:], i128b), (s_i16f[:], i16f),
            (s_ones16f[:], ones16f),
            (s_onesb[:], onesb), (s_onesrb[:], onesrb), (s_onesrf[:], onesrf),
            (s_mlo[:], mask_lo), (s_mue[:], mask_ue), (s_rmask[:], rmask),
            (s_e127[:], e127), (s_flag[:], flag), (s_sel8[:], sel8),
        ):
            kb.do("sync", "dma_start", dst, src)
            kb.mark("sync", "wdma", 16)
        kb.do("sync", "dma_start", s_cst[:], csts)
        kb.mark("sync", "wdma", 16)
        tc0 = kb.tok("wdma")
        for e in ("vector", "scalar", "tensor"):
            kb.wait(e, tc0)
        kb.do("scalar", "activation", s_negea[:, 0:1], s_small[:, 0:1], AF.Exp)
        ta = kb.mark("scalar", "ac")
        kb.wait("vector", ta)
        kb.do("vector", "tensor_scalar_mul", s_negea[:, 0:1],
              s_negea[:, 0:1], -1.0)
        kb.do("vector", "memset", s_state[0][:], 0.0)
        kb.do("vector", "memset", s_state[1][:], 0.0)
        kb.psum_wait(4)
        kb.do("tensor", "matmul", pb[4][:, 0:1], s_onesrf[:], s_flag[:],
              start=True, stop=True)
        kb.do("tensor", "matmul", pb[4][:, 8:16], s_onesrf[:], s_sel8[:],
              start=True, stop=True)
        tp = kb.mark("tensor", "pe")
        kb.wait("vector", tp)
        kb.do("vector", "tensor_copy", s_flagcol[:], pb[4][:, 0:1])
        kb.do("vector", "tensor_copy", s_selcol[:], pb[4][:, 8:16])
        tv = kb.mark("vector", "dv")
        kb.psum_consumed(4, tv)
        kb.barrier()
        kb.ckpt("p0")

        qkv_guard = nc.sbuf_tensor("qkv_s", [128, 32 * TOK], BF16)
        qkv_parent = qkv_guard.__enter__()
        qkv = [qkv_parent[:, j * TOK:(j + 1) * TOK] for j in range(32)]
        core = [qkv_parent[:, (16 + j) * TOK:(17 + j) * TOK]
                for j in range(16)]
        lnx_guards = [nc.sbuf_tensor(f"lnx{j}", [128, HALO + TOK], BF16)
                      for j in range(16)]
        lnx = [g.__enter__() for g in lnx_guards]
        if True:

            with nc.sbuf_tensor("xt0", [128, HID], F32) as xt0, \
                 nc.sbuf_tensor("xt1", [128, HID], F32) as xt1, \
                 nc.sbuf_tensor("lntok", [128, HID], BF16) as lntok, \
                 nc.sbuf_tensor("stat1", [128, 8], F32) as stat1:
                xb_ = [xt0, xt1]
                xfree = [None, None]
                lntok_free = [None]

                def p1_tile(xb, nrows, coloff):
                    kb.wait("vector", lntok_free[0])
                    tln = ln_apply(xb, nrows, lntok[0:nrows, :], stat1)
                    xb_done = (kb.tok("dv"), kb.tok("ac"))
                    for j in range(16):
                        bank = 2 + (j & 1)
                        kb.wait("tensor", tln)
                        kb.psum_wait(bank)
                        rhs = s_i128b[:] if nrows == 128 \
                            else s_i128b[0:nrows, 0:nrows]
                        kb.do("tensor", "matmul", pb[bank][0:128, 0:nrows],
                              lntok[0:nrows, j * 128:(j + 1) * 128], rhs,
                              start=True, stop=True)
                        tpj = kb.mark("tensor", "pe")
                        kb.wait("scalar", tpj)
                        kb.do("scalar", "copy",
                              lnx[j][:, coloff:coloff + nrows],
                              pb[bank][0:128, 0:nrows])
                        taj = kb.mark("scalar", "ac")
                        kb.psum_consumed(bank, taj)
                    lntok_free[0] = kb.tok("pe")
                    return xb_done

                kb.do("sync", "dma_start", xt0[0:HALO, :], x_halo)
                kb.mark("sync", "wdma", 16)
                kb.do("sync", "dma_start", xt0[0:HALO, :], x_halo)
                kb.mark("sync", "wdma", 16)
                th = kb.tok("wdma")
                kb.wait("vector", th)
                kb.wait("scalar", th)
                xfree[0] = p1_tile(xt0, HALO, 0)
                for t in range(NTT):
                    b = t & 1
                    if xfree[b] is not None:
                        for tkn in xfree[b]:
                            kb.wait("sync", tkn)
                    kb.do("sync", "dma_start", xb_[b][:],
                          x_main[t * TT:(t + 1) * TT, :])
                    kb.mark("sync", "wdma", 16)
                    tw = kb.tok("wdma")
                    kb.wait_p16("vector", tw)
                    kb.wait_p16("scalar", tw)
                    xfree[b] = p1_tile(xb_[b], 128, HALO + t * TT)
            kb.barrier()
            kb.ckpt("p1")
            snap("lnx0", lnx[0][:], (128, HALO + TOK), BF16)
            snap("lnx8", lnx[8][:], (128, HALO + TOK), BF16)
            if debug:
                dbg("lnx0", lnx[0][:], (128, HALO + TOK), BF16)
                dbg("lnx15", lnx[15][:], (128, HALO + TOK), BF16)

            with ExitStack() as sc_p2:
                E2 = sc_p2.enter_context
                mixb = [E2(nc.sbuf_tensor(f"mix{i}", [128, HALO + TOK], BF16))
                        for i in range(2)]
                cac = [E2(nc.sbuf_tensor(f"cac{i}", [128, TOK], BF16))
                       for i in range(2)]
                silu_f = E2(nc.sbuf_tensor("silu_f", [128, TOK], BF16))
                sq2 = E2(nc.sbuf_tensor("sq2", [128, TOK], BF16))
                ba_raw = E2(nc.sbuf_tensor("ba_raw", [48, TOK], F32))
                zst = [E2(nc.sbuf_tensor(f"zst{i}", [128, TOK], BF16))
                       for i in range(2)]

                mix_free = [None, None]
                rows_free = [None]
                zst_free = [None, None]
                silu_free = None
                sq2_free = None
                cac_free = [None, None]

                wtok = load_wblk(w_in[0], 0, False, 0)
                for o in range(NIN):
                    widx = o & 1
                    nxt = None
                    if o + 1 < NIN:
                        nxt = load_wblk(
                            w_in[o + 1], (o + 1) & 1, False, (o + 1) & 1)
                    kb.wait("tensor", wtok)
                    is_mix, is_z = o < 32, 32 <= o < 48
                    mt = mixb[o & 1] if is_mix else None
                    slices = ([(0, HALO)] if is_mix else []) + \
                        [(HALO, 512), (HALO + 512, 512)]
                    for si, (soff, nn) in enumerate(slices):
                        bank = (2 * o + si) % 4
                        kb.psum_wait(bank)
                        for kk in range(16):
                            kb.do("tensor", "matmul", pb[bank][0:128, 0:nn],
                                  wb[widx][:, kk * 128:(kk + 1) * 128],
                                  lnx[kk][:, soff:soff + nn],
                                  start=(kk == 0), stop=(kk == 15))
                        tpp = kb.mark("tensor", "pe")
                        wb_free[widx] = tpp
                        if is_mix:
                            kb.wait("vector", tpp)
                            if si == 0:
                                kb.wait("vector", mix_free[o & 1])
                            kb.do("vector", "tensor_scalar",
                                  mt[:, soff:soff + nn],
                                  pb[bank][0:128, 0:nn],
                                  s_bias[:, o:o + 1], None, ALU.add)
                            te = kb.mark("vector", "dv")
                        elif is_z:
                            kb.wait("scalar", tpp)
                            if si == 0:
                                kb.wait("scalar", zst_free[o & 1])
                            kb.do("scalar", "activation",
                                  zst[o & 1][:, soff - HALO:soff - HALO + nn],
                                  pb[bank][0:128, 0:nn], AF.Silu,
                                  bias=s_bias[:, o:o + 1])
                            te = kb.mark("scalar", "ac")
                        else:
                            kb.wait("vector", tpp)
                            kb.do("vector", "tensor_scalar",
                                  ba_raw[0:48, soff - HALO:soff - HALO + nn],
                                  pb[bank][0:48, 0:nn],
                                  s_bias[0:48, o:o + 1], None, ALU.add)
                            te = kb.mark("vector", "dv")
                        kb.psum_consumed(bank, te)
                    if is_z:
                        tz = kb.tok("ac")
                        kb.wait("gpsimd", tz)
                        kb.do("gpsimd", "dma_start",
                              z_hbm[o - 32],
                              zst[o & 1][:])
                        zst_free[o & 1] = kb.mark("gpsimd", "gdma", 16)
                    if is_mix:
                        kb.do("vector", "tensor_scalar_mul", mt[:, 0:HALO],
                              mt[:, 0:HALO], s_flagcol[:, 0:1])
                        kb.wait("vector", cac_free[0])
                        kb.do("vector", "tensor_scalar", cac[0][:],
                              mt[:, HALO:HALO + TOK],
                              s_convw[:, 4 * o + 3:4 * o + 4], None, ALU.mult)
                        kb.wait("vector", cac_free[1])
                        for d in (1, 2, 3):
                            src_acc = cac[(d + 1) & 1]
                            dst_acc = cac[d & 1]
                            kb.do("vector", "scalar_tensor_tensor",
                                  dst_acc[:], mt[:, HALO - d:HALO - d + TOK],
                                  s_convw[:, 4 * o + 3 - d:4 * o + 4 - d],
                                  src_acc[:], ALU.mult, ALU.add)
                        tcv = kb.mark("vector", "dv")
                        mix_free[o & 1] = tcv
                        acc = cac[1]
                        kb.wait("scalar", tcv)
                        if o >= 16:
                            kb.do("scalar", "activation", qkv[o], acc[:],
                                  AF.Silu, bias=s_convb[:, o:o + 1])
                            cac_free[1] = kb.mark("scalar", "ac")
                            cac_free[0] = cac_free[1]
                        else:
                            kb.wait("scalar", silu_free)
                            kb.do("scalar", "activation", silu_f[:], acc[:],
                                  AF.Silu, bias=s_convb[:, o:o + 1])
                            cac_free[1] = kb.mark("scalar", "ac")
                            cac_free[0] = cac_free[1]
                            kb.wait("scalar", sq2_free)
                            kb.do("scalar", "activation", sq2[:], silu_f[:],
                                  AF.Square)
                            tsq = kb.mark("scalar", "ac")
                            kb.wait("tensor", tsq)
                            kb.psum_wait(6)
                            kb.do("tensor", "matmul", pb[6][0:1, 0:512],
                                  s_onesb[:], sq2[:, 0:512],
                                  start=True, stop=True)
                            kb.psum_wait(5)
                            kb.do("tensor", "matmul", pb[5][0:1, 0:512],
                                  s_onesb[:], sq2[:, 512:1024],
                                  start=True, stop=True)
                            tp6 = kb.mark("tensor", "pe")
                            sq2_free = tp6
                            sc_i = 1 if o < 8 else 2
                            kb.wait("scalar", tp6)
                            kb.wait("scalar", rows_free[0])
                            kb.do("scalar", "activation", s_rows[0:1, 0:512],
                                  pb[6][0:1, 0:512], AF.Sqrt,
                                  scale=s_cst[0:1, sc_i:sc_i + 1],
                                  bias=s_cst[0:1, sc_i + 2:sc_i + 3])
                            kb.do("scalar", "activation", s_rows[0:1, 512:1024],
                                  pb[5][0:1, 0:512], AF.Sqrt,
                                  scale=s_cst[0:1, sc_i:sc_i + 1],
                                  bias=s_cst[0:1, sc_i + 2:sc_i + 3])
                            ta6 = kb.mark("scalar", "ac")
                            kb.psum_consumed(6, ta6)
                            kb.psum_consumed(5, ta6)
                            kb.wait("vector", ta6)
                            kb.do("vector", "reciprocal", s_rows[0:1, :],
                                  s_rows[0:1, :])
                            kb.do("vector", "tensor_copy", s_rowb[0:1, :],
                                  s_rows[0:1, :])
                            trr = kb.mark("vector", "dv")
                            rows_free[0] = trr
                            kb.wait("tensor", trr)
                            kb.psum_wait(6)
                            kb.do("tensor", "matmul", pb[6][:, 0:512],
                                  s_onesrb[:], s_rowb[0:1, 0:512],
                                  start=True, stop=True)
                            kb.psum_wait(5)
                            kb.do("tensor", "matmul", pb[5][:, 0:512],
                                  s_onesrb[:], s_rowb[0:1, 512:1024],
                                  start=True, stop=True)
                            tph = kb.mark("tensor", "pe")
                            kb.wait("vector", tph)
                            kb.do("vector", "tensor_tensor",
                                  qkv[o][:, 0:512], silu_f[:, 0:512],
                                  pb[6][:, 0:512], ALU.mult)
                            kb.do("vector", "tensor_tensor",
                                  qkv[o][:, 512:1024], silu_f[:, 512:1024],
                                  pb[5][:, 0:512], ALU.mult)
                            tqn = kb.mark("vector", "dv")
                            silu_free = tqn
                            kb.psum_consumed(6, tqn)
                            kb.psum_consumed(5, tqn)
                    wtok = nxt
                kb.barrier()

                kb.do("scalar", "activation", s_beta[:], ba_raw[0:16, :],
                      AF.Sigmoid, bias=s_small[:, 2:3])
                kb.do("vector", "tensor_scalar", s_g[:], ba_raw[32:48, :],
                      s_small[:, 1:2], None, ALU.add)
                tg = kb.mark("vector", "dv")
                kb.wait("scalar", tg)
                kb.do("scalar", "activation", ba_raw[0:16, :],
                      s_g[:], AF.Abs)
                t1 = kb.mark("scalar", "ac")
                kb.do("scalar", "activation", ba_raw[0:16, :],
                      ba_raw[0:16, :], AF.Exp, scale=s_cstc[0:16, 5:6])
                kb.do("scalar", "activation", ba_raw[0:16, :],
                      ba_raw[0:16, :], AF.Ln, bias=s_cstc[0:16, 6:7])
                t2 = kb.mark("scalar", "ac")
                kb.wait("vector", t2)
                kb.do("vector", "scalar_tensor_tensor", ba_raw[0:16, :],
                      s_g[:], 0.0, ba_raw[0:16, :], ALU.max, ALU.add)
                kb.do("vector", "tensor_scalar", s_g[:], ba_raw[0:16, :],
                      s_negea[:, 0:1], None, ALU.mult)
                kb.do("vector", "tensor_tensor_scan", s_c[:], s_rmask[:],
                      s_g[:], 0.0, ALU.mult, ALU.add)
                tcs = kb.mark("vector", "dv")
                kb.wait("scalar", tcs)
                kb.do("scalar", "activation", s_arow[:], s_c[:], AF.Exp)
                kb.mark("scalar", "ac")
                kb.barrier()
                kb.ckpt("p2")
                snap("qkv0", qkv[0], (128, TOK), BF16)
                snap("qkv8", qkv[8], (128, TOK), BF16)
                snap("qkv16", qkv[16], (128, TOK), BF16)
                snap("qkv31", qkv[31], (128, TOK), BF16)
                snap("beta", s_beta[:], (16, TOK), F32)
                snap("g", s_g[:], (16, TOK), F32)
                snap("arow", s_arow[:], (16, TOK), BF16)
                if debug:
                    dbg("beta", s_beta[:], (16, TOK), F32)
                    dbg("g", s_g[:], (16, TOK), F32)
                    dbg("c", s_c[:], (16, TOK), F32)
                    dbg("qkv0", qkv[0], (128, TOK), BF16)
                    dbg("qkv8", qkv[8], (128, TOK), BF16)
                    dbg("qkv16", qkv[16], (128, TOK), BF16)
                    dbg("qkv31", qkv[31], (128, TOK), BF16)

            for g in reversed(lnx_guards):
                g.__exit__(None, None, None)

            sel_guards = [nc.sbuf_tensor("s_selhf", [16, 16 * 128], F32),
                          nc.sbuf_tensor("s_selhb", [16, 16 * 128], BF16)]
            s_selhf, s_selhb = [g.__enter__() for g in sel_guards]
            kb.do("sync", "dma_start", s_selhf[:], selhf)
            kb.mark("sync", "wdma", 16)
            kb.do("sync", "dma_start", s_selhb[:], selhb)
            kb.mark("sync", "wdma", 16)
            tsel_ld = kb.tok("wdma")
            kb.wait_p16("tensor", tsel_ld)

            for i in range(NCH):
                conv_pump(2)
                cs = slice(i * C, (i + 1) * C)
                co = i * 16
                kb.psum_wait(2)
                kb.do("tensor", "matmul", pb[2][0:128, 0:16],
                      s_c[:, cs], s_i16f[:], start=True, stop=True)
                kb.do("tensor", "matmul", pb[2][0:128, 16:32],
                      s_beta[:, cs], s_i16f[:], start=True, stop=True)
                tpc = kb.mark("tensor", "pe")
                kb.wait("vector", tpc)
                kb.do("vector", "tensor_copy", s_cT[:, co:co + 16],
                      pb[2][0:128, 0:16])
                kb.do("vector", "tensor_copy", s_bT[:, co:co + 16],
                      pb[2][0:128, 16:32])
                kb.do("vector", "tensor_scalar_mul", s_bTn[:, co:co + 16],
                      pb[2][0:128, 16:32], -1.0)
                tvc = kb.mark("vector", "dv")
                kb.psum_consumed(2, tvc)
                kb.wait("scalar", tvc)
                kb.do("scalar", "activation", s_bA[:, co:co + 16],
                      s_cT[:, co:co + 16], AF.Exp)
                tac = kb.mark("scalar", "ac")
                kb.wait("vector", tac)
                kb.do("vector", "tensor_tensor", s_bA[:, co:co + 16],
                      s_bT[:, co:co + 16], s_bA[:, co:co + 16], ALU.mult)
                kb.wait("tensor", tvc)
                kb.psum_wait(3)
                kb.do("tensor", "matmul", pb[3][0:16, 0:1],
                      s_cT[:, co:co + 16], s_e127[:],
                      start=True, stop=True)
                tp3 = kb.mark("tensor", "pe")
                kb.wait("vector", tp3)
                kb.do("vector", "tensor_copy", s_col[0:16, 0:1],
                      pb[3][0:16, 0:1])
                tv3 = kb.mark("vector", "dv")
                kb.psum_consumed(3, tv3)
                kb.do("vector", "tensor_scalar", s_dg16[:],
                      s_i16f[:], s_col[0:16, 0:1], None, ALU.mult)
                tdg = kb.mark("vector", "dv")
                kb.wait("tensor", tdg)
                kb.psum_wait(3)
                kb.do("tensor", "matmul", pb[3][0:128, 0:16],
                      s_ones16f[:], s_dg16[:], start=True, stop=True)
                tp3b = kb.mark("tensor", "pe")
                kb.wait("scalar", tp3b)
                kb.do("scalar", "activation", s_acb[:, co:co + 16],
                      pb[3][0:128, 0:16], AF.Exp)
                ta3 = kb.mark("scalar", "ac")
                kb.wait("vector", tp3b)
                kb.do("vector", "tensor_tensor", s_kbar[:, co:co + 16],
                      pb[3][0:128, 0:16], s_cT[:, co:co + 16],
                      ALU.subtract)
                kb.do("vector", "tensor_single_scalar",
                      s_kbar[:, co:co + 16], s_kbar[:, co:co + 16],
                      0.0, ALU.min)
                tv4 = kb.mark("vector", "dv")
                kb.psum_consumed(3, ta3)
                kb.psum_consumed(3, tv4, add=True)
                kb.wait("scalar", tv4)
                kb.do("scalar", "activation", s_kbar[:, co:co + 16],
                      s_kbar[:, co:co + 16], AF.Exp)
                kb.mark("scalar", "ac")
            kb.barrier()
            kb.ckpt("p3")
            snap("cT", s_cT[:], (128, NCH * 16), F32)
            snap("kbar", s_kbar[:], (128, NCH * 16), F32)
            snap("acb", s_acb[:], (128, NCH * 16), F32)
            if debug:
                dbg("cT", s_cT[:], (128, NCH * 16), F32)
                dbg("kbar", s_kbar[:], (128, NCH * 16), F32)
                dbg("acb", s_acb[:], (128, NCH * 16), F32)

            with ExitStack() as sc4:
                E4 = sc4.enter_context
                s_nt = [E4(nc.sbuf_tensor(f"s_nt{k}", [128, 128], BF16))
                        for k in range(7)]
                s_np = [E4(nc.sbuf_tensor(f"s_np{k}", [128, 128], BF16))
                        for k in range(2)]
                s_gm = E4(nc.sbuf_tensor("s_gm", [128, 128], BF16))
                s_edN = E4(nc.sbuf_tensor("s_edN", [128, 128], BF16))
                s_edG = E4(nc.sbuf_tensor("s_edG", [128, 128], BF16))
                s_wkT = E4(nc.sbuf_tensor("s_wkT", [128, 128], BF16))
                s_wfl = E4(nc.sbuf_tensor("s_wfl", [128, 128], BF16))
                s_kb_ = E4(nc.sbuf_tensor("s_kb_", [128, 128], BF16))
                s_pan2 = [E4(nc.sbuf_tensor(f"s_pan2{k}", [128, 256],
                                            BF16)) for k in range(2)]
                panC = [E4(nc.sbuf_tensor(f"panC{k}", [128, HV * 256],
                                          BF16)) for k in range(2)]
                gtC = [E4(nc.sbuf_tensor(f"gtC{k}", [128, HV * 128],
                                         BF16)) for k in range(2)]
                ktC = [E4(nc.sbuf_tensor(f"ktC{k}", [128, HK * 128],
                                         BF16)) for k in range(2)]
                panC_free = [None, None]
                gtC_free = [None, None]
                ktC_free = [None, None]
                t_scr = None

                def chunk_pass1(i):
                    nonlocal t_scr
                    pp = i & 1
                    cs = slice(i * C, (i + 1) * C)
                    co = i * 16
                    kb.wait("vector", panC_free[pp])
                    kb.wait("vector", gtC_free[pp])
                    kb.wait("vector", ktC_free[pp])
                    for h in range(HV):
                        conv_pump(1)
                        kq = h // 2
                        gbank = 3 + (h & 1)
                        kb.wait("tensor", t_scr)
                        kb.psum_wait(gbank)
                        kb.do("tensor", "matmul", pb[gbank][0:128, 0:128],
                              qkv[8 + kq][:, cs], qkv[8 + kq][:, cs],
                              start=True, stop=True)
                        kb.do("tensor", "matmul",
                              pb[gbank][0:128, 128:256],
                              qkv[8 + kq][:, cs], qkv[kq][:, cs],
                              start=True, stop=True)
                        kb.psum_wait(5)
                        kb.do("tensor", "matmul", pb[5][0:128, 0:128],
                              s_selhf[:, h * 128:(h + 1) * 128],
                              s_c[:, cs], start=True, stop=True)
                        tpg = kb.mark("tensor", "pe")
                        kb.wait("vector", tpg)
                        kb.do("vector", "tensor_scalar", s_edf[:],
                              pb[5][0:128, 0:128],
                              s_cT[:, co + h:co + h + 1], 0.0,
                              ALU.subtract, ALU.max)
                        tv1 = kb.mark("vector", "dv")
                        kb.wait("scalar", tv1)
                        kb.do("scalar", "activation", s_edN[:], s_edf[:],
                              AF.Exp, scale=s_cstc[0:128, 5:6])
                        te1 = kb.mark("scalar", "ac")
                        kb.wait("vector", te1)
                        kb.do("vector", "tensor_scalar", s_edf[:],
                              pb[5][0:128, 0:128],
                              s_cT[:, co + h:co + h + 1], 0.0,
                              ALU.subtract, ALU.min)
                        tv2 = kb.mark("vector", "dv")
                        kb.psum_consumed(5, tv2)
                        kb.wait("scalar", tv2)
                        kb.do("scalar", "activation", s_edG[:], s_edf[:],
                              AF.Exp)
                        tEd = kb.mark("scalar", "ac")
                        kb.wait("vector", tEd)
                        kb.do("vector", "tensor_tensor", s_gm[:],
                              pb[gbank][0:128, 0:128], s_mlo[:], ALU.mult)
                        kb.do("vector", "scalar_tensor_tensor",
                              s_np[0][:], s_edN[:],
                              s_bTn[:, co + h:co + h + 1], s_gm[:],
                              ALU.mult, ALU.mult)
                        tN = kb.mark("vector", "dv")
                        kb.do("vector", "tensor_tensor", s_gm[:],
                              pb[gbank][0:128, 128:256], s_mue[:],
                              ALU.mult)
                        kb.do("vector", "tensor_tensor",
                              gtC[pp][:, h * 128:(h + 1) * 128],
                              s_gm[:], s_edG[:], ALU.mult)
                        tGt = kb.mark("vector", "dv")
                        kb.psum_consumed(gbank, tGt)
                        kb.wait("tensor", tN)
                        kb.psum_wait(6)
                        kb.do("tensor", "matmul", pb[6][0:128, 0:128],
                              s_np[0][:], s_i128b[:],
                              start=True, stop=True)
                        tpt = kb.mark("tensor", "pe")
                        kb.wait("scalar", tpt)
                        kb.do("scalar", "copy", s_nt[0][:],
                              pb[6][0:128, 0:128])
                        tnt = kb.mark("scalar", "ac")
                        kb.psum_consumed(6, tnt)
                        if h % 2 == 0:
                            kb.wait("tensor", ktC_free[pp])
                            kb.psum_wait(2)
                            kb.do("tensor", "matmul",
                                  pb[2][0:128, 0:128],
                                  qkv[8 + kq][:, cs], s_i128b[:],
                                  start=True, stop=True)
                            tkt = kb.mark("tensor", "pe")
                            kb.wait("vector", tkt)
                            kb.do("vector", "tensor_copy",
                                  ktC[pp][:, kq * 128:(kq + 1) * 128],
                                  pb[2][0:128, 0:128])
                            tke = kb.mark("vector", "dv")
                            kb.psum_consumed(2, tke)
                        kb.psum_wait(2)
                        kb.do("tensor", "matmul", pb[2][0:128, 0:128],
                              qkv[16 + h][:, cs], s_i128b[:],
                              start=True, stop=True)
                        tvt = kb.mark("tensor", "pe")
                        kb.wait("vector", tvt)
                        kb.do("vector", "tensor_scalar",
                              panC[pp][:, h * 256:h * 256 + 128],
                              pb[2][0:128, 0:128],
                              s_bT[:, co + h:co + h + 1], None, ALU.mult)
                        tuv = kb.mark("vector", "dv")
                        kb.psum_consumed(2, tuv)
                        kb.do("vector", "tensor_scalar",
                              panC[pp][:, h * 256 + 128:h * 256 + 256],
                              ktC[pp][:, kq * 128:(kq + 1) * 128],
                              s_bA[:, co + h:co + h + 1], None, ALU.mult)
                        tuk = kb.mark("vector", "dv")
                        if debug and i == 0 and h == 0:
                            dbg("pan_pre", panC[pp][:, 0:256], (128, 256),
                                BF16)
                        cur = 0
                        tnp, tns = tN, tnt
                        for k in range(1, 7):
                            kb.wait("tensor", tns)
                            kb.wait("tensor", tnp)
                            kb.psum_wait(6)
                            kb.do("tensor", "matmul",
                                  pb[6][0:128, 0:128],
                                  s_nt[k - 1][:], s_np[cur][:],
                                  start=True, stop=True)
                            kb.psum_wait(5)
                            kb.do("tensor", "matmul",
                                  pb[5][0:128, 0:128],
                                  s_np[cur][:], s_nt[k - 1][:],
                                  start=True, stop=True)
                            tsq_ = kb.mark("tensor", "pe")
                            kb.wait("vector", tsq_)
                            kb.do("vector", "tensor_copy",
                                  s_np[1 - cur][:], pb[6][0:128, 0:128])
                            tnp = kb.mark("vector", "dv")
                            kb.psum_consumed(6, tnp)
                            kb.wait("scalar", tsq_)
                            kb.do("scalar", "copy", s_nt[k][:],
                                  pb[5][0:128, 0:128])
                            tns = kb.mark("scalar", "ac")
                            kb.psum_consumed(5, tns)
                            cur = 1 - cur
                        src = panC[pp][:, h * 256:h * 256 + 256]
                        kb.wait("tensor", tuv)
                        kb.wait("tensor", tuk)
                        kb.wait("tensor", tns)
                        tev = None
                        for ai, k in enumerate(range(6, -1, -1)):
                            bank = 5 if (ai & 1) == 0 else 6
                            dst = (s_pan2[ai & 1][:] if ai < 6 else
                                   panC[pp][:, h * 256:h * 256 + 256])
                            kb.wait("tensor", tev)
                            kb.psum_wait(bank)
                            kb.do("tensor", "matmul",
                                  pb[bank][0:128, 0:256], s_nt[k][:],
                                  src, start=True, stop=True)
                            tap = kb.mark("tensor", "pe")
                            kb.wait("vector", tap)
                            kb.do("vector", "tensor_tensor",
                                  dst, pb[bank][0:128, 0:256], src, ALU.add)
                            tev = kb.mark("vector", "dv")
                            kb.psum_consumed(bank, tev)
                            src = dst
                        t_scr = kb.tok("ac")
                        if debug and i == 0 and h == 0:
                            dbg("edN", s_edN[:], (128, 128), BF16)
                            dbg("edG", s_edG[:], (128, 128), BF16)
                            dbg("np0", s_np[0][:], (128, 128), BF16)
                            dbg("nt0", s_nt[0][:], (128, 128), BF16)
                            dbg("nt6", s_nt[6][:], (128, 128), BF16)
                            dbg("ktok0", ktC[pp][:, 0:128], (128, 128), BF16)
                            dbg("gt0", gtC[pp][:, 0:128], (128, 128), BF16)
                            dbg("pan_post", panC[pp][:, 0:256],
                                (128, 256), BF16)
                    for h in range(HV):
                        kq = h // 2
                        sl = slice(h * 128, (h + 1) * 128)
                        rb = i & 1
                        wbuf = 1 - rb
                        kb.psum_wait(2)
                        kb.do("tensor", "matmul", pb[2][0:128, 0:128],
                              panC[pp][:, h * 256 + 128:h * 256 + 256],
                              s_i128b[:], start=True, stop=True)
                        tw1 = kb.mark("tensor", "pe")
                        kb.wait("vector", tw1)
                        kb.do("vector", "tensor_copy", s_wkT[:],
                              pb[2][0:128, 0:128])
                        tw2 = kb.mark("vector", "dv")
                        kb.psum_consumed(2, tw2)
                        kb.wait("tensor", tw2)
                        kb.psum_wait(6)
                        kb.do("tensor", "matmul", pb[6][0:128, 0:128],
                              s_wkT[:], s_state[rb][:, sl],
                              start=True, stop=True)
                        tw3 = kb.mark("tensor", "pe")
                        kb.wait("vector", tw3)
                        kb.do("vector", "tensor_tensor", s_wfl[:],
                              panC[pp][:, h * 256:h * 256 + 128],
                              pb[6][0:128, 0:128], ALU.subtract)
                        kb.do("vector", "tensor_scalar", s_kb_[:],
                              ktC[pp][:, kq * 128:(kq + 1) * 128],
                              s_kbar[:, co + h:co + h + 1], None,
                              ALU.mult)
                        tw4 = kb.mark("vector", "dv")
                        kb.psum_consumed(6, tw4)
                        kb.wait("tensor", tw4)
                        kb.psum_wait(5)
                        kb.do("tensor", "matmul", pb[5][0:128, 0:128],
                              s_kb_[:], s_wfl[:], start=True, stop=True)
                        tw5 = kb.mark("tensor", "pe")
                        kb.wait("vector", tw5)
                        kb.do("vector", "scalar_tensor_tensor",
                              s_state[wbuf][:, sl], s_state[rb][:, sl],
                              s_acb[:, co + h:co + h + 1],
                              pb[5][0:128, 0:128], ALU.mult, ALU.add)
                        tw6 = kb.mark("vector", "dv")
                        kb.psum_consumed(5, tw6)
                    tsp = kb.tok("dv")
                    tss = kb.tok("ac")
                    kb.wait("gpsimd", tsp)
                    kb.wait("gpsimd", tss)
                    kb.do("gpsimd", "dma_start",
                          pan_hbm[i],
                          panC[pp][:])
                    panC_free[pp] = kb.mark("gpsimd", "gdma", 16)
                    kb.do("gpsimd", "dma_start",
                          gt_hbm[i],
                          gtC[pp][:])
                    gtC_free[pp] = kb.mark("gpsimd", "gdma", 16)
                    kb.do("gpsimd", "dma_start",
                          kt_hbm[i],
                          ktC[pp][:])
                    ktC_free[pp] = kb.mark("gpsimd", "gdma", 16)

                for i in range(NCH):
                    chunk_pass1(i)
                kb.barrier()
                kb.ckpt("p4")
                snap("state_p1", s_state[NCH & 1][:], (128, HV * DV), BF16)
                snap("pan0", pan_hbm[0], (128, HV * 256), BF16)
                snap("gwbf0", gw_bf[0], (128, 2048), BF16)
                snap("dnbf47", dn_bf[NI - 1], (128, 2048), BF16)
                snap("gw_all", gw_bf.rearrange("a p m -> (a p) m"),
                     (NI * 128, 2048), BF16)
                snap("uw_all", uw_bf.rearrange("a p m -> (a p) m"),
                     (NI * 128, 2048), BF16)
                snap("dn_all", dn_bf.rearrange("a p m -> (a p) m"),
                     (NI * 128, 2048), BF16)
                if debug:
                    dbg("state_p1",
                        s_state[NCH & 1][:], (128, HV * DV), BF16)

            fin = NCH & 1
            kb.do("gpsimd", "dma_start", cc_in, s_state[fin][:])
            kb.mark("gpsimd", "gdma", 16)
            kb.wait("gpsimd", kb.tok("gdma"))
            kb.ckpt("c1")
            kb.do("gpsimd", "collective_compute",
                  "AllGather", ALU.bypass, rg, [cc_in], [cc_out])
            tcoll = kb.mark("gpsimd", "Bg", 1)
            kb.ckpt("c2")
            with nc.sbuf_tensor("s_recv", [128, 8 * HV * DV], BF16) \
                    as s_recv:
                kb.wait("gpsimd", tcoll)
                ccv = cc_out.rearrange("(r p) f -> r p f", p=128)
                for r in range(8):
                    kb.do("gpsimd", "dma_start",
                          s_recv[:, r * 2048:(r + 1) * 2048], ccv[r])
                    kb.mark("gpsimd", "gdma", 16)
                trcv = kb.tok("gdma")
                kb.wait_p16("vector", trcv)
                kb.do("vector", "tensor_scalar", s_state[0][:],
                      s_recv[:, 0:2048], s_selcol[:, 0:1], None, ALU.mult)
                for r in range(1, 8):
                    kb.do("vector", "scalar_tensor_tensor",
                          s_state[0][:],
                          s_recv[:, r * 2048:(r + 1) * 2048],
                          s_selcol[:, r:r + 1], s_state[0][:],
                          ALU.mult, ALU.add)
                tsel = kb.mark("vector", "dv")
            kb.barrier()
            kb.ckpt("coll")
            snap("state_init", s_state[0][:], (128, HV * DV), BF16)
            if debug:
                dbg("state_init", s_state[0][:], (128, HV * DV), BF16)

            with ExitStack() as sc5:
                E5 = sc5.enter_context
                panI = [E5(nc.sbuf_tensor(f"panI{k}", [128, HV * 256],
                                          BF16)) for k in range(2)]
                gtI = [E5(nc.sbuf_tensor(f"gtI{k}", [128, HV * 128],
                                         BF16)) for k in range(2)]
                ktI = [E5(nc.sbuf_tensor(f"ktI{k}", [128, HK * 128],
                                         BF16)) for k in range(2)]
                s_wkT = E5(nc.sbuf_tensor("s_wkT5", [128, 128], BF16))
                s_wfl = E5(nc.sbuf_tensor("s_wfl5", [128, 128], BF16))
                s_kb_ = E5(nc.sbuf_tensor("s_kb5", [128, 128], BF16))
                s_qt = E5(nc.sbuf_tensor("s_qt5", [128, 128], BF16))
                ld_free = [None, None]

                def load_chunk(i):
                    pp = i & 1
                    kb.wait("sync", ld_free[pp])
                    kb.do("sync", "dma_start", panI[pp][:],
                          pan_hbm[i])
                    kb.mark("sync", "wdma", 16)
                    kb.do("sync", "dma_start", gtI[pp][:],
                          gt_hbm[i])
                    kb.mark("sync", "wdma", 16)
                    kb.do("sync", "dma_start", ktI[pp][:],
                          kt_hbm[i])
                    kb.mark("sync", "wdma", 16)
                    return kb.tok("wdma")

                tld = [None] * NCH
                tld[0] = load_chunk(0)
                for i in range(NCH):
                    pp = i & 1
                    cs = slice(i * C, (i + 1) * C)
                    co = i * 16
                    if i + 1 < NCH:
                        tld[i + 1] = load_chunk(i + 1)
                        kb.wait_p16("tensor", tld[i])
                    else:
                        kb.do("sync", "dma_start", ktI[pp][:], kt_hbm[i])
                        kb.mark("sync", "wdma", 16)
                        kb.wait("tensor", kb.tok("wdma"))
                    kb.wait("vector", tld[i])
                    for h in range(HV):
                        kq = h // 2
                        sl = slice(h * 128, (h + 1) * 128)
                        rb = i & 1
                        wbuf = 1 - rb
                        kb.psum_wait(4)
                        kb.do("tensor", "matmul", pb[4][:, 0:128],
                              s_selhb[:, h * 128:(h + 1) * 128],
                              s_arow[:, cs], start=True, stop=True)
                        tq1 = kb.mark("tensor", "pe")
                        kb.wait("vector", tq1)
                        kb.do("vector", "tensor_tensor", s_qt[:],
                              qkv[kq][:, cs], pb[4][:, 0:128], ALU.mult)
                        tq2 = kb.mark("vector", "dv")
                        kb.psum_consumed(4, tq2)
                        kb.psum_wait(2)
                        kb.do("tensor", "matmul", pb[2][0:128, 0:128],
                              panI[pp][:, h * 256 + 128:h * 256 + 256],
                              s_i128b[:], start=True, stop=True)
                        tw1 = kb.mark("tensor", "pe")
                        kb.wait("vector", tw1)
                        kb.do("vector", "tensor_copy", s_wkT[:],
                              pb[2][0:128, 0:128])
                        tw2 = kb.mark("vector", "dv")
                        kb.psum_consumed(2, tw2)
                        kb.wait("tensor", tw2)
                        kb.psum_wait(6)
                        kb.do("tensor", "matmul", pb[6][0:128, 0:128],
                              s_wkT[:], s_state[rb][:, sl],
                              start=True, stop=True)
                        tw3 = kb.mark("tensor", "pe")
                        kb.wait("vector", tw3)
                        kb.do("vector", "tensor_tensor", s_wfl[:],
                              panI[pp][:, h * 256:h * 256 + 128],
                              pb[6][0:128, 0:128], ALU.subtract)
                        kb.do("vector", "tensor_scalar", s_kb_[:],
                              ktI[pp][:, kq * 128:(kq + 1) * 128],
                              s_kbar[:, co + h:co + h + 1], None,
                              ALU.mult)
                        tw4 = kb.mark("vector", "dv")
                        kb.psum_consumed(6, tw4)
                        kb.wait("tensor", tq2)
                        kb.wait("tensor", tw4)
                        kb.psum_wait(3)
                        kb.do("tensor", "matmul", pb[3][0:128, 0:128],
                              s_state[rb][:, sl], s_qt[:],
                              start=True, stop=False)
                        kb.do("tensor", "matmul", pb[3][0:128, 0:128],
                              s_wfl[:],
                              gtI[pp][:, h * 128:(h + 1) * 128],
                              start=False, stop=True)
                        tO = kb.mark("tensor", "pe")
                        kb.wait("scalar", tO)
                        kb.do("scalar", "copy", core[h][:, cs],
                              pb[3][0:128, 0:128])
                        tOe = kb.mark("scalar", "ac")
                        kb.psum_consumed(3, tOe)
                        kb.psum_wait(5)
                        kb.do("tensor", "matmul", pb[5][0:128, 0:128],
                              s_kb_[:], s_wfl[:], start=True, stop=True)
                        tw5 = kb.mark("tensor", "pe")
                        kb.wait("vector", tw5)
                        kb.do("vector", "scalar_tensor_tensor",
                              s_state[wbuf][:, sl], s_state[rb][:, sl],
                              s_acb[:, co + h:co + h + 1],
                              pb[5][0:128, 0:128], ALU.mult, ALU.add)
                        tw6 = kb.mark("vector", "dv")
                        kb.psum_consumed(5, tw6)
                    ld_free[pp] = kb.tok("pe")
                kb.barrier()
                kb.ckpt("p5")
                snap("core_raw0", core[0][:], (128, TOK), BF16)
                snap("core_raw15", core[15][:], (128, TOK), BF16)
            if debug:
                dbg("core_raw0", core[0][:], (128, TOK), BF16)
                dbg("core_raw15", core[15][:], (128, TOK), BF16)

            for g in reversed(sel_guards):
                g.__exit__(None, None, None)

            with nc.sbuf_tensor("zin0", [128, TOK], BF16) as zin0, \
                 nc.sbuf_tensor("zin1", [128, TOK], BF16) as zin1, \
                 nc.sbuf_tensor("gsc", [128, TOK], BF16) as gsc:
                zin = [zin0, zin1]
                zfree = [None, None]
                gsc_free = [None]
                rows6_free = [None]
                kb.do("sync", "dma_start", zin[0][:],
                      z_hbm[0])
                kb.mark("sync", "wdma", 16)
                tz = [None] * HV
                tz[0] = kb.tok("wdma")
                for h in range(HV):
                    if h + 1 < HV:
                        kb.wait("sync", zfree[(h + 1) & 1])
                        kb.do("sync", "dma_start", zin[(h + 1) & 1][:],
                              z_hbm[h + 1])
                        kb.mark("sync", "wdma", 16)
                        tz[h + 1] = kb.tok("wdma")
                    kb.wait("scalar", gsc_free[0])
                    kb.do("scalar", "activation", gsc[:], core[h][:],
                          AF.Square)
                    tq = kb.mark("scalar", "ac")
                    kb.wait("tensor", tq)
                    kb.psum_wait(6)
                    kb.do("tensor", "matmul", pb[6][0:1, 0:512],
                          s_onesb[:], gsc[:, 0:512],
                          start=True, stop=True)
                    kb.psum_wait(5)
                    kb.do("tensor", "matmul", pb[5][0:1, 0:512],
                          s_onesb[:], gsc[:, 512:1024],
                          start=True, stop=True)
                    tsm = kb.mark("tensor", "pe")
                    kb.wait("scalar", tsm)
                    kb.wait("scalar", rows6_free[0])
                    kb.do("scalar", "activation", s_rows[0:1, 0:512],
                          pb[6][0:1, 0:512], AF.Sqrt,
                          scale=s_cst[0:1, 7:8], bias=s_cst[0:1, 8:9])
                    kb.do("scalar", "activation", s_rows[0:1, 512:1024],
                          pb[5][0:1, 0:512], AF.Sqrt,
                          scale=s_cst[0:1, 7:8], bias=s_cst[0:1, 8:9])
                    tsr = kb.mark("scalar", "ac")
                    kb.psum_consumed(6, tsr)
                    kb.psum_consumed(5, tsr)
                    kb.wait("vector", tsr)
                    kb.do("vector", "reciprocal", s_rows[0:1, :],
                          s_rows[0:1, :])
                    kb.do("vector", "tensor_copy", s_rowb[0:1, :],
                          s_rows[0:1, :])
                    trc = kb.mark("vector", "dv")
                    rows6_free[0] = trc
                    kb.wait("tensor", trc)
                    kb.psum_wait(6)
                    kb.do("tensor", "matmul", pb[6][:, 0:512],
                          s_onesrb[:], s_rowb[0:1, 0:512],
                          start=True, stop=True)
                    kb.psum_wait(5)
                    kb.do("tensor", "matmul", pb[5][:, 0:512],
                          s_onesrb[:], s_rowb[0:1, 512:1024],
                          start=True, stop=True)
                    tbc = kb.mark("tensor", "pe")
                    kb.wait("vector", tbc)
                    kb.do("vector", "tensor_tensor", gsc[:, 0:512],
                          core[h][:, 0:512], pb[6][:, 0:512], ALU.mult)
                    kb.do("vector", "tensor_tensor", gsc[:, 512:1024],
                          core[h][:, 512:1024], pb[5][:, 0:512], ALU.mult)
                    tgg = kb.mark("vector", "dv")
                    kb.psum_consumed(6, tgg)
                    kb.psum_consumed(5, tgg)
                    kb.wait_p16("vector", tz[h])
                    kb.do("vector", "scalar_tensor_tensor", gsc[:],
                          gsc[:], s_normw[:, 0:1], zin[h & 1][:],
                          ALU.mult, ALU.mult)
                    kb.do("vector", "tensor_copy", core[h][:], gsc[:])
                    zfree[h & 1] = kb.mark("vector", "dv")
                    gsc_free[0] = zfree[h & 1]
                kb.barrier()
                kb.ckpt("p6")
                snap("core_g0", core[0][:], (128, TOK), BF16)
            if debug:
                dbg("core_g0", core[0][:], (128, TOK), BF16)


        with ExitStack() as sc7:
            E7 = sc7.enter_context
            att_guards = [nc.sbuf_tensor(f"att{j}", [128, TOK], BF16)
                          for j in range(16)]
            att = [g.__enter__() for g in att_guards]
            wtok2 = load_wblk(wout_b[0], 0, False, 0)
            for o in range(16):
                widx = o & 1
                nxt = None
                if o + 1 < 16:
                    nxt = load_wblk(
                        wout_b[o + 1], (o + 1) & 1, False, (o + 1) & 1)
                kb.wait("tensor", wtok2)
                for si in range(2):
                    bank = (2 * o + si) % 2
                    kb.psum_wait(bank)
                    for kk in range(16):
                        kb.do("tensor", "matmul", pb[bank][0:128, 0:512],
                              wb[widx][:, kk * 128:(kk + 1) * 128],
                              core[kk][:, si * 512:(si + 1) * 512],
                              start=(kk == 0), stop=(kk == 15))
                    tpp = kb.mark("tensor", "pe")
                    wb_free[widx] = tpp
                    kb.wait("vector", tpp)
                    kb.do("vector", "tensor_scalar",
                          att[o][:, si * 512:(si + 1) * 512],
                          pb[bank][0:128, 0:512],
                          s_boutc[:, o:o + 1], None, ALU.add)
                    te = kb.mark("vector", "dv")
                    kb.psum_consumed(bank, te)
                wtok2 = nxt
            conv_pump(len(conv_seq))
            kb.barrier()
            kb.ckpt("p7a")
            snap("att0", att[0][:], (128, TOK), BF16)
            snap("att15", att[15][:], (128, TOK), BF16)
            if debug:
                dbg("att0", att[0][:], (128, TOK), BF16)

            with nc.sbuf_tensor("atok", [128, HID], BF16) as atok, \
                 nc.sbuf_tensor("xt", [128, HID], F32) as xt, \
                 nc.sbuf_tensor("x1t", [128, HID], F32) as x1t, \
                 nc.sbuf_tensor("l3t", [128, HID], BF16) as l3t, \
                 nc.sbuf_tensor("l3s", [128, HID], BF16) as l3s, \
                 nc.sbuf_tensor("st7", [128, 8], F32) as st7:
                xt_free = None
                atok_free = None
                x1t_free = []
                l3t_free = None
                l3s_free = None
                for t in range(NTT):
                    kb.wait("sync", xt_free)
                    kb.do("sync", "dma_start", xt[:],
                          x_main[t * TT:(t + 1) * TT, :])
                    kb.mark("sync", "wdma", 16)
                    kb.do("sync", "dma_start", xt[:],
                          x_main[t * TT:(t + 1) * TT, :])
                    kb.mark("sync", "wdma", 16)
                    txx = kb.tok("wdma")
                    kb.wait("scalar", atok_free)
                    for j in range(16):
                        bank = 2 + (j & 1)
                        kb.psum_wait(bank)
                        kb.do("tensor", "matmul", pb[bank][0:128, 0:128],
                              att[j][:, t * TT:(t + 1) * TT], s_i128b[:],
                              start=True, stop=True)
                        tpj = kb.mark("tensor", "pe")
                        kb.wait("scalar", tpj)
                        kb.do("scalar", "copy",
                              atok[:, j * 128:(j + 1) * 128],
                              pb[bank][0:128, 0:128])
                        taj = kb.mark("scalar", "ac")
                        kb.psum_consumed(bank, taj)
                    tat = kb.tok("ac")
                    kb.wait("vector", tat)
                    kb.wait("vector", txx)
                    for tkn in x1t_free:
                        kb.wait("vector", tkn)
                    tl2 = ln_apply(atok, 128, None, st7,
                                   add_ap=xt[:], out_f32=x1t[:])
                    kb.wait("gpsimd", tl2)
                    kb.do("gpsimd", "dma_start",
                          x1_hbm[t * TT:(t + 1) * TT, :], x1t[:])
                    tx1d = kb.mark("gpsimd", "gdma", 16)
                    xt_free = tl2
                    atok_free = tl2
                    kb.wait("vector", l3t_free)
                    tl3 = ln_apply(x1t, 128, l3t[:], st7)
                    x1t_free = [tx1d, kb.tok("ac"), tl3]
                    kb.wait("scalar", l3s_free)
                    for j in range(16):
                        bank = 2 + (j & 1)
                        kb.wait("tensor", tl3)
                        kb.psum_wait(bank)
                        kb.do("tensor", "matmul", pb[bank][0:128, 0:128],
                              l3t[:, j * 128:(j + 1) * 128], s_i128b[:],
                              start=True, stop=True)
                        tpj = kb.mark("tensor", "pe")
                        kb.wait("scalar", tpj)
                        kb.do("scalar", "copy",
                              l3s[:, j * 128:(j + 1) * 128],
                              pb[bank][0:128, 0:128])
                        taj = kb.mark("scalar", "ac")
                        kb.psum_consumed(bank, taj)
                    l3t_free = kb.tok("pe")
                    tl3s = kb.tok("ac")
                    kb.wait("gpsimd", tl3s)
                    for j in range(16):
                        kb.do("gpsimd", "dma_start",
                              ln3_hbm[t // 4, j][:, (t % 4) * 128:
                                                 (t % 4) * 128 + 128],
                              l3s[:, j * 128:(j + 1) * 128])
                        kb.mark("gpsimd", "gdma", 16)
                    l3s_free = kb.tok("gdma")
            kb.barrier()
            snap("x1a", x1_hbm[0:128, :], (128, HID), F32)
            snap("x1b", x1_hbm[TOK - 128:TOK, :], (128, HID), F32)
            snap("ln3a", ln3_hbm[0, 0], (128, 512), BF16)

            if debug:
                dx1 = nc.dram_tensor("dbg_x1", [TOK, HID], F32,
                                     kind="ExternalOutput").ap()
                kb.barrier()
                kb.do("gpsimd", "dma_start", dx1, x1_hbm)
                kb.mark("gpsimd", "gdma", 16)
                dl3 = nc.dram_tensor("dbg_ln3", [128, 512], BF16,
                                     kind="ExternalOutput").ap()
                kb.do("gpsimd", "dma_start", dl3, ln3_hbm[0, 0])
                kb.mark("gpsimd", "gdma", 16)
                kb.barrier()
            for g in reversed(att_guards):
                g.__exit__(None, None, None)
            qkv_guard.__exit__(None, None, None)

            with ExitStack() as sc8:
                E8 = sc8.enter_context
                fuse = E8(nc.sbuf_tensor("fuse", [128, NI * 512], BF16))
                l3in = E8(nc.sbuf_tensor("l3in", [128, 16 * 512], BF16))
                gact = [E8(nc.sbuf_tensor(f"gact{k}", [128, 512], BF16))
                        for k in range(2)]
                ffncm = E8(nc.sbuf_tensor("ffncm", [128, 16 * 512], BF16))
                gub = [E8(nc.sbuf_tensor(f"gub{k}", [128, 2048], BF16))
                       for k in range(4)]
                dnb = [E8(nc.sbuf_tensor(f"dnb{k}", [128, 2048], BF16))
                       for k in range(2)]
                ftok = E8(nc.sbuf_tensor("ftok", [128, HID], BF16))
                x1r = E8(nc.sbuf_tensor("x1r", [128, HID], F32))
                l4t = E8(nc.sbuf_tensor("l4t", [128, HID], F32))
                st8 = E8(nc.sbuf_tensor("st8", [128, 8], F32))
                gact_free = [None, None]
                gub_free = [None] * 4
                dnb_free = [None] * 2
                fuse_free = None
                ffncm_free = None
                ftok_free = None
                x1r_free = None
                l4t_free = None

                def load_gu(ib, which):
                    bi = 2 * which + (ib & 1)
                    kb.wait("sync", gub_free[bi])
                    w = gw_bf if which == 0 else uw_bf
                    kb.do("sync", "dma_start", gub[bi][:], w[ib])
                    kb.mark("sync", "wdma", 16)
                    return kb.tok("wdma")

                def load_dn(ib, slot):
                    kb.wait("sync", dnb_free[slot])
                    kb.do("sync", "dma_start", dnb[slot][:], dn_bf[ib])
                    kb.mark("sync", "wdma", 16)
                    return kb.tok("wdma")

                l3in_free = None
                for half in range(2):
                    kb.wait("sync", l3in_free)
                    for kk in range(16):
                        kb.do("sync", "dma_start",
                              l3in[:, kk * 512:(kk + 1) * 512],
                              ln3_hbm[half, kk])
                        kb.mark("sync", "wdma", 16)
                    tl3in = kb.tok("wdma")
                    kb.wait_p16("tensor", tl3in)
                    tg_tok = load_gu(0, 0)
                    tu_tok = load_gu(0, 1)
                    kb.wait("vector", fuse_free)
                    for ib in range(NI):
                        gb = ib & 1
                        ub = 2 + (ib & 1)
                        kb.wait_p16("tensor", tg_tok)
                        bank = ib % 2
                        kb.psum_wait(bank)
                        for kk in range(16):
                            kb.do("tensor", "matmul", pb[bank][0:128, 0:512],
                                  gub[gb][:, kk * 128:(kk + 1) * 128],
                                  l3in[:, kk * 512:(kk + 1) * 512],
                                  start=(kk == 0), stop=(kk == 15))
                        tg1 = kb.mark("tensor", "pe")
                        gub_free[gb] = tg1
                        if ib + 1 < NI:
                            tg_tok = load_gu(ib + 1, 0)
                        kb.wait("scalar", tg1)
                        kb.wait("scalar", gact_free[ib & 1])
                        kb.do("scalar", "activation", gact[ib & 1][:],
                              pb[bank][0:128, 0:512], AF.Silu,
                              scale=s_gsc[:, ib:ib + 1])
                        tg2 = kb.mark("scalar", "ac")
                        kb.psum_consumed(bank, tg2)
                        kb.wait_p16("tensor", tu_tok)
                        bank2 = 2 + (ib % 2)
                        kb.psum_wait(bank2)
                        for kk in range(16):
                            kb.do("tensor", "matmul",
                                  pb[bank2][0:128, 0:512],
                                  gub[ub][:, kk * 128:(kk + 1) * 128],
                                  l3in[:, kk * 512:(kk + 1) * 512],
                                  start=(kk == 0), stop=(kk == 15))
                        tu1 = kb.mark("tensor", "pe")
                        gub_free[ub] = tu1
                        if ib + 1 < NI:
                            tu_tok = load_gu(ib + 1, 1)
                        kb.wait("vector", tu1)
                        kb.wait("vector", tg2)
                        kb.do("vector", "scalar_tensor_tensor",
                              fuse[:, ib * 512:(ib + 1) * 512],
                              pb[bank2][0:128, 0:512],
                              s_usc[:, ib:ib + 1], gact[ib & 1][:],
                              ALU.mult, ALU.mult)
                        tf = kb.mark("vector", "dv")
                        gact_free[ib & 1] = tf
                        kb.psum_consumed(bank2, tf)
                    tfuse = kb.tok("dv")
                    l3in_free = kb.tok("pe")
                    if debug2:
                        kb.wait("gpsimd", tfuse)
                        snap(f"fuse_h{half}", fuse[:, 0:4096], (128, 4096),
                             BF16)
                        snap(f"l3in_h{half}", l3in[:], (128, 16 * 512), BF16)
                    kb.wait("tensor", tfuse)
                    tdn = [None] * 2
                    for s in range(2):
                        tdn[s] = load_dn(s, s)
                    kb.wait("vector", ffncm_free)
                    for gi in range(2 * NI):
                        og, ib = divmod(gi, NI)
                        slot = gi & 1
                        if ib == 0:
                            for oo in range(8):
                                kb.psum_wait(oo)
                        kb.wait_p16("tensor", tdn[slot])
                        for oo in range(8):
                            o = og * 8 + oo
                            kb.do("tensor", "matmul",
                                  pb[oo][0:128, 0:512],
                                  dnb[slot][:, o * 128:(o + 1) * 128],
                                  fuse[:, ib * 512:(ib + 1) * 512],
                                  start=(ib == 0), stop=(ib == NI - 1))
                        tdd = kb.mark("tensor", "pe")
                        dnb_free[slot] = tdd
                        if gi + 2 < 2 * NI:
                            tdn[slot] = load_dn((gi + 2) % NI, slot)
                        if ib == NI - 1:
                            tdall = kb.tok("pe")
                            kb.wait("vector", tdall)
                            for oo in range(8):
                                o = og * 8 + oo
                                kb.do("vector", "tensor_scalar",
                                      ffncm[:, o * 512:(o + 1) * 512],
                                      pb[oo][0:128, 0:512],
                                      s_dsc[:, o:o + 1], None, ALU.mult)
                                tev = kb.mark("vector", "dv")
                                kb.psum_consumed(oo, tev)
                    tffn = kb.tok("dv")
                    fuse_free = kb.tok("pe")
                    if debug2:
                        kb.wait("gpsimd", tffn)
                        snap(f"ffncm_h{half}", ffncm[:], (128, 16 * 512),
                             BF16)
                    if debug and half == 0:
                        dbg("fuse0", fuse[:, 0:512], (128, 512), BF16)
                        dbg("ffncm0", ffncm[:, 0:512], (128, 512), BF16)
                        dbg("l3in0", l3in[:, 0:512], (128, 512), BF16)
                    for sub in range(4):
                        tglob = half * 512 + sub * 128
                        kb.wait("sync", x1r_free)
                        kb.do("sync", "dma_start", x1r[:],
                              x1_hbm[tglob:tglob + 128, :])
                        kb.mark("sync", "wdma", 16)
                        kb.do("sync", "dma_start", x1r[:],
                              x1_hbm[tglob:tglob + 128, :])
                        kb.mark("sync", "wdma", 16)
                        tx1 = kb.tok("wdma")
                        kb.wait("scalar", ftok_free)
                        for j in range(16):
                            bank = 6 + (j & 1)
                            kb.wait("tensor", tffn)
                            kb.psum_wait(bank)
                            kb.do("tensor", "matmul",
                                  pb[bank][0:128, 0:128],
                                  ffncm[:, j * 512 + sub * 128:
                                        j * 512 + sub * 128 + 128],
                                  s_i128b[:], start=True, stop=True)
                            tpj = kb.mark("tensor", "pe")
                            kb.wait("scalar", tpj)
                            kb.do("scalar", "copy",
                                  ftok[:, j * 128:(j + 1) * 128],
                                  pb[bank][0:128, 0:128])
                            taj = kb.mark("scalar", "ac")
                            kb.psum_consumed(bank, taj)
                        tft = kb.tok("ac")
                        kb.wait("vector", tft)
                        kb.wait("vector", tx1)
                        kb.wait("vector", l4t_free)
                        tl4 = ln_apply(ftok, 128, None, st8,
                                       add_ap=x1r[:], out_f32=l4t[:])
                        ftok_free = tl4
                        x1r_free = tl4
                        kb.wait("gpsimd", tl4)
                        kb.do("gpsimd", "dma_start",
                              out_x[tglob:tglob + 128, :], l4t[:])
                        l4t_free = kb.mark("gpsimd", "gdma", 16)
                    ffncm_free = kb.tok("pe")
                kb.barrier()
        kb.barrier()
        kb.ckpt("full")
        if cut is not None:
            kb.cut_at(cut)
        kb.wait("gpsimd", kb.tok("gdma"))

        if kb_sink is not None:
            kb_sink.append(kb)
        with nc.Block() as block:
            kb.emit(block)
    return nc


_NC_CACHE = {}


def _get_nc(debug=False):
    key = bool(debug)
    if key not in _NC_CACHE:
        _NC_CACHE[key] = build_nc(debug=key)
    return _NC_CACHE[key]


def _host_inputs(inp):
    f32 = np.float32
    bf16 = ml_dtypes.bfloat16
    x = np.ascontiguousarray(inp["L0_attn_hidden_states"], dtype=f32)
    Wqkv = np.asarray(inp["L0_attn_Wqkv"], f32)
    Wz = np.asarray(inp["L0_attn_Wz"], f32)
    Wb = np.asarray(inp["L0_attn_Wb"], f32)
    Wa = np.asarray(inp["L0_attn_Wa"], f32)
    bqkv = np.asarray(inp["L0_attn_bqkv"], f32)
    bz = np.asarray(inp["L0_attn_bz"], f32)
    bb = np.asarray(inp["L0_attn_bb"], f32)
    ba = np.asarray(inp["L0_attn_ba"], f32)
    convw = np.asarray(inp["L0_attn_conv_w"], f32)
    convb = np.asarray(inp["L0_attn_conv_b"], f32)
    dtb = np.asarray(inp["L0_attn_dt_bias"], f32)
    alog = np.asarray(inp["L0_attn_A_log"], f32)
    normw = np.asarray(inp["L0_attn_norm_w"], f32)
    Wout = np.asarray(inp["L0_attn_Wout"], f32)
    bout = np.asarray(inp["L0_attn_bout"], f32)
    gw = np.asarray(inp["L0_ffn_gate_w"], f32)
    uw = np.asarray(inp["L0_ffn_up_w"], f32)
    dw = np.asarray(inp["L0_ffn_down_w"], f32)
    gs = np.asarray(inp["L0_ffn_gate_scale"], f32)
    us = np.asarray(inp["L0_ffn_up_scale"], f32)
    ds = np.asarray(inp["L0_ffn_down_scale"], f32)

    Wcat = np.zeros((NIN * 128, HID), f32)
    Wcat[0:CONVD] = Wqkv
    Wcat[CONVD:CONVD + HID] = Wz
    Wcat[6144:6160] = Wb
    Wcat[6176:6192] = Wa
    WinT = Wcat.T
    w_in = np.ascontiguousarray(
        WinT.reshape(16, 128, NIN, 128).transpose(2, 1, 0, 3)
        .reshape(NIN, 128, 2, 1024).transpose(0, 2, 1, 3))
    b_cat = np.zeros((NIN * 128,), f32)
    b_cat[0:CONVD] = bqkv
    b_cat[CONVD:CONVD + HID] = bz
    b_cat[6144:6160] = bb * 0.0
    b_cat[6176:6192] = ba
    b_in = np.ascontiguousarray(b_cat.reshape(NIN, 128).T)

    conv_w_d = np.ascontiguousarray(
        convw.reshape(32, 128, KC).transpose(1, 0, 2).reshape(128, 32 * KC))
    conv_b_d = np.ascontiguousarray(convb.reshape(32, 128).T)
    small = np.zeros((16, 8), f32)
    small[:, 0] = alog
    small[:, 1] = dtb
    small[:, 2] = bb
    norm_w_d = normw.reshape(128, 1)

    WoutT = np.ascontiguousarray(Wout.T)
    wout_b = np.ascontiguousarray(
        WoutT.reshape(16, 128, 16, 128).transpose(2, 1, 0, 3)
        .reshape(16, 128, 2, 1024).transpose(0, 2, 1, 3))
    bout_c = np.ascontiguousarray(bout.reshape(16, 128).T)

    gate_b = np.ascontiguousarray(
        gw.reshape(16, 128, NI, 128).transpose(2, 1, 0, 3)
        .reshape(NI, 128, 2, 1024).transpose(0, 2, 1, 3))
    up_b = np.ascontiguousarray(
        uw.reshape(16, 128, NI, 128).transpose(2, 1, 0, 3)
        .reshape(NI, 128, 2, 1024).transpose(0, 2, 1, 3))
    down_b = np.ascontiguousarray(
        dw.reshape(NI, 128, 2, 1024).transpose(0, 2, 1, 3))
    g_sc = np.ascontiguousarray(gs.reshape(NI, 128).T)
    u_sc = np.ascontiguousarray(us.reshape(NI, 128).T)
    d_sc = np.ascontiguousarray(ds.reshape(16, 128).T)

    i128b = np.eye(128, dtype=bf16)
    i16f = np.eye(16, dtype=f32)
    selh = np.zeros((16, 16 * 128), f32)
    for h in range(16):
        selh[h, h * 128:(h + 1) * 128] = 1.0
    selhb = selh.astype(bf16)
    ones16f = np.ones((16, 128), f32)
    onesb = np.ones((128, 1), bf16)
    onesrb = np.ones((1, 128), bf16)
    onesrf = np.ones((1, 128), f32)
    jj, ff = np.mgrid[0:128, 0:128]
    mask_lo = (ff < jj).astype(bf16)
    mask_ue = (ff >= jj).astype(bf16)
    rmask = np.ones((16, TOK), f32)
    rmask[:, ::C] = 0.0
    e127 = np.zeros((128, 1), f32)
    e127[127, 0] = 1.0
    csts = np.zeros((1, 16), f32)
    csts[0, 0] = LN_EPS
    csts[0, 1] = 128.0
    csts[0, 2] = 1.0
    csts[0, 3] = 128.0 * L2_EPS
    csts[0, 4] = L2_EPS
    csts[0, 5] = -1.0
    csts[0, 6] = 1.0
    csts[0, 7] = 1.0 / 128.0
    csts[0, 8] = RMS_EPS
    cstc = np.broadcast_to(csts, (128, 16)).copy()

    shared = dict(w_in=w_in, b_in=b_in, conv_w=conv_w_d, conv_b=conv_b_d,
                  small=small, norm_w=norm_w_d, wout_b=wout_b, bout_c=bout_c,
                  gate_w=gate_b, up_w=up_b, down_in=down_b, g_scale=g_sc,
                  u_scale=u_sc, d_scale=d_sc, i128b=i128b, i16f=i16f,
                  selhb=selhb, selhf=selh, ones16f=ones16f, onesb=onesb,
                  onesrb=onesrb, onesrf=onesrf, mask_lo=mask_lo,
                  mask_ue=mask_ue, rmask=rmask, e127=e127, csts=csts, cstc=cstc)

    in_maps = []
    for c in range(8):
        b, half = c // 2, c % 2
        xm = np.ascontiguousarray(x[b, half * TOK:(half + 1) * TOK])
        if half == 1:
            xh = np.ascontiguousarray(x[b, TOK - HALO:TOK])
            fl = 1.0
        else:
            xh = np.zeros((HALO, HID), np.float32)
            fl = 0.0
        sel = np.zeros((1, 8), np.float32)
        if half == 1:
            sel[0, c - 1] = 1.0
        m = dict(shared)
        m.update(x_main=xm, x_halo=xh,
                 flag=np.full((1, 1), fl, np.float32), sel8=sel)
        in_maps.append(m)
    return in_maps


def kernel(**inputs):
    nc = _get_nc(debug=False)
    in_maps = _host_inputs(inputs)
    res = run_bass_kernel_spmd(nc, in_maps, core_ids=list(range(8)))
    out = np.empty((4, 2048, HID), np.float32)
    for c in range(8):
        b, half = c // 2, c % 2
        out[b, half * TOK:(half + 1) * TOK] = res.results[c]["out_x"]
    return out


if __name__ == "__main__":
    print("kernel module OK")


# version 17
# speedup vs baseline: 1.0166x; 1.0166x over previous
from contextlib import ExitStack

import numpy as np
import ml_dtypes

import concourse.bass as bass
import concourse.mybir as mybir
from concourse.bass_utils import run_bass_kernel_spmd
from concourse.replica_groups import maybe_share_collective_output_space

F32 = mybir.dt.float32
BF16 = mybir.dt.bfloat16
FP8 = mybir.dt.float8e4
AF = mybir.ActivationFunctionType
ALU = mybir.AluOpType
AX = mybir.AxisListType

HID = 2048
HK, HV, DK, DV, KC = 8, 16, 128, 128, 4
KEY, VAL, CONVD = HK * DK, HV * DV, 4096
TOK, HALO, TT, NTT = 1024, 8, 128, 8
C, NCH = 128, 8
NIN = 49
NI = 48
LN_EPS, L2_EPS, RMS_EPS = 1e-5, 1e-6, 1e-6

ENGS = ("sync", "gpsimd", "tensor", "vector", "scalar")
BSEM = {"sync": "Bs", "gpsimd": "Bg", "tensor": "Bt", "vector": "Bv",
        "scalar": "Ba"}


class KB:
    ESEM = {"vector": "dv", "scalar": "ac"}

    def __init__(self, nc):
        self.nc = nc
        self.streams = {e: [] for e in ENGS}
        self.cnt = {}
        self.hwm = {}
        self.sems = {}
        self.psum_free = {}
        self.wr_hist = {e: [] for e in ENGS}

    @staticmethod
    def _ap_names(vals):
        out = set()
        for v in vals:
            t = getattr(v, "tensor", None)
            if t is not None:
                lo = v.offset
                hi = lo + sum((n - 1) * st for st, n in v.ap) + 1
                out.add((t.name, lo, hi))
        return out

    @staticmethod
    def _overlaps(s1, s2):
        for (n1, a1, b1) in s1:
            for (n2, a2, b2) in s2:
                if n1 == n2 and a1 < b2 and a2 < b1:
                    return True
        return False

    def _hazard_guard(self, eng, meth, a, k):
        if eng not in self.ESEM:
            return
        writes = self._ap_names(a[:1])
        if "accum_out" in k and k["accum_out"] is not None:
            writes |= self._ap_names([k["accum_out"]])
        reads = self._ap_names(a[1:]) | self._ap_names(
            v for kk, v in k.items() if kk != "accum_out")
        hist = self.wr_hist[eng]
        need = False
        for dist in (1, 2):
            if len(hist) >= dist:
                idx, wset, rset = hist[-dist]
                if (self._overlaps(reads, wset) or
                        self._overlaps(writes, wset) or
                        self._overlaps(writes, rset)):
                    need = True
                    writer_idx = idx
                    break
        if need:
            ent = self.streams[eng][writer_idx]
            tail = self.streams[eng][writer_idx + 1:]
            if ent[4]:
                sem = ent[4][-1][0]
            else:
                sem = "hz" + eng[0]
                ent[4].append((sem, 1))
                self.cnt[sem] += 1
            after = sum(n for it in tail if it[0] == "op"
                        for (sm, n) in it[4] if sm == sem)
            self.wait(eng, (sem, self.cnt[sem] - after))
            sp = getattr(self, "spacer", None)
            if sp and eng in sp:
                meth_s, args_s = sp[eng]
                for _ in range(2):
                    self.streams[eng].append(["op", meth_s, args_s, {}, []])
            self.wr_hist[eng] = []
        self.wr_hist[eng].append((len(self.streams[eng]), writes, reads))
        if len(self.wr_hist[eng]) > 2:
            self.wr_hist[eng] = self.wr_hist[eng][-2:]

    def ckpt(self, name):
        if not hasattr(self, "ckpts"):
            self.ckpts = {}
        self.ckpts[name] = {e: len(s) for e, s in self.streams.items()}

    def cut_at(self, name):
        marks = self.ckpts[name]
        for e in ENGS:
            self.streams[e] = self.streams[e][:marks[e]]

    def new_sem(self, stack, name):
        self.sems[name] = stack.enter_context(self.nc.semaphore(name))
        self.cnt[name] = 0

    def do(self, eng, meth, *a, **k):
        self._hazard_guard(eng, meth, a, k)
        self.streams[eng].append(["op", meth, a, k, []])

    def mark(self, eng, sem, n=1):
        assert self.streams[eng] and self.streams[eng][-1][0] == "op", \
            f"mark on empty/wait stream {eng}"
        self.streams[eng][-1][4].append((sem, n))
        self.cnt[sem] += n
        return (sem, self.cnt[sem])

    def tok(self, sem):
        return (sem, self.cnt[sem])

    def wait_p16(self, eng, token):
        if token is None:
            return
        self.wait(eng, (token[0], token[1] + 16))

    def wait(self, eng, token):
        if token is None:
            return
        sem, v = token
        if v <= 0 or self.hwm.get((eng, sem), 0) >= v:
            return
        self.hwm[(eng, sem)] = v
        self.streams[eng].append(["wait", sem, v])

    def barrier(self):
        for e in ("sync", "gpsimd"):
            st = self.streams[e]
            idx = len(st) - 1
            while idx >= 0 and st[idx][0] != "op":
                idx -= 1
            if idx >= 0 and st[idx][1] == "dma_start" and st[idx][4]:
                sem = st[idx][4][-1][0]
                self.streams[e].append(
                    ["op", "dma_start", st[idx][2], dict(st[idx][3]), []])
                self.mark(e, sem, 16)
        toks = []
        for e in ENGS:
            idx = len(self.streams[e]) - 1
            while idx >= 0 and self.streams[e][idx][0] != "op":
                idx -= 1
            if idx < 0:
                toks.append(self.tok(BSEM[e]))
                continue
            ent = self.streams[e][idx]
            if ent[4]:
                toks.append((ent[4][-1][0], self.cnt[ent[4][-1][0]]))
            else:
                n = 16 if e in ("sync", "gpsimd") else 1
                ent[4].append((BSEM[e], n))
                self.cnt[BSEM[e]] += n
                toks.append((BSEM[e], self.cnt[BSEM[e]]))
        for e in ENGS:
            for t in toks:
                self.wait(e, t)

    def psum_wait(self, bank):
        for t in self.psum_free.get(bank, []):
            self.wait("tensor", t)

    def psum_consumed(self, bank, token, add=False):
        if add:
            self.psum_free.setdefault(bank, []).append(token)
        else:
            self.psum_free[bank] = [token]

    def emit(self, block):
        def mk(stream):
            def body(e):
                for it in stream:
                    if it[0] == "wait":
                        e.wait_ge(self.sems[it[1]], it[2])
                    else:
                        _, meth, a, k, incs = it
                        inst = getattr(e, meth)(*a, **k)
                        for (s, n) in incs:
                            inst = inst.then_inc(self.sems[s], n)
            return body
        for e in ENGS:
            getattr(block, e)(mk(self.streams[e]))


def build_nc(debug=False, cut=None, kb_sink=None, debug2=False):
    nc = bass.Bass("TRN2", target_bir_lowering=False)
    dbg_list = []

    def din(name, shape, dt=F32):
        return nc.dram_tensor(name, list(shape), dt, kind="ExternalInput").ap()

    x_main = din("x_main", (TOK, HID))
    x_halo = din("x_halo", (HALO, HID))
    flag = din("flag", (1, 1))
    sel8 = din("sel8", (1, 8))
    w_in = din("w_in", (NIN, 2, 128, 1024))
    b_in = din("b_in", (128, NIN))
    conv_w = din("conv_w", (128, 32 * KC))
    conv_b = din("conv_b", (128, 32))
    small = din("small", (16, 8))
    norm_w = din("norm_w", (128, 1))
    wout_b = din("wout_b", (16, 2, 128, 1024))
    bout_c = din("bout_c", (128, 16))
    gate_w = din("gate_w", (NI, 2, 128, 1024))
    up_w = din("up_w", (NI, 2, 128, 1024))
    down_in = din("down_in", (NI, 2, 128, 1024))
    g_scale = din("g_scale", (128, NI))
    u_scale = din("u_scale", (128, NI))
    d_scale = din("d_scale", (128, 16))
    i128b = din("i128b", (128, 128), BF16)
    i16f = din("i16f", (16, 16))
    selhb = din("selhb", (16, 16 * 128), BF16)
    selhf = din("selhf", (16, 16 * 128))
    ones16f = din("ones16f", (16, 128))
    onesb = din("onesb", (128, 1), BF16)
    onesrb = din("onesrb", (1, 128), BF16)
    onesrf = din("onesrf", (1, 128))
    mask_lo = din("mask_lo", (128, 128), BF16)
    mask_ue = din("mask_ue", (128, 128), BF16)
    rmask = din("rmask", (16, TOK))
    e127 = din("e127", (128, 1))
    csts = din("csts", (1, 16))
    cstc = din("cstc", (128, 16))

    out_x = nc.dram_tensor("out_x", [TOK, HID], F32, kind="ExternalOutput").ap()

    def dscr(name, shape, dt=F32, **kw):
        return nc.dram_tensor(name, list(shape), dt, kind="Internal", **kw).ap()

    pan_hbm = dscr("pan_hbm", (NCH, 128, HV * 256), BF16)
    gt_hbm = dscr("gt_hbm", (NCH, 128, HV * 128), BF16)
    kt_hbm = dscr("kt_hbm", (NCH, 128, HK * 128), BF16)
    z_hbm = dscr("z_hbm", (HV, 128, TOK), BF16)
    x1_hbm = dscr("x1_hbm", (TOK, HID), F32)
    ln3_hbm = dscr("ln3_hbm", (2, 16, 128, 512), BF16)
    gw_bf = dscr("gw_bf", (NI, 2, 128, 1024), BF16)
    uw_bf = dscr("uw_bf", (NI, 2, 128, 1024), BF16)
    dn_bf = dscr("dn_bf", (NI, 2, 128, 1024), BF16)
    f8mid = dscr("f8mid", (3, 2, 128, 1024), FP8)
    rg = [[0, 1, 2, 3, 4, 5, 6, 7]]
    aspace = "Local"
    cc_in = dscr("cc_in", (128, HV * DV), BF16)
    cc_out = dscr("cc_out", (8 * 128, HV * DV), BF16, addr_space=aspace)

    with ExitStack() as st:
        E = st.enter_context
        kb = KB(nc)
        for s in ("wdma", "gdma", "pe", "dv", "ac",
                  "Bs", "Bg", "Bt", "Bv", "Ba", "hzv", "hzs"):
            kb.new_sem(st, s)

        pb = [E(nc.psum_tensor(f"pb{i}", [128, 512], F32)) for i in range(8)]

        def sb(n, sh, dt):
            return E(nc.sbuf_tensor(n, list(sh), dt))

        s_state = [sb(f"s_state{i}", (128, HV * DV), BF16) for i in range(2)]
        s_cst = sb("s_cst", (1, 16), F32)
        s_cstc = sb("s_cstc", (128, 16), F32)
        s_small = sb("s_small", (16, 8), F32)
        s_bias = sb("s_bias", (128, NIN), F32)
        s_convw = sb("s_convw", (128, 32 * KC), F32)
        s_convb = sb("s_convb", (128, 32), F32)
        s_normw = sb("s_normw", (128, 1), F32)
        s_boutc = sb("s_boutc", (128, 16), F32)
        s_gsc = sb("s_gsc", (128, NI), F32)
        s_usc = sb("s_usc", (128, NI), F32)
        s_dsc = sb("s_dsc", (128, 16), F32)
        s_i128b = sb("s_i128b", (128, 128), BF16)
        s_i16f = sb("s_i16f", (16, 16), F32)
        s_ones16f = sb("s_ones16f", (16, 128), F32)
        s_onesb = sb("s_onesb", (128, 1), BF16)
        s_onesrb = sb("s_onesrb", (1, 128), BF16)
        s_onesrf = sb("s_onesrf", (1, 128), F32)
        s_mlo = sb("s_mlo", (128, 128), BF16)
        s_mue = sb("s_mue", (128, 128), BF16)
        s_rmask = sb("s_rmask", (16, TOK), F32)
        s_e127 = sb("s_e127", (128, 1), F32)
        s_flag = sb("s_flag", (1, 1), F32)
        s_sel8 = sb("s_sel8", (1, 8), F32)
        s_flagcol = sb("s_flagcol", (128, 1), F32)
        s_selcol = sb("s_selcol", (128, 8), F32)
        s_negea = sb("s_negea", (16, 1), F32)
        s_beta = sb("s_beta", (16, TOK), F32)
        s_g = sb("s_g", (16, TOK), F32)
        s_c = sb("s_c", (16, TOK), F32)
        s_arow = sb("s_arow", (16, TOK), BF16)
        s_cT = sb("s_cT", (128, NCH * 16), F32)
        s_bT = sb("s_bT", (128, NCH * 16), F32)
        s_bTn = sb("s_bTn", (128, NCH * 16), F32)
        s_bA = sb("s_bA", (128, NCH * 16), F32)
        s_kbar = sb("s_kbar", (128, NCH * 16), F32)
        s_acb = sb("s_acb", (128, NCH * 16), F32)
        s_dg16 = sb("s_dg16", (16, 16), F32)
        s_sqd = sb("s_sqd", (128, 1024), BF16)
        wf = [sb(f"wf{i}", (128, 1024), F32) for i in range(2)]
        wb = [sb(f"wb{i}", (128, 2048), BF16) for i in range(2)]

        w8e = {"vector": sb("w8v", (128, 1024), FP8),
               "scalar": sb("w8s", (128, 1024), FP8)}
        s_rows = sb("s_rows", (1, TOK), F32)
        s_rowb = sb("s_rowb", (1, TOK), BF16)
        s_spc = sb("s_spc", (1, 8), F32)
        s_col = sb("s_col", (128, 4), F32)
        s_edf = sb("s_edf", (128, 128), F32)

        wf_free = [None, None]
        wb_free = [None, None]

        kb.spacer = {"vector": ("memset", (s_spc[0:1, 0:2], 0.0)),
                     "scalar": ("copy", (s_spc[0:1, 2:3], s_spc[0:1, 3:4]))}

        def dbg(name, ap, shape, dt):
            if not debug:
                return
            d = nc.dram_tensor("dbg_" + name, list(shape), dt,
                               kind="ExternalOutput").ap()
            kb.barrier()
            kb.do("gpsimd", "dma_start", d, ap)
            kb.mark("gpsimd", "gdma", 16)
            kb.barrier()

        def snap(name, ap, shape, dt):
            if not debug2:
                return
            d = nc.dram_tensor("snap_" + name, list(shape), dt,
                               kind="ExternalOutput").ap()
            kb.do("gpsimd", "dma_start", d, ap)
            kb.mark("gpsimd", "gdma", 16)

        lw_half = [0]

        def load_wblk(src_ap, widx, cvt_fp8, parity):
            eng = "vector" if parity == 0 else "scalar"
            sem = "dv" if parity == 0 else "ac"
            meth = "tensor_copy" if eng == "vector" else "copy"
            for hh in range(2):
                wfi = lw_half[0] & 1
                lw_half[0] += 1
                kb.wait("sync", wf_free[wfi])
                kb.do("sync", "dma_start", wf[wfi][:], src_ap[hh])
                kb.mark("sync", "wdma", 16)
                tw = kb.tok("wdma")
                kb.wait_p16(eng, tw)
                if hh == 0:
                    kb.wait(eng, wb_free[widx])
                if cvt_fp8:
                    w8 = w8e[eng]
                    kb.do(eng, meth, w8[:], wf[wfi][:])
                    kb.do(eng, meth,
                          wb[widx][:, hh * 1024:(hh + 1) * 1024], w8[:])
                else:
                    kb.do(eng, meth,
                          wb[widx][:, hh * 1024:(hh + 1) * 1024], wf[wfi][:])
                t = kb.mark(eng, sem)
                wf_free[wfi] = t
            return t

        conv_seq = []
        for ib in range(NI):
            conv_seq.append((gate_w[ib], gw_bf[ib]))
            conv_seq.append((up_w[ib], uw_bf[ib]))
            conv_seq.append((down_in[ib], dn_bf[ib]))
        conv_state = {"i": 0}

        conv_emitted = {"d1": [], "d2": 0}

        def conv_pump(nblk=1):
            for _ in range(nblk):
                i = conv_state["i"]
                n = len(conv_seq)
                if i >= n + 1:
                    return
                conv_state["i"] = i + 1
                if i < n:
                    src, dst = conv_seq[i]
                    if i >= 3:
                        kb.wait("gpsimd", conv_emitted["d1"][i - 1])
                    kb.do("gpsimd", "dma_start", f8mid[i % 3], src)
                    kb.mark("gpsimd", "gdma", 16)
                    conv_emitted["d1"].append(kb.tok("gdma"))
                j = conv_emitted["d2"]
                if j < n - 1 and j == i - 1:
                    kb.wait("gpsimd", conv_emitted["d1"][j + 1])
                    kb.do("gpsimd", "dma_start", conv_seq[j][1],
                          f8mid[j % 3])
                    kb.mark("gpsimd", "gdma", 16)
                    conv_emitted["d2"] = j + 1
                if i == n:
                    kb.do("gpsimd", "dma_start", f8mid[(n - 1) % 3],
                          conv_seq[n - 1][0])
                    kb.mark("gpsimd", "gdma", 16)
                    kb.wait("gpsimd", kb.tok("gdma"))
                    kb.do("gpsimd", "dma_start", conv_seq[n - 1][1],
                          f8mid[(n - 1) % 3])
                    kb.mark("gpsimd", "gdma", 16)
                    conv_emitted["d2"] = n

        ln_state = {"prev": None}

        def ln_apply(xb, nrows, out_ap, stat,
                     add_ap=None, out_f32=None):
            kb.wait("scalar", ln_state["prev"])
            kb.do("vector", "reduce_sum", stat[0:nrows, 0:1],
                  xb[0:nrows, :], AX.X)
            kb.do("scalar", "activation", s_sqd[0:nrows, :],
                  xb[0:nrows, 0:1024], AF.Square,
                  accum_out=stat[0:nrows, 1:2])
            kb.do("scalar", "activation", s_sqd[0:nrows, :],
                  xb[0:nrows, 1024:2048], AF.Square,
                  accum_out=stat[0:nrows, 7:8])
            ta = kb.mark("scalar", "ac")
            kb.do("vector", "tensor_scalar_mul", stat[0:nrows, 2:3],
                  stat[0:nrows, 0:1], 1.0 / HID)
            kb.wait("vector", ta)
            kb.do("vector", "tensor_tensor", stat[0:nrows, 1:2],
                  stat[0:nrows, 1:2], stat[0:nrows, 7:8], ALU.add)
            kb.do("vector", "tensor_scalar_mul", stat[0:nrows, 3:4],
                  stat[0:nrows, 1:2], 1.0 / HID)
            kb.do("vector", "tensor_tensor", stat[0:nrows, 4:5],
                  stat[0:nrows, 2:3], stat[0:nrows, 2:3], ALU.mult)
            kb.do("vector", "tensor_tensor", stat[0:nrows, 4:5],
                  stat[0:nrows, 3:4], stat[0:nrows, 4:5], ALU.subtract)
            tv = kb.mark("vector", "dv")
            kb.wait("scalar", tv)
            kb.do("scalar", "activation", stat[0:nrows, 5:6],
                  stat[0:nrows, 4:5], AF.Sqrt, bias=s_cstc[0:nrows, 0:1])
            ta = kb.mark("scalar", "ac")
            kb.wait("vector", ta)
            kb.do("vector", "reciprocal", stat[0:nrows, 6:7],
                  stat[0:nrows, 5:6])
            if out_ap is not None:
                kb.do("vector", "tensor_scalar", out_ap,
                      xb[0:nrows, :], stat[0:nrows, 2:3], stat[0:nrows, 6:7],
                      ALU.subtract, ALU.mult)
            if add_ap is not None:
                kb.do("vector", "tensor_scalar", out_f32,
                      xb[0:nrows, :], stat[0:nrows, 2:3], stat[0:nrows, 6:7],
                      ALU.subtract, ALU.mult)
                kb.do("vector", "tensor_tensor", out_f32, out_f32, add_ap,
                      ALU.add)
            t = kb.mark("vector", "dv")
            ln_state["prev"] = t
            return t

        for dst, src in (
            (s_cst[:], csts), (s_cstc[:], cstc),
            (s_small[:], small), (s_bias[:], b_in),
            (s_convw[:], conv_w), (s_convb[:], conv_b), (s_normw[:], norm_w),
            (s_boutc[:], bout_c), (s_gsc[:], g_scale), (s_usc[:], u_scale),
            (s_dsc[:], d_scale), (s_i128b[:], i128b), (s_i16f[:], i16f),
            (s_ones16f[:], ones16f),
            (s_onesb[:], onesb), (s_onesrb[:], onesrb), (s_onesrf[:], onesrf),
            (s_mlo[:], mask_lo), (s_mue[:], mask_ue), (s_rmask[:], rmask),
            (s_e127[:], e127), (s_flag[:], flag), (s_sel8[:], sel8),
        ):
            kb.do("sync", "dma_start", dst, src)
            kb.mark("sync", "wdma", 16)
        kb.do("sync", "dma_start", s_cst[:], csts)
        kb.mark("sync", "wdma", 16)
        tc0 = kb.tok("wdma")
        for e in ("vector", "scalar", "tensor"):
            kb.wait(e, tc0)
        kb.do("scalar", "activation", s_negea[:, 0:1], s_small[:, 0:1], AF.Exp)
        ta = kb.mark("scalar", "ac")
        kb.wait("vector", ta)
        kb.do("vector", "tensor_scalar_mul", s_negea[:, 0:1],
              s_negea[:, 0:1], -1.0)
        kb.do("vector", "memset", s_state[0][:], 0.0)
        kb.do("vector", "memset", s_state[1][:], 0.0)
        kb.psum_wait(4)
        kb.do("tensor", "matmul", pb[4][:, 0:1], s_onesrf[:], s_flag[:],
              start=True, stop=True)
        kb.do("tensor", "matmul", pb[4][:, 8:16], s_onesrf[:], s_sel8[:],
              start=True, stop=True)
        tp = kb.mark("tensor", "pe")
        kb.wait("vector", tp)
        kb.do("vector", "tensor_copy", s_flagcol[:], pb[4][:, 0:1])
        kb.do("vector", "tensor_copy", s_selcol[:], pb[4][:, 8:16])
        tv = kb.mark("vector", "dv")
        kb.psum_consumed(4, tv)
        kb.barrier()
        kb.ckpt("p0")

        qkv_guard = nc.sbuf_tensor("qkv_s", [128, 32 * TOK], BF16)
        qkv_parent = qkv_guard.__enter__()
        qkv = [qkv_parent[:, j * TOK:(j + 1) * TOK] for j in range(32)]
        core = [qkv_parent[:, (16 + j) * TOK:(17 + j) * TOK]
                for j in range(16)]
        lnx_guards = [nc.sbuf_tensor(f"lnx{j}", [128, HALO + TOK], BF16)
                      for j in range(16)]
        lnx = [g.__enter__() for g in lnx_guards]
        if True:

            with nc.sbuf_tensor("xt0", [128, HID], F32) as xt0, \
                 nc.sbuf_tensor("xt1", [128, HID], F32) as xt1, \
                 nc.sbuf_tensor("lntok", [128, HID], BF16) as lntok, \
                 nc.sbuf_tensor("stat1", [128, 8], F32) as stat1:
                xb_ = [xt0, xt1]
                xfree = [None, None]
                lntok_free = [None]

                def p1_tile(xb, nrows, coloff):
                    kb.wait("vector", lntok_free[0])
                    tln = ln_apply(xb, nrows, lntok[0:nrows, :], stat1)
                    xb_done = (kb.tok("dv"), kb.tok("ac"))
                    for j in range(16):
                        bank = 2 + (j & 1)
                        kb.wait("tensor", tln)
                        kb.psum_wait(bank)
                        rhs = s_i128b[:] if nrows == 128 \
                            else s_i128b[0:nrows, 0:nrows]
                        kb.do("tensor", "matmul", pb[bank][0:128, 0:nrows],
                              lntok[0:nrows, j * 128:(j + 1) * 128], rhs,
                              start=True, stop=True)
                        tpj = kb.mark("tensor", "pe")
                        kb.wait("scalar", tpj)
                        kb.do("scalar", "copy",
                              lnx[j][:, coloff:coloff + nrows],
                              pb[bank][0:128, 0:nrows])
                        taj = kb.mark("scalar", "ac")
                        kb.psum_consumed(bank, taj)
                    lntok_free[0] = kb.tok("pe")
                    return xb_done

                kb.do("sync", "dma_start", xt0[0:HALO, :], x_halo)
                kb.mark("sync", "wdma", 16)
                kb.do("sync", "dma_start", xt0[0:HALO, :], x_halo)
                kb.mark("sync", "wdma", 16)
                th = kb.tok("wdma")
                kb.wait("vector", th)
                kb.wait("scalar", th)
                xfree[0] = p1_tile(xt0, HALO, 0)
                for t in range(NTT):
                    b = t & 1
                    if xfree[b] is not None:
                        for tkn in xfree[b]:
                            kb.wait("sync", tkn)
                    kb.do("sync", "dma_start", xb_[b][:],
                          x_main[t * TT:(t + 1) * TT, :])
                    kb.mark("sync", "wdma", 16)
                    tw = kb.tok("wdma")
                    kb.wait_p16("vector", tw)
                    kb.wait_p16("scalar", tw)
                    xfree[b] = p1_tile(xb_[b], 128, HALO + t * TT)
            kb.barrier()
            kb.ckpt("p1")
            snap("lnx0", lnx[0][:], (128, HALO + TOK), BF16)
            snap("lnx8", lnx[8][:], (128, HALO + TOK), BF16)
            if debug:
                dbg("lnx0", lnx[0][:], (128, HALO + TOK), BF16)
                dbg("lnx15", lnx[15][:], (128, HALO + TOK), BF16)

            with ExitStack() as sc_p2:
                E2 = sc_p2.enter_context
                mixb = [E2(nc.sbuf_tensor(f"mix{i}", [128, HALO + TOK], BF16))
                        for i in range(2)]
                cac = [E2(nc.sbuf_tensor(f"cac{i}", [128, TOK], BF16))
                       for i in range(2)]
                silu_f = E2(nc.sbuf_tensor("silu_f", [128, TOK], BF16))
                sq2 = E2(nc.sbuf_tensor("sq2", [128, TOK], BF16))
                ba_raw = E2(nc.sbuf_tensor("ba_raw", [48, TOK], F32))
                zst = [E2(nc.sbuf_tensor(f"zst{i}", [128, TOK], BF16))
                       for i in range(2)]

                mix_free = [None, None]
                rows_free = [None]
                zst_free = [None, None]
                silu_free = None
                sq2_free = None
                cac_free = [None, None]

                wtok = load_wblk(w_in[0], 0, False, 0)
                for o in range(NIN):
                    widx = o & 1
                    nxt = None
                    if o + 1 < NIN:
                        nxt = load_wblk(
                            w_in[o + 1], (o + 1) & 1, False, (o + 1) & 1)
                    kb.wait("tensor", wtok)
                    is_mix, is_z = o < 32, 32 <= o < 48
                    mt = mixb[o & 1] if is_mix else None
                    slices = ([(0, HALO)] if is_mix else []) + \
                        [(HALO, 512), (HALO + 512, 512)]
                    for si, (soff, nn) in enumerate(slices):
                        bank = (2 * o + si) % 4
                        kb.psum_wait(bank)
                        for kk in range(16):
                            kb.do("tensor", "matmul", pb[bank][0:128, 0:nn],
                                  wb[widx][:, kk * 128:(kk + 1) * 128],
                                  lnx[kk][:, soff:soff + nn],
                                  start=(kk == 0), stop=(kk == 15))
                        tpp = kb.mark("tensor", "pe")
                        wb_free[widx] = tpp
                        if is_mix:
                            kb.wait("vector", tpp)
                            if si == 0:
                                kb.wait("vector", mix_free[o & 1])
                            kb.do("vector", "tensor_scalar",
                                  mt[:, soff:soff + nn],
                                  pb[bank][0:128, 0:nn],
                                  s_bias[:, o:o + 1], None, ALU.add)
                            te = kb.mark("vector", "dv")
                        elif is_z:
                            kb.wait("scalar", tpp)
                            if si == 0:
                                kb.wait("scalar", zst_free[o & 1])
                            kb.do("scalar", "activation",
                                  zst[o & 1][:, soff - HALO:soff - HALO + nn],
                                  pb[bank][0:128, 0:nn], AF.Silu,
                                  bias=s_bias[:, o:o + 1])
                            te = kb.mark("scalar", "ac")
                        else:
                            kb.wait("vector", tpp)
                            kb.do("vector", "tensor_scalar",
                                  ba_raw[0:48, soff - HALO:soff - HALO + nn],
                                  pb[bank][0:48, 0:nn],
                                  s_bias[0:48, o:o + 1], None, ALU.add)
                            te = kb.mark("vector", "dv")
                        kb.psum_consumed(bank, te)
                    if is_z:
                        tz = kb.tok("ac")
                        kb.wait("gpsimd", tz)
                        kb.do("gpsimd", "dma_start",
                              z_hbm[o - 32],
                              zst[o & 1][:])
                        zst_free[o & 1] = kb.mark("gpsimd", "gdma", 16)
                    if is_mix:
                        kb.do("vector", "tensor_scalar_mul", mt[:, 0:HALO],
                              mt[:, 0:HALO], s_flagcol[:, 0:1])
                        kb.wait("vector", cac_free[0])
                        kb.do("vector", "tensor_scalar", cac[0][:],
                              mt[:, HALO:HALO + TOK],
                              s_convw[:, 4 * o + 3:4 * o + 4], None, ALU.mult)
                        kb.wait("vector", cac_free[1])
                        for d in (1, 2, 3):
                            src_acc = cac[(d + 1) & 1]
                            dst_acc = cac[d & 1]
                            kb.do("vector", "scalar_tensor_tensor",
                                  dst_acc[:], mt[:, HALO - d:HALO - d + TOK],
                                  s_convw[:, 4 * o + 3 - d:4 * o + 4 - d],
                                  src_acc[:], ALU.mult, ALU.add)
                        tcv = kb.mark("vector", "dv")
                        mix_free[o & 1] = tcv
                        acc = cac[1]
                        kb.wait("scalar", tcv)
                        if o >= 16:
                            kb.do("scalar", "activation", qkv[o], acc[:],
                                  AF.Silu, bias=s_convb[:, o:o + 1])
                            cac_free[1] = kb.mark("scalar", "ac")
                            cac_free[0] = cac_free[1]
                        else:
                            kb.wait("scalar", silu_free)
                            kb.do("scalar", "activation", silu_f[:], acc[:],
                                  AF.Silu, bias=s_convb[:, o:o + 1])
                            cac_free[1] = kb.mark("scalar", "ac")
                            cac_free[0] = cac_free[1]
                            kb.wait("scalar", sq2_free)
                            kb.do("scalar", "activation", sq2[:], silu_f[:],
                                  AF.Square)
                            tsq = kb.mark("scalar", "ac")
                            kb.wait("tensor", tsq)
                            kb.psum_wait(6)
                            kb.do("tensor", "matmul", pb[6][0:1, 0:512],
                                  s_onesb[:], sq2[:, 0:512],
                                  start=True, stop=True)
                            kb.psum_wait(5)
                            kb.do("tensor", "matmul", pb[5][0:1, 0:512],
                                  s_onesb[:], sq2[:, 512:1024],
                                  start=True, stop=True)
                            tp6 = kb.mark("tensor", "pe")
                            sq2_free = tp6
                            sc_i = 1 if o < 8 else 2
                            kb.wait("scalar", tp6)
                            kb.wait("scalar", rows_free[0])
                            kb.do("scalar", "activation", s_rows[0:1, 0:512],
                                  pb[6][0:1, 0:512], AF.Sqrt,
                                  scale=s_cst[0:1, sc_i:sc_i + 1],
                                  bias=s_cst[0:1, sc_i + 2:sc_i + 3])
                            kb.do("scalar", "activation", s_rows[0:1, 512:1024],
                                  pb[5][0:1, 0:512], AF.Sqrt,
                                  scale=s_cst[0:1, sc_i:sc_i + 1],
                                  bias=s_cst[0:1, sc_i + 2:sc_i + 3])
                            ta6 = kb.mark("scalar", "ac")
                            kb.psum_consumed(6, ta6)
                            kb.psum_consumed(5, ta6)
                            kb.wait("vector", ta6)
                            kb.do("vector", "reciprocal", s_rows[0:1, :],
                                  s_rows[0:1, :])
                            kb.do("vector", "tensor_copy", s_rowb[0:1, :],
                                  s_rows[0:1, :])
                            trr = kb.mark("vector", "dv")
                            rows_free[0] = trr
                            kb.wait("tensor", trr)
                            kb.psum_wait(6)
                            kb.do("tensor", "matmul", pb[6][:, 0:512],
                                  s_onesrb[:], s_rowb[0:1, 0:512],
                                  start=True, stop=True)
                            kb.psum_wait(5)
                            kb.do("tensor", "matmul", pb[5][:, 0:512],
                                  s_onesrb[:], s_rowb[0:1, 512:1024],
                                  start=True, stop=True)
                            tph = kb.mark("tensor", "pe")
                            kb.wait("vector", tph)
                            kb.do("vector", "tensor_tensor",
                                  qkv[o][:, 0:512], silu_f[:, 0:512],
                                  pb[6][:, 0:512], ALU.mult)
                            kb.do("vector", "tensor_tensor",
                                  qkv[o][:, 512:1024], silu_f[:, 512:1024],
                                  pb[5][:, 0:512], ALU.mult)
                            tqn = kb.mark("vector", "dv")
                            silu_free = tqn
                            kb.psum_consumed(6, tqn)
                            kb.psum_consumed(5, tqn)
                    wtok = nxt
                kb.barrier()

                kb.do("scalar", "activation", s_beta[:], ba_raw[0:16, :],
                      AF.Sigmoid, bias=s_small[:, 2:3])
                kb.do("vector", "tensor_scalar", s_g[:], ba_raw[32:48, :],
                      s_small[:, 1:2], None, ALU.add)
                tg = kb.mark("vector", "dv")
                kb.wait("scalar", tg)
                kb.do("scalar", "activation", ba_raw[0:16, :],
                      s_g[:], AF.Abs)
                t1 = kb.mark("scalar", "ac")
                kb.do("scalar", "activation", ba_raw[0:16, :],
                      ba_raw[0:16, :], AF.Exp, scale=s_cstc[0:16, 5:6])
                kb.do("scalar", "activation", ba_raw[0:16, :],
                      ba_raw[0:16, :], AF.Ln, bias=s_cstc[0:16, 6:7])
                t2 = kb.mark("scalar", "ac")
                kb.wait("vector", t2)
                kb.do("vector", "scalar_tensor_tensor", ba_raw[0:16, :],
                      s_g[:], 0.0, ba_raw[0:16, :], ALU.max, ALU.add)
                kb.do("vector", "tensor_scalar", s_g[:], ba_raw[0:16, :],
                      s_negea[:, 0:1], None, ALU.mult)
                kb.do("vector", "tensor_tensor_scan", s_c[:], s_rmask[:],
                      s_g[:], 0.0, ALU.mult, ALU.add)
                tcs = kb.mark("vector", "dv")
                kb.wait("scalar", tcs)
                kb.do("scalar", "activation", s_arow[:], s_c[:], AF.Exp)
                kb.mark("scalar", "ac")
                kb.barrier()
                kb.ckpt("p2")
                snap("qkv0", qkv[0], (128, TOK), BF16)
                snap("qkv8", qkv[8], (128, TOK), BF16)
                snap("qkv16", qkv[16], (128, TOK), BF16)
                snap("qkv31", qkv[31], (128, TOK), BF16)
                snap("beta", s_beta[:], (16, TOK), F32)
                snap("g", s_g[:], (16, TOK), F32)
                snap("arow", s_arow[:], (16, TOK), BF16)
                if debug:
                    dbg("beta", s_beta[:], (16, TOK), F32)
                    dbg("g", s_g[:], (16, TOK), F32)
                    dbg("c", s_c[:], (16, TOK), F32)
                    dbg("qkv0", qkv[0], (128, TOK), BF16)
                    dbg("qkv8", qkv[8], (128, TOK), BF16)
                    dbg("qkv16", qkv[16], (128, TOK), BF16)
                    dbg("qkv31", qkv[31], (128, TOK), BF16)

            for g in reversed(lnx_guards):
                g.__exit__(None, None, None)

            sel_guards = [nc.sbuf_tensor("s_selhf", [16, 16 * 128], F32),
                          nc.sbuf_tensor("s_selhb", [16, 16 * 128], BF16)]
            s_selhf, s_selhb = [g.__enter__() for g in sel_guards]
            kb.do("sync", "dma_start", s_selhf[:], selhf)
            kb.mark("sync", "wdma", 16)
            kb.do("sync", "dma_start", s_selhb[:], selhb)
            kb.mark("sync", "wdma", 16)
            tsel_ld = kb.tok("wdma")
            kb.wait_p16("tensor", tsel_ld)

            for i in range(NCH):
                conv_pump(2)
                cs = slice(i * C, (i + 1) * C)
                co = i * 16
                kb.psum_wait(2)
                kb.do("tensor", "matmul", pb[2][0:128, 0:16],
                      s_c[:, cs], s_i16f[:], start=True, stop=True)
                kb.do("tensor", "matmul", pb[2][0:128, 16:32],
                      s_beta[:, cs], s_i16f[:], start=True, stop=True)
                tpc = kb.mark("tensor", "pe")
                kb.wait("vector", tpc)
                kb.do("vector", "tensor_copy", s_cT[:, co:co + 16],
                      pb[2][0:128, 0:16])
                kb.do("vector", "tensor_copy", s_bT[:, co:co + 16],
                      pb[2][0:128, 16:32])
                kb.do("vector", "tensor_scalar_mul", s_bTn[:, co:co + 16],
                      pb[2][0:128, 16:32], -1.0)
                tvc = kb.mark("vector", "dv")
                kb.psum_consumed(2, tvc)
                kb.wait("scalar", tvc)
                kb.do("scalar", "activation", s_bA[:, co:co + 16],
                      s_cT[:, co:co + 16], AF.Exp)
                tac = kb.mark("scalar", "ac")
                kb.wait("vector", tac)
                kb.do("vector", "tensor_tensor", s_bA[:, co:co + 16],
                      s_bT[:, co:co + 16], s_bA[:, co:co + 16], ALU.mult)
                kb.wait("tensor", tvc)
                kb.psum_wait(3)
                kb.do("tensor", "matmul", pb[3][0:16, 0:1],
                      s_cT[:, co:co + 16], s_e127[:],
                      start=True, stop=True)
                tp3 = kb.mark("tensor", "pe")
                kb.wait("vector", tp3)
                kb.do("vector", "tensor_copy", s_col[0:16, 0:1],
                      pb[3][0:16, 0:1])
                tv3 = kb.mark("vector", "dv")
                kb.psum_consumed(3, tv3)
                kb.do("vector", "tensor_scalar", s_dg16[:],
                      s_i16f[:], s_col[0:16, 0:1], None, ALU.mult)
                tdg = kb.mark("vector", "dv")
                kb.wait("tensor", tdg)
                kb.psum_wait(3)
                kb.do("tensor", "matmul", pb[3][0:128, 0:16],
                      s_ones16f[:], s_dg16[:], start=True, stop=True)
                tp3b = kb.mark("tensor", "pe")
                kb.wait("scalar", tp3b)
                kb.do("scalar", "activation", s_acb[:, co:co + 16],
                      pb[3][0:128, 0:16], AF.Exp)
                ta3 = kb.mark("scalar", "ac")
                kb.wait("vector", tp3b)
                kb.do("vector", "tensor_tensor", s_kbar[:, co:co + 16],
                      pb[3][0:128, 0:16], s_cT[:, co:co + 16],
                      ALU.subtract)
                kb.do("vector", "tensor_single_scalar",
                      s_kbar[:, co:co + 16], s_kbar[:, co:co + 16],
                      0.0, ALU.min)
                tv4 = kb.mark("vector", "dv")
                kb.psum_consumed(3, ta3)
                kb.psum_consumed(3, tv4, add=True)
                kb.wait("scalar", tv4)
                kb.do("scalar", "activation", s_kbar[:, co:co + 16],
                      s_kbar[:, co:co + 16], AF.Exp)
                kb.mark("scalar", "ac")
            kb.barrier()
            kb.ckpt("p3")
            snap("cT", s_cT[:], (128, NCH * 16), F32)
            snap("kbar", s_kbar[:], (128, NCH * 16), F32)
            snap("acb", s_acb[:], (128, NCH * 16), F32)
            if debug:
                dbg("cT", s_cT[:], (128, NCH * 16), F32)
                dbg("kbar", s_kbar[:], (128, NCH * 16), F32)
                dbg("acb", s_acb[:], (128, NCH * 16), F32)

            with ExitStack() as sc4:
                E4 = sc4.enter_context
                s_nt = [E4(nc.sbuf_tensor(f"s_nt{k}", [128, 128], BF16))
                        for k in range(7)]
                s_np = [E4(nc.sbuf_tensor(f"s_np{k}", [128, 128], BF16))
                        for k in range(2)]
                s_gm = E4(nc.sbuf_tensor("s_gm", [128, 128], BF16))
                s_edN = E4(nc.sbuf_tensor("s_edN", [128, 128], BF16))
                s_edG = E4(nc.sbuf_tensor("s_edG", [128, 128], BF16))
                s_wkT = E4(nc.sbuf_tensor("s_wkT", [128, 128], BF16))
                s_wfl = E4(nc.sbuf_tensor("s_wfl", [128, 128], BF16))
                s_kb_ = E4(nc.sbuf_tensor("s_kb_", [128, 128], BF16))
                s_pan2 = [E4(nc.sbuf_tensor(f"s_pan2{k}", [128, 256],
                                            BF16)) for k in range(2)]
                panC = [E4(nc.sbuf_tensor(f"panC{k}", [128, HV * 256],
                                          BF16)) for k in range(2)]
                gtC = [E4(nc.sbuf_tensor(f"gtC{k}", [128, HV * 128],
                                         BF16)) for k in range(2)]
                ktC = [E4(nc.sbuf_tensor(f"ktC{k}", [128, HK * 128],
                                         BF16)) for k in range(2)]
                panC_free = [None, None]
                gtC_free = [None, None]
                ktC_free = [None, None]
                t_scr = None

                def chunk_pass1(i):
                    nonlocal t_scr
                    pp = i & 1
                    cs = slice(i * C, (i + 1) * C)
                    co = i * 16
                    kb.wait("vector", panC_free[pp])
                    kb.wait("vector", gtC_free[pp])
                    kb.wait("vector", ktC_free[pp])
                    for h in range(HV):
                        conv_pump(1)
                        kq = h // 2
                        gbank = 3 + (h & 1)
                        kb.wait("tensor", t_scr)
                        kb.psum_wait(gbank)
                        kb.do("tensor", "matmul", pb[gbank][0:128, 0:128],
                              qkv[8 + kq][:, cs], qkv[8 + kq][:, cs],
                              start=True, stop=True)
                        kb.do("tensor", "matmul",
                              pb[gbank][0:128, 128:256],
                              qkv[8 + kq][:, cs], qkv[kq][:, cs],
                              start=True, stop=True)
                        kb.psum_wait(5)
                        kb.do("tensor", "matmul", pb[5][0:128, 0:128],
                              s_selhf[:, h * 128:(h + 1) * 128],
                              s_c[:, cs], start=True, stop=True)
                        tpg = kb.mark("tensor", "pe")
                        kb.wait("vector", tpg)
                        kb.do("vector", "tensor_scalar", s_edf[:],
                              pb[5][0:128, 0:128],
                              s_cT[:, co + h:co + h + 1], 0.0,
                              ALU.subtract, ALU.max)
                        tv1 = kb.mark("vector", "dv")
                        kb.wait("scalar", tv1)
                        kb.do("scalar", "activation", s_edN[:], s_edf[:],
                              AF.Exp, scale=s_cstc[0:128, 5:6])
                        te1 = kb.mark("scalar", "ac")
                        kb.wait("vector", te1)
                        kb.do("vector", "tensor_scalar", s_edf[:],
                              pb[5][0:128, 0:128],
                              s_cT[:, co + h:co + h + 1], 0.0,
                              ALU.subtract, ALU.min)
                        tv2 = kb.mark("vector", "dv")
                        kb.psum_consumed(5, tv2)
                        kb.wait("scalar", tv2)
                        kb.do("scalar", "activation", s_edG[:], s_edf[:],
                              AF.Exp)
                        tEd = kb.mark("scalar", "ac")
                        kb.wait("vector", tEd)
                        kb.do("vector", "tensor_tensor", s_gm[:],
                              pb[gbank][0:128, 0:128], s_mlo[:], ALU.mult)
                        kb.do("vector", "scalar_tensor_tensor",
                              s_np[0][:], s_edN[:],
                              s_bTn[:, co + h:co + h + 1], s_gm[:],
                              ALU.mult, ALU.mult)
                        tN = kb.mark("vector", "dv")
                        kb.do("vector", "tensor_tensor", s_gm[:],
                              pb[gbank][0:128, 128:256], s_mue[:],
                              ALU.mult)
                        kb.do("vector", "tensor_tensor",
                              gtC[pp][:, h * 128:(h + 1) * 128],
                              s_gm[:], s_edG[:], ALU.mult)
                        tGt = kb.mark("vector", "dv")
                        kb.psum_consumed(gbank, tGt)
                        kb.wait("tensor", tN)
                        kb.psum_wait(6)
                        kb.do("tensor", "matmul", pb[6][0:128, 0:128],
                              s_np[0][:], s_i128b[:],
                              start=True, stop=True)
                        tpt = kb.mark("tensor", "pe")
                        kb.wait("scalar", tpt)
                        kb.do("scalar", "copy", s_nt[0][:],
                              pb[6][0:128, 0:128])
                        tnt = kb.mark("scalar", "ac")
                        kb.psum_consumed(6, tnt)
                        if h % 2 == 0:
                            kb.wait("tensor", ktC_free[pp])
                            kb.psum_wait(2)
                            kb.do("tensor", "matmul",
                                  pb[2][0:128, 0:128],
                                  qkv[8 + kq][:, cs], s_i128b[:],
                                  start=True, stop=True)
                            tkt = kb.mark("tensor", "pe")
                            kb.wait("vector", tkt)
                            kb.do("vector", "tensor_copy",
                                  ktC[pp][:, kq * 128:(kq + 1) * 128],
                                  pb[2][0:128, 0:128])
                            tke = kb.mark("vector", "dv")
                            kb.psum_consumed(2, tke)
                        kb.psum_wait(2)
                        kb.do("tensor", "matmul", pb[2][0:128, 0:128],
                              qkv[16 + h][:, cs], s_i128b[:],
                              start=True, stop=True)
                        tvt = kb.mark("tensor", "pe")
                        kb.wait("vector", tvt)
                        kb.do("vector", "tensor_scalar",
                              panC[pp][:, h * 256:h * 256 + 128],
                              pb[2][0:128, 0:128],
                              s_bT[:, co + h:co + h + 1], None, ALU.mult)
                        tuv = kb.mark("vector", "dv")
                        kb.psum_consumed(2, tuv)
                        kb.do("vector", "tensor_scalar",
                              panC[pp][:, h * 256 + 128:h * 256 + 256],
                              ktC[pp][:, kq * 128:(kq + 1) * 128],
                              s_bA[:, co + h:co + h + 1], None, ALU.mult)
                        tuk = kb.mark("vector", "dv")
                        if debug and i == 0 and h == 0:
                            dbg("pan_pre", panC[pp][:, 0:256], (128, 256),
                                BF16)
                        cur = 0
                        tnp, tns = tN, tnt
                        for k in range(1, 7):
                            kb.wait("tensor", tns)
                            kb.wait("tensor", tnp)
                            kb.psum_wait(6)
                            kb.do("tensor", "matmul",
                                  pb[6][0:128, 0:128],
                                  s_nt[k - 1][:], s_np[cur][:],
                                  start=True, stop=True)
                            kb.psum_wait(5)
                            kb.do("tensor", "matmul",
                                  pb[5][0:128, 0:128],
                                  s_np[cur][:], s_nt[k - 1][:],
                                  start=True, stop=True)
                            tsq_ = kb.mark("tensor", "pe")
                            kb.wait("vector", tsq_)
                            kb.do("vector", "tensor_copy",
                                  s_np[1 - cur][:], pb[6][0:128, 0:128])
                            tnp = kb.mark("vector", "dv")
                            kb.psum_consumed(6, tnp)
                            kb.wait("scalar", tsq_)
                            kb.do("scalar", "copy", s_nt[k][:],
                                  pb[5][0:128, 0:128])
                            tns = kb.mark("scalar", "ac")
                            kb.psum_consumed(5, tns)
                            cur = 1 - cur
                        src = panC[pp][:, h * 256:h * 256 + 256]
                        kb.wait("tensor", tuv)
                        kb.wait("tensor", tuk)
                        kb.wait("tensor", tns)
                        tev = None
                        for ai, k in enumerate(range(6, -1, -1)):
                            bank = 5 if (ai & 1) == 0 else 6
                            dst = (s_pan2[ai & 1][:] if ai < 6 else
                                   panC[pp][:, h * 256:h * 256 + 256])
                            kb.wait("tensor", tev)
                            kb.psum_wait(bank)
                            if ai & 1:
                                kb.do("tensor", "matmul",
                                      pb[bank][0:128, 0:256], s_i128b[:],
                                      src, start=True, stop=False)
                                kb.do("tensor", "matmul",
                                      pb[bank][0:128, 0:256], s_nt[k][:],
                                      src, start=False, stop=True)
                                tap = kb.mark("tensor", "pe")
                                kb.wait("scalar", tap)
                                kb.do("scalar", "copy",
                                      dst, pb[bank][0:128, 0:256])
                                tev = kb.mark("scalar", "ac")
                            else:
                                kb.do("tensor", "matmul",
                                      pb[bank][0:128, 0:256], s_nt[k][:],
                                      src, start=True, stop=True)
                                tap = kb.mark("tensor", "pe")
                                kb.wait("vector", tap)
                                kb.do("vector", "tensor_tensor",
                                      dst, pb[bank][0:128, 0:256], src,
                                      ALU.add)
                                tev = kb.mark("vector", "dv")
                            kb.psum_consumed(bank, tev)
                            src = dst
                        t_scr = kb.tok("ac")
                        if debug and i == 0 and h == 0:
                            dbg("edN", s_edN[:], (128, 128), BF16)
                            dbg("edG", s_edG[:], (128, 128), BF16)
                            dbg("np0", s_np[0][:], (128, 128), BF16)
                            dbg("nt0", s_nt[0][:], (128, 128), BF16)
                            dbg("nt6", s_nt[6][:], (128, 128), BF16)
                            dbg("ktok0", ktC[pp][:, 0:128], (128, 128), BF16)
                            dbg("gt0", gtC[pp][:, 0:128], (128, 128), BF16)
                            dbg("pan_post", panC[pp][:, 0:256],
                                (128, 256), BF16)
                    for h in range(HV):
                        kq = h // 2
                        sl = slice(h * 128, (h + 1) * 128)
                        rb = i & 1
                        wbuf = 1 - rb
                        kb.psum_wait(2)
                        kb.do("tensor", "matmul", pb[2][0:128, 0:128],
                              panC[pp][:, h * 256 + 128:h * 256 + 256],
                              s_i128b[:], start=True, stop=True)
                        tw1 = kb.mark("tensor", "pe")
                        kb.wait("vector", tw1)
                        kb.do("vector", "tensor_copy", s_wkT[:],
                              pb[2][0:128, 0:128])
                        tw2 = kb.mark("vector", "dv")
                        kb.psum_consumed(2, tw2)
                        kb.wait("tensor", tw2)
                        kb.psum_wait(6)
                        kb.do("tensor", "matmul", pb[6][0:128, 0:128],
                              s_wkT[:], s_state[rb][:, sl],
                              start=True, stop=True)
                        tw3 = kb.mark("tensor", "pe")
                        kb.wait("vector", tw3)
                        kb.do("vector", "tensor_tensor", s_wfl[:],
                              panC[pp][:, h * 256:h * 256 + 128],
                              pb[6][0:128, 0:128], ALU.subtract)
                        kb.do("vector", "tensor_scalar", s_kb_[:],
                              ktC[pp][:, kq * 128:(kq + 1) * 128],
                              s_kbar[:, co + h:co + h + 1], None,
                              ALU.mult)
                        tw4 = kb.mark("vector", "dv")
                        kb.psum_consumed(6, tw4)
                        kb.wait("tensor", tw4)
                        kb.psum_wait(5)
                        kb.do("tensor", "matmul", pb[5][0:128, 0:128],
                              s_kb_[:], s_wfl[:], start=True, stop=True)
                        tw5 = kb.mark("tensor", "pe")
                        kb.wait("vector", tw5)
                        kb.do("vector", "scalar_tensor_tensor",
                              s_state[wbuf][:, sl], s_state[rb][:, sl],
                              s_acb[:, co + h:co + h + 1],
                              pb[5][0:128, 0:128], ALU.mult, ALU.add)
                        tw6 = kb.mark("vector", "dv")
                        kb.psum_consumed(5, tw6)
                    tsp = kb.tok("dv")
                    tss = kb.tok("ac")
                    kb.wait("gpsimd", tsp)
                    kb.wait("gpsimd", tss)
                    kb.do("gpsimd", "dma_start",
                          pan_hbm[i],
                          panC[pp][:])
                    panC_free[pp] = kb.mark("gpsimd", "gdma", 16)
                    kb.do("gpsimd", "dma_start",
                          gt_hbm[i],
                          gtC[pp][:])
                    gtC_free[pp] = kb.mark("gpsimd", "gdma", 16)
                    kb.do("gpsimd", "dma_start",
                          kt_hbm[i],
                          ktC[pp][:])
                    ktC_free[pp] = kb.mark("gpsimd", "gdma", 16)

                for i in range(NCH):
                    chunk_pass1(i)
                kb.barrier()
                kb.ckpt("p4")
                snap("state_p1", s_state[NCH & 1][:], (128, HV * DV), BF16)
                snap("pan0", pan_hbm[0], (128, HV * 256), BF16)
                snap("gwbf0", gw_bf[0, 0], (128, 1024), BF16)
                snap("dnbf47", dn_bf[NI - 1, 0], (128, 1024), BF16)
                snap("gw_all", gw_bf.rearrange("a h p m -> (a h p) m"),
                     (NI * 2 * 128, 1024), BF16)
                snap("uw_all", uw_bf.rearrange("a h p m -> (a h p) m"),
                     (NI * 2 * 128, 1024), BF16)
                snap("dn_all", dn_bf.rearrange("a h p m -> (a h p) m"),
                     (NI * 2 * 128, 1024), BF16)
                if debug:
                    dbg("state_p1",
                        s_state[NCH & 1][:], (128, HV * DV), BF16)

            fin = NCH & 1
            kb.do("gpsimd", "dma_start", cc_in, s_state[fin][:])
            kb.mark("gpsimd", "gdma", 16)
            kb.wait("gpsimd", kb.tok("gdma"))
            kb.ckpt("c1")
            kb.do("gpsimd", "collective_compute",
                  "AllGather", ALU.bypass, rg, [cc_in], [cc_out])
            tcoll = kb.mark("gpsimd", "Bg", 1)
            kb.ckpt("c2")
            with nc.sbuf_tensor("s_recv", [128, 8 * HV * DV], BF16) \
                    as s_recv:
                kb.wait("gpsimd", tcoll)
                ccv = cc_out.rearrange("(r p) f -> r p f", p=128)
                for r in range(8):
                    kb.do("gpsimd", "dma_start",
                          s_recv[:, r * 2048:(r + 1) * 2048], ccv[r])
                    kb.mark("gpsimd", "gdma", 16)
                trcv = kb.tok("gdma")
                kb.wait_p16("vector", trcv)
                kb.do("vector", "tensor_scalar", s_state[0][:],
                      s_recv[:, 0:2048], s_selcol[:, 0:1], None, ALU.mult)
                for r in range(1, 8):
                    kb.do("vector", "scalar_tensor_tensor",
                          s_state[0][:],
                          s_recv[:, r * 2048:(r + 1) * 2048],
                          s_selcol[:, r:r + 1], s_state[0][:],
                          ALU.mult, ALU.add)
                tsel = kb.mark("vector", "dv")
            kb.barrier()
            kb.ckpt("coll")
            snap("state_init", s_state[0][:], (128, HV * DV), BF16)
            if debug:
                dbg("state_init", s_state[0][:], (128, HV * DV), BF16)

            with ExitStack() as sc5:
                E5 = sc5.enter_context
                panI = [E5(nc.sbuf_tensor(f"panI{k}", [128, HV * 256],
                                          BF16)) for k in range(2)]
                gtI = [E5(nc.sbuf_tensor(f"gtI{k}", [128, HV * 128],
                                         BF16)) for k in range(2)]
                ktI = [E5(nc.sbuf_tensor(f"ktI{k}", [128, HK * 128],
                                         BF16)) for k in range(2)]
                s_wkT = E5(nc.sbuf_tensor("s_wkT5", [128, 128], BF16))
                s_wfl = E5(nc.sbuf_tensor("s_wfl5", [128, 128], BF16))
                s_kb_ = E5(nc.sbuf_tensor("s_kb5", [128, 128], BF16))
                s_qt = E5(nc.sbuf_tensor("s_qt5", [128, 128], BF16))
                ld_free = [None, None]

                def load_chunk(i):
                    pp = i & 1
                    kb.wait("sync", ld_free[pp])
                    kb.do("sync", "dma_start", panI[pp][:],
                          pan_hbm[i])
                    kb.mark("sync", "wdma", 16)
                    kb.do("sync", "dma_start", gtI[pp][:],
                          gt_hbm[i])
                    kb.mark("sync", "wdma", 16)
                    kb.do("sync", "dma_start", ktI[pp][:],
                          kt_hbm[i])
                    kb.mark("sync", "wdma", 16)
                    return kb.tok("wdma")

                tld = [None] * NCH
                tld[0] = load_chunk(0)
                for i in range(NCH):
                    pp = i & 1
                    cs = slice(i * C, (i + 1) * C)
                    co = i * 16
                    if i + 1 < NCH:
                        tld[i + 1] = load_chunk(i + 1)
                        kb.wait_p16("tensor", tld[i])
                    else:
                        kb.do("sync", "dma_start", ktI[pp][:], kt_hbm[i])
                        kb.mark("sync", "wdma", 16)
                        kb.wait("tensor", kb.tok("wdma"))
                    kb.wait("vector", tld[i])
                    for h in range(HV):
                        kq = h // 2
                        sl = slice(h * 128, (h + 1) * 128)
                        rb = i & 1
                        wbuf = 1 - rb
                        kb.psum_wait(4)
                        kb.do("tensor", "matmul", pb[4][:, 0:128],
                              s_selhb[:, h * 128:(h + 1) * 128],
                              s_arow[:, cs], start=True, stop=True)
                        tq1 = kb.mark("tensor", "pe")
                        kb.wait("vector", tq1)
                        kb.do("vector", "tensor_tensor", s_qt[:],
                              qkv[kq][:, cs], pb[4][:, 0:128], ALU.mult)
                        tq2 = kb.mark("vector", "dv")
                        kb.psum_consumed(4, tq2)
                        kb.psum_wait(2)
                        kb.do("tensor", "matmul", pb[2][0:128, 0:128],
                              panI[pp][:, h * 256 + 128:h * 256 + 256],
                              s_i128b[:], start=True, stop=True)
                        tw1 = kb.mark("tensor", "pe")
                        kb.wait("vector", tw1)
                        kb.do("vector", "tensor_copy", s_wkT[:],
                              pb[2][0:128, 0:128])
                        tw2 = kb.mark("vector", "dv")
                        kb.psum_consumed(2, tw2)
                        kb.wait("tensor", tw2)
                        kb.psum_wait(6)
                        kb.do("tensor", "matmul", pb[6][0:128, 0:128],
                              s_wkT[:], s_state[rb][:, sl],
                              start=True, stop=True)
                        tw3 = kb.mark("tensor", "pe")
                        kb.wait("vector", tw3)
                        kb.do("vector", "tensor_tensor", s_wfl[:],
                              panI[pp][:, h * 256:h * 256 + 128],
                              pb[6][0:128, 0:128], ALU.subtract)
                        kb.do("vector", "tensor_scalar", s_kb_[:],
                              ktI[pp][:, kq * 128:(kq + 1) * 128],
                              s_kbar[:, co + h:co + h + 1], None,
                              ALU.mult)
                        tw4 = kb.mark("vector", "dv")
                        kb.psum_consumed(6, tw4)
                        kb.wait("tensor", tq2)
                        kb.wait("tensor", tw4)
                        kb.psum_wait(3)
                        kb.do("tensor", "matmul", pb[3][0:128, 0:128],
                              s_state[rb][:, sl], s_qt[:],
                              start=True, stop=False)
                        kb.do("tensor", "matmul", pb[3][0:128, 0:128],
                              s_wfl[:],
                              gtI[pp][:, h * 128:(h + 1) * 128],
                              start=False, stop=True)
                        tO = kb.mark("tensor", "pe")
                        kb.wait("scalar", tO)
                        kb.do("scalar", "copy", core[h][:, cs],
                              pb[3][0:128, 0:128])
                        tOe = kb.mark("scalar", "ac")
                        kb.psum_consumed(3, tOe)
                        kb.psum_wait(5)
                        kb.do("tensor", "matmul", pb[5][0:128, 0:128],
                              s_kb_[:], s_wfl[:], start=True, stop=True)
                        tw5 = kb.mark("tensor", "pe")
                        kb.wait("vector", tw5)
                        kb.do("vector", "scalar_tensor_tensor",
                              s_state[wbuf][:, sl], s_state[rb][:, sl],
                              s_acb[:, co + h:co + h + 1],
                              pb[5][0:128, 0:128], ALU.mult, ALU.add)
                        tw6 = kb.mark("vector", "dv")
                        kb.psum_consumed(5, tw6)
                    ld_free[pp] = kb.tok("pe")
                kb.barrier()
                kb.ckpt("p5")
                snap("core_raw0", core[0][:], (128, TOK), BF16)
                snap("core_raw15", core[15][:], (128, TOK), BF16)
            if debug:
                dbg("core_raw0", core[0][:], (128, TOK), BF16)
                dbg("core_raw15", core[15][:], (128, TOK), BF16)

            for g in reversed(sel_guards):
                g.__exit__(None, None, None)

            with nc.sbuf_tensor("zin0", [128, TOK], BF16) as zin0, \
                 nc.sbuf_tensor("zin1", [128, TOK], BF16) as zin1, \
                 nc.sbuf_tensor("gsc", [128, TOK], BF16) as gsc:
                zin = [zin0, zin1]
                zfree = [None, None]
                gsc_free = [None]
                rows6_free = [None]
                kb.do("sync", "dma_start", zin[0][:],
                      z_hbm[0])
                kb.mark("sync", "wdma", 16)
                tz = [None] * HV
                tz[0] = kb.tok("wdma")
                for h in range(HV):
                    if h + 1 < HV:
                        kb.wait("sync", zfree[(h + 1) & 1])
                        kb.do("sync", "dma_start", zin[(h + 1) & 1][:],
                              z_hbm[h + 1])
                        kb.mark("sync", "wdma", 16)
                        tz[h + 1] = kb.tok("wdma")
                    kb.wait("scalar", gsc_free[0])
                    kb.do("scalar", "activation", gsc[:], core[h][:],
                          AF.Square)
                    tq = kb.mark("scalar", "ac")
                    kb.wait("tensor", tq)
                    kb.psum_wait(6)
                    kb.do("tensor", "matmul", pb[6][0:1, 0:512],
                          s_onesb[:], gsc[:, 0:512],
                          start=True, stop=True)
                    kb.psum_wait(5)
                    kb.do("tensor", "matmul", pb[5][0:1, 0:512],
                          s_onesb[:], gsc[:, 512:1024],
                          start=True, stop=True)
                    tsm = kb.mark("tensor", "pe")
                    kb.wait("scalar", tsm)
                    kb.wait("scalar", rows6_free[0])
                    kb.do("scalar", "activation", s_rows[0:1, 0:512],
                          pb[6][0:1, 0:512], AF.Sqrt,
                          scale=s_cst[0:1, 7:8], bias=s_cst[0:1, 8:9])
                    kb.do("scalar", "activation", s_rows[0:1, 512:1024],
                          pb[5][0:1, 0:512], AF.Sqrt,
                          scale=s_cst[0:1, 7:8], bias=s_cst[0:1, 8:9])
                    tsr = kb.mark("scalar", "ac")
                    kb.psum_consumed(6, tsr)
                    kb.psum_consumed(5, tsr)
                    kb.wait("vector", tsr)
                    kb.do("vector", "reciprocal", s_rows[0:1, :],
                          s_rows[0:1, :])
                    kb.do("vector", "tensor_copy", s_rowb[0:1, :],
                          s_rows[0:1, :])
                    trc = kb.mark("vector", "dv")
                    rows6_free[0] = trc
                    kb.wait("tensor", trc)
                    kb.psum_wait(6)
                    kb.do("tensor", "matmul", pb[6][:, 0:512],
                          s_onesrb[:], s_rowb[0:1, 0:512],
                          start=True, stop=True)
                    kb.psum_wait(5)
                    kb.do("tensor", "matmul", pb[5][:, 0:512],
                          s_onesrb[:], s_rowb[0:1, 512:1024],
                          start=True, stop=True)
                    tbc = kb.mark("tensor", "pe")
                    kb.wait("vector", tbc)
                    kb.do("vector", "tensor_tensor", gsc[:, 0:512],
                          core[h][:, 0:512], pb[6][:, 0:512], ALU.mult)
                    kb.do("vector", "tensor_tensor", gsc[:, 512:1024],
                          core[h][:, 512:1024], pb[5][:, 0:512], ALU.mult)
                    tgg = kb.mark("vector", "dv")
                    kb.psum_consumed(6, tgg)
                    kb.psum_consumed(5, tgg)
                    kb.wait_p16("vector", tz[h])
                    kb.do("vector", "scalar_tensor_tensor", gsc[:],
                          gsc[:], s_normw[:, 0:1], zin[h & 1][:],
                          ALU.mult, ALU.mult)
                    kb.do("vector", "tensor_copy", core[h][:], gsc[:])
                    zfree[h & 1] = kb.mark("vector", "dv")
                    gsc_free[0] = zfree[h & 1]
                kb.barrier()
                kb.ckpt("p6")
                snap("core_g0", core[0][:], (128, TOK), BF16)
            if debug:
                dbg("core_g0", core[0][:], (128, TOK), BF16)


        with ExitStack() as sc7:
            E7 = sc7.enter_context
            att_guards = [nc.sbuf_tensor(f"att{j}", [128, TOK], BF16)
                          for j in range(16)]
            att = [g.__enter__() for g in att_guards]
            wtok2 = load_wblk(wout_b[0], 0, False, 0)
            for o in range(16):
                widx = o & 1
                nxt = None
                if o + 1 < 16:
                    nxt = load_wblk(
                        wout_b[o + 1], (o + 1) & 1, False, (o + 1) & 1)
                kb.wait("tensor", wtok2)
                for si in range(2):
                    bank = (2 * o + si) % 2
                    kb.psum_wait(bank)
                    for kk in range(16):
                        kb.do("tensor", "matmul", pb[bank][0:128, 0:512],
                              wb[widx][:, kk * 128:(kk + 1) * 128],
                              core[kk][:, si * 512:(si + 1) * 512],
                              start=(kk == 0), stop=(kk == 15))
                    tpp = kb.mark("tensor", "pe")
                    wb_free[widx] = tpp
                    kb.wait("vector", tpp)
                    kb.do("vector", "tensor_scalar",
                          att[o][:, si * 512:(si + 1) * 512],
                          pb[bank][0:128, 0:512],
                          s_boutc[:, o:o + 1], None, ALU.add)
                    te = kb.mark("vector", "dv")
                    kb.psum_consumed(bank, te)
                wtok2 = nxt
            conv_pump(len(conv_seq))
            kb.barrier()
            kb.ckpt("p7a")
            snap("att0", att[0][:], (128, TOK), BF16)
            snap("att15", att[15][:], (128, TOK), BF16)
            if debug:
                dbg("att0", att[0][:], (128, TOK), BF16)

            with nc.sbuf_tensor("atok", [128, HID], BF16) as atok, \
                 nc.sbuf_tensor("xt", [128, HID], F32) as xt, \
                 nc.sbuf_tensor("x1t", [128, HID], F32) as x1t, \
                 nc.sbuf_tensor("l3t", [128, HID], BF16) as l3t, \
                 nc.sbuf_tensor("l3s", [128, HID], BF16) as l3s, \
                 nc.sbuf_tensor("st7", [128, 8], F32) as st7:
                xt_free = None
                atok_free = None
                x1t_free = []
                l3t_free = None
                l3s_free = None
                for t in range(NTT):
                    kb.wait("sync", xt_free)
                    kb.do("sync", "dma_start", xt[:],
                          x_main[t * TT:(t + 1) * TT, :])
                    kb.mark("sync", "wdma", 16)
                    kb.do("sync", "dma_start", xt[:],
                          x_main[t * TT:(t + 1) * TT, :])
                    kb.mark("sync", "wdma", 16)
                    txx = kb.tok("wdma")
                    kb.wait("scalar", atok_free)
                    for j in range(16):
                        bank = 2 + (j & 1)
                        kb.psum_wait(bank)
                        kb.do("tensor", "matmul", pb[bank][0:128, 0:128],
                              att[j][:, t * TT:(t + 1) * TT], s_i128b[:],
                              start=True, stop=True)
                        tpj = kb.mark("tensor", "pe")
                        kb.wait("scalar", tpj)
                        kb.do("scalar", "copy",
                              atok[:, j * 128:(j + 1) * 128],
                              pb[bank][0:128, 0:128])
                        taj = kb.mark("scalar", "ac")
                        kb.psum_consumed(bank, taj)
                    tat = kb.tok("ac")
                    kb.wait("vector", tat)
                    kb.wait("vector", txx)
                    for tkn in x1t_free:
                        kb.wait("vector", tkn)
                    tl2 = ln_apply(atok, 128, None, st7,
                                   add_ap=xt[:], out_f32=x1t[:])
                    kb.wait("gpsimd", tl2)
                    kb.do("gpsimd", "dma_start",
                          x1_hbm[t * TT:(t + 1) * TT, :], x1t[:])
                    tx1d = kb.mark("gpsimd", "gdma", 16)
                    xt_free = tl2
                    atok_free = tl2
                    kb.wait("vector", l3t_free)
                    tl3 = ln_apply(x1t, 128, l3t[:], st7)
                    x1t_free = [tx1d, kb.tok("ac"), tl3]
                    kb.wait("scalar", l3s_free)
                    for j in range(16):
                        bank = 2 + (j & 1)
                        kb.wait("tensor", tl3)
                        kb.psum_wait(bank)
                        kb.do("tensor", "matmul", pb[bank][0:128, 0:128],
                              l3t[:, j * 128:(j + 1) * 128], s_i128b[:],
                              start=True, stop=True)
                        tpj = kb.mark("tensor", "pe")
                        kb.wait("scalar", tpj)
                        kb.do("scalar", "copy",
                              l3s[:, j * 128:(j + 1) * 128],
                              pb[bank][0:128, 0:128])
                        taj = kb.mark("scalar", "ac")
                        kb.psum_consumed(bank, taj)
                    l3t_free = kb.tok("pe")
                    tl3s = kb.tok("ac")
                    kb.wait("gpsimd", tl3s)
                    for j in range(16):
                        kb.do("gpsimd", "dma_start",
                              ln3_hbm[t // 4, j][:, (t % 4) * 128:
                                                 (t % 4) * 128 + 128],
                              l3s[:, j * 128:(j + 1) * 128])
                        kb.mark("gpsimd", "gdma", 16)
                    l3s_free = kb.tok("gdma")
            kb.barrier()
            snap("x1a", x1_hbm[0:128, :], (128, HID), F32)
            snap("x1b", x1_hbm[TOK - 128:TOK, :], (128, HID), F32)
            snap("ln3a", ln3_hbm[0, 0], (128, 512), BF16)

            if debug:
                dx1 = nc.dram_tensor("dbg_x1", [TOK, HID], F32,
                                     kind="ExternalOutput").ap()
                kb.barrier()
                kb.do("gpsimd", "dma_start", dx1, x1_hbm)
                kb.mark("gpsimd", "gdma", 16)
                dl3 = nc.dram_tensor("dbg_ln3", [128, 512], BF16,
                                     kind="ExternalOutput").ap()
                kb.do("gpsimd", "dma_start", dl3, ln3_hbm[0, 0])
                kb.mark("gpsimd", "gdma", 16)
                kb.barrier()
            for g in reversed(att_guards):
                g.__exit__(None, None, None)
            qkv_guard.__exit__(None, None, None)

            with ExitStack() as sc8:
                E8 = sc8.enter_context
                fuse = E8(nc.sbuf_tensor("fuse", [128, NI * 512], BF16))
                l3in = E8(nc.sbuf_tensor("l3in", [128, 16 * 512], BF16))
                gact = [E8(nc.sbuf_tensor(f"gact{k}", [128, 512], BF16))
                        for k in range(2)]
                ffncm = E8(nc.sbuf_tensor("ffncm", [128, 16 * 512], BF16))
                gub = [E8(nc.sbuf_tensor(f"gub{k}", [128, 2048], BF16))
                       for k in range(4)]
                dnb = [E8(nc.sbuf_tensor(f"dnb{k}", [128, 2048], BF16))
                       for k in range(2)]
                ftok = E8(nc.sbuf_tensor("ftok", [128, HID], BF16))
                x1r = E8(nc.sbuf_tensor("x1r", [128, HID], F32))
                l4t = E8(nc.sbuf_tensor("l4t", [128, HID], F32))
                st8 = E8(nc.sbuf_tensor("st8", [128, 8], F32))
                gact_free = [None, None]
                gub_free = [None] * 4
                dnb_free = [None] * 2
                fuse_free = None
                ffncm_free = None
                ftok_free = None
                x1r_free = None
                l4t_free = None

                def load_gu(ib, which):
                    bi = 2 * which + (ib & 1)
                    kb.wait("sync", gub_free[bi])
                    w = gw_bf if which == 0 else uw_bf
                    for hh in range(2):
                        kb.do("sync", "dma_start",
                              gub[bi][:, hh * 1024:(hh + 1) * 1024],
                              w[ib, hh])
                        kb.mark("sync", "wdma", 16)
                    return kb.tok("wdma")

                def load_dn(ib, slot):
                    kb.wait("sync", dnb_free[slot])
                    for hh in range(2):
                        kb.do("sync", "dma_start",
                              dnb[slot][:, hh * 1024:(hh + 1) * 1024],
                              dn_bf[ib, hh])
                        kb.mark("sync", "wdma", 16)
                    return kb.tok("wdma")

                l3in_free = None
                for half in range(2):
                    kb.wait("sync", l3in_free)
                    for kk in range(16):
                        kb.do("sync", "dma_start",
                              l3in[:, kk * 512:(kk + 1) * 512],
                              ln3_hbm[half, kk])
                        kb.mark("sync", "wdma", 16)
                    tl3in = kb.tok("wdma")
                    kb.wait_p16("tensor", tl3in)
                    tg_tok = load_gu(0, 0)
                    tu_tok = load_gu(0, 1)
                    kb.wait("vector", fuse_free)
                    for ib in range(NI):
                        gb = ib & 1
                        ub = 2 + (ib & 1)
                        kb.wait_p16("tensor", tg_tok)
                        bank = ib % 2
                        kb.psum_wait(bank)
                        for kk in range(16):
                            kb.do("tensor", "matmul", pb[bank][0:128, 0:512],
                                  gub[gb][:, kk * 128:(kk + 1) * 128],
                                  l3in[:, kk * 512:(kk + 1) * 512],
                                  start=(kk == 0), stop=(kk == 15))
                        tg1 = kb.mark("tensor", "pe")
                        gub_free[gb] = tg1
                        if ib + 1 < NI:
                            tg_tok = load_gu(ib + 1, 0)
                        kb.wait("scalar", tg1)
                        kb.wait("scalar", gact_free[ib & 1])
                        kb.do("scalar", "activation", gact[ib & 1][:],
                              pb[bank][0:128, 0:512], AF.Silu,
                              scale=s_gsc[:, ib:ib + 1])
                        tg2 = kb.mark("scalar", "ac")
                        kb.psum_consumed(bank, tg2)
                        kb.wait_p16("tensor", tu_tok)
                        bank2 = 2 + (ib % 2)
                        kb.psum_wait(bank2)
                        for kk in range(16):
                            kb.do("tensor", "matmul",
                                  pb[bank2][0:128, 0:512],
                                  gub[ub][:, kk * 128:(kk + 1) * 128],
                                  l3in[:, kk * 512:(kk + 1) * 512],
                                  start=(kk == 0), stop=(kk == 15))
                        tu1 = kb.mark("tensor", "pe")
                        gub_free[ub] = tu1
                        if ib + 1 < NI:
                            tu_tok = load_gu(ib + 1, 1)
                        kb.wait("vector", tu1)
                        kb.wait("vector", tg2)
                        kb.do("vector", "scalar_tensor_tensor",
                              fuse[:, ib * 512:(ib + 1) * 512],
                              pb[bank2][0:128, 0:512],
                              s_usc[:, ib:ib + 1], gact[ib & 1][:],
                              ALU.mult, ALU.mult)
                        tf = kb.mark("vector", "dv")
                        gact_free[ib & 1] = tf
                        kb.psum_consumed(bank2, tf)
                    tfuse = kb.tok("dv")
                    l3in_free = kb.tok("pe")
                    if debug2:
                        kb.wait("gpsimd", tfuse)
                        snap(f"fuse_h{half}", fuse[:, 0:4096], (128, 4096),
                             BF16)
                        snap(f"l3in_h{half}", l3in[:], (128, 16 * 512), BF16)
                    kb.wait("tensor", tfuse)
                    tdn = [None] * 2
                    for s in range(2):
                        tdn[s] = load_dn(s, s)
                    kb.wait("vector", ffncm_free)
                    for gi in range(2 * NI):
                        og, ib = divmod(gi, NI)
                        slot = gi & 1
                        if ib == 0:
                            for oo in range(8):
                                kb.psum_wait(oo)
                        kb.wait_p16("tensor", tdn[slot])
                        for oo in range(8):
                            o = og * 8 + oo
                            kb.do("tensor", "matmul",
                                  pb[oo][0:128, 0:512],
                                  dnb[slot][:, o * 128:(o + 1) * 128],
                                  fuse[:, ib * 512:(ib + 1) * 512],
                                  start=(ib == 0), stop=(ib == NI - 1))
                        tdd = kb.mark("tensor", "pe")
                        dnb_free[slot] = tdd
                        if gi + 2 < 2 * NI:
                            tdn[slot] = load_dn((gi + 2) % NI, slot)
                        if ib == NI - 1:
                            tdall = kb.tok("pe")
                            kb.wait("vector", tdall)
                            for oo in range(8):
                                o = og * 8 + oo
                                kb.do("vector", "tensor_scalar",
                                      ffncm[:, o * 512:(o + 1) * 512],
                                      pb[oo][0:128, 0:512],
                                      s_dsc[:, o:o + 1], None, ALU.mult)
                                tev = kb.mark("vector", "dv")
                                kb.psum_consumed(oo, tev)
                    tffn = kb.tok("dv")
                    fuse_free = kb.tok("pe")
                    if debug2:
                        kb.wait("gpsimd", tffn)
                        snap(f"ffncm_h{half}", ffncm[:], (128, 16 * 512),
                             BF16)
                    if debug and half == 0:
                        dbg("fuse0", fuse[:, 0:512], (128, 512), BF16)
                        dbg("ffncm0", ffncm[:, 0:512], (128, 512), BF16)
                        dbg("l3in0", l3in[:, 0:512], (128, 512), BF16)
                    for sub in range(4):
                        tglob = half * 512 + sub * 128
                        kb.wait("sync", x1r_free)
                        kb.do("sync", "dma_start", x1r[:],
                              x1_hbm[tglob:tglob + 128, :])
                        kb.mark("sync", "wdma", 16)
                        kb.do("sync", "dma_start", x1r[:],
                              x1_hbm[tglob:tglob + 128, :])
                        kb.mark("sync", "wdma", 16)
                        tx1 = kb.tok("wdma")
                        kb.wait("scalar", ftok_free)
                        for j in range(16):
                            bank = 6 + (j & 1)
                            kb.wait("tensor", tffn)
                            kb.psum_wait(bank)
                            kb.do("tensor", "matmul",
                                  pb[bank][0:128, 0:128],
                                  ffncm[:, j * 512 + sub * 128:
                                        j * 512 + sub * 128 + 128],
                                  s_i128b[:], start=True, stop=True)
                            tpj = kb.mark("tensor", "pe")
                            kb.wait("scalar", tpj)
                            kb.do("scalar", "copy",
                                  ftok[:, j * 128:(j + 1) * 128],
                                  pb[bank][0:128, 0:128])
                            taj = kb.mark("scalar", "ac")
                            kb.psum_consumed(bank, taj)
                        tft = kb.tok("ac")
                        kb.wait("vector", tft)
                        kb.wait("vector", tx1)
                        kb.wait("vector", l4t_free)
                        tl4 = ln_apply(ftok, 128, None, st8,
                                       add_ap=x1r[:], out_f32=l4t[:])
                        ftok_free = tl4
                        x1r_free = tl4
                        kb.wait("gpsimd", tl4)
                        kb.do("gpsimd", "dma_start",
                              out_x[tglob:tglob + 128, :], l4t[:])
                        l4t_free = kb.mark("gpsimd", "gdma", 16)
                    ffncm_free = kb.tok("pe")
                kb.barrier()
        kb.barrier()
        kb.ckpt("full")
        if cut is not None:
            kb.cut_at(cut)
        kb.wait("gpsimd", kb.tok("gdma"))

        if kb_sink is not None:
            kb_sink.append(kb)
        with nc.Block() as block:
            kb.emit(block)
    return nc


_NC_CACHE = {}


def _get_nc(debug=False):
    key = bool(debug)
    if key not in _NC_CACHE:
        _NC_CACHE[key] = build_nc(debug=key)
    return _NC_CACHE[key]


def _host_inputs(inp):
    f32 = np.float32
    bf16 = ml_dtypes.bfloat16
    x = np.ascontiguousarray(inp["L0_attn_hidden_states"], dtype=f32)
    Wqkv = np.asarray(inp["L0_attn_Wqkv"], f32)
    Wz = np.asarray(inp["L0_attn_Wz"], f32)
    Wb = np.asarray(inp["L0_attn_Wb"], f32)
    Wa = np.asarray(inp["L0_attn_Wa"], f32)
    bqkv = np.asarray(inp["L0_attn_bqkv"], f32)
    bz = np.asarray(inp["L0_attn_bz"], f32)
    bb = np.asarray(inp["L0_attn_bb"], f32)
    ba = np.asarray(inp["L0_attn_ba"], f32)
    convw = np.asarray(inp["L0_attn_conv_w"], f32)
    convb = np.asarray(inp["L0_attn_conv_b"], f32)
    dtb = np.asarray(inp["L0_attn_dt_bias"], f32)
    alog = np.asarray(inp["L0_attn_A_log"], f32)
    normw = np.asarray(inp["L0_attn_norm_w"], f32)
    Wout = np.asarray(inp["L0_attn_Wout"], f32)
    bout = np.asarray(inp["L0_attn_bout"], f32)
    gw = np.asarray(inp["L0_ffn_gate_w"], f32)
    uw = np.asarray(inp["L0_ffn_up_w"], f32)
    dw = np.asarray(inp["L0_ffn_down_w"], f32)
    gs = np.asarray(inp["L0_ffn_gate_scale"], f32)
    us = np.asarray(inp["L0_ffn_up_scale"], f32)
    ds = np.asarray(inp["L0_ffn_down_scale"], f32)

    Wcat = np.zeros((NIN * 128, HID), f32)
    Wcat[0:CONVD] = Wqkv
    Wcat[CONVD:CONVD + HID] = Wz
    Wcat[6144:6160] = Wb
    Wcat[6176:6192] = Wa
    WinT = Wcat.T
    w_in = np.ascontiguousarray(
        WinT.reshape(16, 128, NIN, 128).transpose(2, 1, 0, 3)
        .reshape(NIN, 128, 2, 1024).transpose(0, 2, 1, 3))
    b_cat = np.zeros((NIN * 128,), f32)
    b_cat[0:CONVD] = bqkv
    b_cat[CONVD:CONVD + HID] = bz
    b_cat[6144:6160] = bb * 0.0
    b_cat[6176:6192] = ba
    b_in = np.ascontiguousarray(b_cat.reshape(NIN, 128).T)

    conv_w_d = np.ascontiguousarray(
        convw.reshape(32, 128, KC).transpose(1, 0, 2).reshape(128, 32 * KC))
    conv_b_d = np.ascontiguousarray(convb.reshape(32, 128).T)
    small = np.zeros((16, 8), f32)
    small[:, 0] = alog
    small[:, 1] = dtb
    small[:, 2] = bb
    norm_w_d = normw.reshape(128, 1)

    WoutT = np.ascontiguousarray(Wout.T)
    wout_b = np.ascontiguousarray(
        WoutT.reshape(16, 128, 16, 128).transpose(2, 1, 0, 3)
        .reshape(16, 128, 2, 1024).transpose(0, 2, 1, 3))
    bout_c = np.ascontiguousarray(bout.reshape(16, 128).T)

    gate_b = np.ascontiguousarray(
        gw.reshape(16, 128, NI, 128).transpose(2, 1, 0, 3)
        .reshape(NI, 128, 2, 1024).transpose(0, 2, 1, 3))
    up_b = np.ascontiguousarray(
        uw.reshape(16, 128, NI, 128).transpose(2, 1, 0, 3)
        .reshape(NI, 128, 2, 1024).transpose(0, 2, 1, 3))
    down_b = np.ascontiguousarray(
        dw.reshape(NI, 128, 2, 1024).transpose(0, 2, 1, 3))
    g_sc = np.ascontiguousarray(gs.reshape(NI, 128).T)
    u_sc = np.ascontiguousarray(us.reshape(NI, 128).T)
    d_sc = np.ascontiguousarray(ds.reshape(16, 128).T)

    i128b = np.eye(128, dtype=bf16)
    i16f = np.eye(16, dtype=f32)
    selh = np.zeros((16, 16 * 128), f32)
    for h in range(16):
        selh[h, h * 128:(h + 1) * 128] = 1.0
    selhb = selh.astype(bf16)
    ones16f = np.ones((16, 128), f32)
    onesb = np.ones((128, 1), bf16)
    onesrb = np.ones((1, 128), bf16)
    onesrf = np.ones((1, 128), f32)
    jj, ff = np.mgrid[0:128, 0:128]
    mask_lo = (ff < jj).astype(bf16)
    mask_ue = (ff >= jj).astype(bf16)
    rmask = np.ones((16, TOK), f32)
    rmask[:, ::C] = 0.0
    e127 = np.zeros((128, 1), f32)
    e127[127, 0] = 1.0
    csts = np.zeros((1, 16), f32)
    csts[0, 0] = LN_EPS
    csts[0, 1] = 128.0
    csts[0, 2] = 1.0
    csts[0, 3] = 128.0 * L2_EPS
    csts[0, 4] = L2_EPS
    csts[0, 5] = -1.0
    csts[0, 6] = 1.0
    csts[0, 7] = 1.0 / 128.0
    csts[0, 8] = RMS_EPS
    cstc = np.broadcast_to(csts, (128, 16)).copy()

    shared = dict(w_in=w_in, b_in=b_in, conv_w=conv_w_d, conv_b=conv_b_d,
                  small=small, norm_w=norm_w_d, wout_b=wout_b, bout_c=bout_c,
                  gate_w=gate_b, up_w=up_b, down_in=down_b, g_scale=g_sc,
                  u_scale=u_sc, d_scale=d_sc, i128b=i128b, i16f=i16f,
                  selhb=selhb, selhf=selh, ones16f=ones16f, onesb=onesb,
                  onesrb=onesrb, onesrf=onesrf, mask_lo=mask_lo,
                  mask_ue=mask_ue, rmask=rmask, e127=e127, csts=csts, cstc=cstc)

    in_maps = []
    for c in range(8):
        b, half = c // 2, c % 2
        xm = np.ascontiguousarray(x[b, half * TOK:(half + 1) * TOK])
        if half == 1:
            xh = np.ascontiguousarray(x[b, TOK - HALO:TOK])
            fl = 1.0
        else:
            xh = np.zeros((HALO, HID), np.float32)
            fl = 0.0
        sel = np.zeros((1, 8), np.float32)
        if half == 1:
            sel[0, c - 1] = 1.0
        m = dict(shared)
        m.update(x_main=xm, x_halo=xh,
                 flag=np.full((1, 1), fl, np.float32), sel8=sel)
        in_maps.append(m)
    return in_maps


def kernel(**inputs):
    nc = _get_nc(debug=False)
    in_maps = _host_inputs(inputs)
    res = run_bass_kernel_spmd(nc, in_maps, core_ids=list(range(8)))
    out = np.empty((4, 2048, HID), np.float32)
    for c in range(8):
        b, half = c // 2, c % 2
        out[b, half * TOK:(half + 1) * TOK] = res.results[c]["out_x"]
    return out


if __name__ == "__main__":
    print("kernel module OK")


# version 18
# speedup vs baseline: 1.0248x; 1.0081x over previous
from contextlib import ExitStack

import numpy as np
import ml_dtypes

import concourse.bass as bass
import concourse.mybir as mybir
from concourse.bass_utils import run_bass_kernel_spmd
from concourse.replica_groups import maybe_share_collective_output_space

F32 = mybir.dt.float32
BF16 = mybir.dt.bfloat16
FP8 = mybir.dt.float8e4
AF = mybir.ActivationFunctionType
ALU = mybir.AluOpType
AX = mybir.AxisListType

HID = 2048
HK, HV, DK, DV, KC = 8, 16, 128, 128, 4
KEY, VAL, CONVD = HK * DK, HV * DV, 4096
TOK, HALO, TT, NTT = 1024, 8, 128, 8
C, NCH = 128, 8
NIN = 49
NI = 48
LN_EPS, L2_EPS, RMS_EPS = 1e-5, 1e-6, 1e-6

ENGS = ("sync", "gpsimd", "tensor", "vector", "scalar")
BSEM = {"sync": "Bs", "gpsimd": "Bg", "tensor": "Bt", "vector": "Bv",
        "scalar": "Ba"}


class KB:
    ESEM = {"vector": "dv", "scalar": "ac"}

    def __init__(self, nc):
        self.nc = nc
        self.streams = {e: [] for e in ENGS}
        self.cnt = {}
        self.hwm = {}
        self.sems = {}
        self.psum_free = {}
        self.wr_hist = {e: [] for e in ENGS}

    @staticmethod
    def _ap_names(vals):
        out = set()
        for v in vals:
            t = getattr(v, "tensor", None)
            if t is not None:
                lo = v.offset
                hi = lo + sum((n - 1) * st for st, n in v.ap) + 1
                out.add((t.name, lo, hi))
        return out

    @staticmethod
    def _overlaps(s1, s2):
        for (n1, a1, b1) in s1:
            for (n2, a2, b2) in s2:
                if n1 == n2 and a1 < b2 and a2 < b1:
                    return True
        return False

    def _hazard_guard(self, eng, meth, a, k):
        if eng not in self.ESEM:
            return
        writes = self._ap_names(a[:1])
        if "accum_out" in k and k["accum_out"] is not None:
            writes |= self._ap_names([k["accum_out"]])
        reads = self._ap_names(a[1:]) | self._ap_names(
            v for kk, v in k.items() if kk != "accum_out")
        hist = self.wr_hist[eng]
        need = False
        for dist in (1, 2):
            if len(hist) >= dist:
                idx, wset, rset = hist[-dist]
                if (self._overlaps(reads, wset) or
                        self._overlaps(writes, wset) or
                        self._overlaps(writes, rset)):
                    need = True
                    writer_idx = idx
                    break
        if need:
            ent = self.streams[eng][writer_idx]
            tail = self.streams[eng][writer_idx + 1:]
            if ent[4]:
                sem = ent[4][-1][0]
            else:
                sem = "hz" + eng[0]
                ent[4].append((sem, 1))
                self.cnt[sem] += 1
            after = sum(n for it in tail if it[0] == "op"
                        for (sm, n) in it[4] if sm == sem)
            self.wait(eng, (sem, self.cnt[sem] - after))
            sp = getattr(self, "spacer", None)
            if sp and eng in sp:
                meth_s, args_s = sp[eng]
                for _ in range(2):
                    self.streams[eng].append(["op", meth_s, args_s, {}, []])
            self.wr_hist[eng] = []
        self.wr_hist[eng].append((len(self.streams[eng]), writes, reads))
        if len(self.wr_hist[eng]) > 2:
            self.wr_hist[eng] = self.wr_hist[eng][-2:]

    def ckpt(self, name):
        if not hasattr(self, "ckpts"):
            self.ckpts = {}
        self.ckpts[name] = {e: len(s) for e, s in self.streams.items()}

    def cut_at(self, name):
        marks = self.ckpts[name]
        for e in ENGS:
            self.streams[e] = self.streams[e][:marks[e]]

    def new_sem(self, stack, name):
        self.sems[name] = stack.enter_context(self.nc.semaphore(name))
        self.cnt[name] = 0

    def do(self, eng, meth, *a, **k):
        self._hazard_guard(eng, meth, a, k)
        self.streams[eng].append(["op", meth, a, k, []])

    def mark(self, eng, sem, n=1):
        assert self.streams[eng] and self.streams[eng][-1][0] == "op", \
            f"mark on empty/wait stream {eng}"
        self.streams[eng][-1][4].append((sem, n))
        self.cnt[sem] += n
        return (sem, self.cnt[sem])

    def tok(self, sem):
        return (sem, self.cnt[sem])

    def wait_p16(self, eng, token):
        if token is None:
            return
        self.wait(eng, (token[0], token[1] + 16))

    def wait(self, eng, token):
        if token is None:
            return
        sem, v = token
        if v <= 0 or self.hwm.get((eng, sem), 0) >= v:
            return
        self.hwm[(eng, sem)] = v
        self.streams[eng].append(["wait", sem, v])

    def barrier(self):
        for e in ("sync", "gpsimd"):
            st = self.streams[e]
            idx = len(st) - 1
            while idx >= 0 and st[idx][0] != "op":
                idx -= 1
            if idx >= 0 and st[idx][1] == "dma_start" and st[idx][4]:
                sem = st[idx][4][-1][0]
                self.streams[e].append(
                    ["op", "dma_start", st[idx][2], dict(st[idx][3]), []])
                self.mark(e, sem, 16)
        toks = []
        for e in ENGS:
            idx = len(self.streams[e]) - 1
            while idx >= 0 and self.streams[e][idx][0] != "op":
                idx -= 1
            if idx < 0:
                toks.append(self.tok(BSEM[e]))
                continue
            ent = self.streams[e][idx]
            if ent[4]:
                toks.append((ent[4][-1][0], self.cnt[ent[4][-1][0]]))
            else:
                n = 16 if e in ("sync", "gpsimd") else 1
                ent[4].append((BSEM[e], n))
                self.cnt[BSEM[e]] += n
                toks.append((BSEM[e], self.cnt[BSEM[e]]))
        for e in ENGS:
            for t in toks:
                self.wait(e, t)

    def psum_wait(self, bank):
        for t in self.psum_free.get(bank, []):
            self.wait("tensor", t)

    def psum_consumed(self, bank, token, add=False):
        if add:
            self.psum_free.setdefault(bank, []).append(token)
        else:
            self.psum_free[bank] = [token]

    def emit(self, block):
        def mk(stream):
            def body(e):
                for it in stream:
                    if it[0] == "wait":
                        e.wait_ge(self.sems[it[1]], it[2])
                    else:
                        _, meth, a, k, incs = it
                        inst = getattr(e, meth)(*a, **k)
                        for (s, n) in incs:
                            inst = inst.then_inc(self.sems[s], n)
            return body
        for e in ENGS:
            getattr(block, e)(mk(self.streams[e]))


def build_nc(debug=False, cut=None, kb_sink=None, debug2=False):
    nc = bass.Bass("TRN2", target_bir_lowering=False)
    dbg_list = []

    def din(name, shape, dt=F32):
        return nc.dram_tensor(name, list(shape), dt, kind="ExternalInput").ap()

    x_main = din("x_main", (TOK, HID))
    x_halo = din("x_halo", (HALO, HID))
    flag = din("flag", (1, 1))
    sel8 = din("sel8", (1, 8))
    w_in = din("w_in", (NIN, 2, 128, 1024))
    b_in = din("b_in", (128, NIN))
    conv_w = din("conv_w", (128, 32 * KC))
    conv_b = din("conv_b", (128, 32))
    small = din("small", (16, 8))
    norm_w = din("norm_w", (128, 1))
    wout_b = din("wout_b", (16, 2, 128, 1024))
    bout_c = din("bout_c", (128, 16))
    gate_w = din("gate_w", (NI, 2, 128, 1024))
    up_w = din("up_w", (NI, 2, 128, 1024))
    down_in = din("down_in", (NI, 2, 128, 1024))
    g_scale = din("g_scale", (128, NI))
    u_scale = din("u_scale", (128, NI))
    d_scale = din("d_scale", (128, 16))
    i128b = din("i128b", (128, 128), BF16)
    i16f = din("i16f", (16, 16))
    selhb = din("selhb", (16, 16 * 128), BF16)
    selhf = din("selhf", (16, 16 * 128))
    ones16f = din("ones16f", (16, 128))
    onesb = din("onesb", (128, 1), BF16)
    onesrb = din("onesrb", (1, 128), BF16)
    onesrf = din("onesrf", (1, 128))
    mask_lo = din("mask_lo", (128, 128), BF16)
    mask_ue = din("mask_ue", (128, 128), BF16)
    rmask = din("rmask", (16, TOK))
    e127 = din("e127", (128, 1))
    csts = din("csts", (1, 16))
    cstc = din("cstc", (128, 16))

    out_x = nc.dram_tensor("out_x", [TOK, HID], F32, kind="ExternalOutput").ap()

    def dscr(name, shape, dt=F32, **kw):
        return nc.dram_tensor(name, list(shape), dt, kind="Internal", **kw).ap()

    pan_hbm = dscr("pan_hbm", (NCH, 128, HV * 256), BF16)
    gt_hbm = dscr("gt_hbm", (NCH, 128, HV * 128), BF16)
    kt_hbm = dscr("kt_hbm", (NCH, 128, HK * 128), BF16)
    z_hbm = dscr("z_hbm", (HV, 128, TOK), BF16)
    x1_hbm = dscr("x1_hbm", (TOK, HID), F32)
    ln3_hbm = dscr("ln3_hbm", (2, 16, 128, 512), BF16)
    gw_bf = dscr("gw_bf", (NI, 2, 128, 1024), BF16)
    uw_bf = dscr("uw_bf", (NI, 2, 128, 1024), BF16)
    dn_bf = dscr("dn_bf", (NI, 2, 128, 1024), BF16)
    f8mid = dscr("f8mid", (3, 2, 128, 1024), FP8)
    rg = [[0, 1, 2, 3, 4, 5, 6, 7]]
    aspace = "Local"
    cc_in = dscr("cc_in", (128, HV * DV), BF16)
    cc_out = dscr("cc_out", (8 * 128, HV * DV), BF16, addr_space=aspace)

    with ExitStack() as st:
        E = st.enter_context
        kb = KB(nc)
        for s in ("wdma", "gdma", "pe", "dv", "ac",
                  "Bs", "Bg", "Bt", "Bv", "Ba", "hzv", "hzs"):
            kb.new_sem(st, s)

        pb = [E(nc.psum_tensor(f"pb{i}", [128, 512], F32)) for i in range(8)]

        def sb(n, sh, dt):
            return E(nc.sbuf_tensor(n, list(sh), dt))

        s_state = [sb(f"s_state{i}", (128, HV * DV), BF16) for i in range(2)]
        s_cst = sb("s_cst", (1, 16), F32)
        s_cstc = sb("s_cstc", (128, 16), F32)
        s_small = sb("s_small", (16, 8), F32)
        s_bias = sb("s_bias", (128, NIN), F32)
        s_convw = sb("s_convw", (128, 32 * KC), F32)
        s_convb = sb("s_convb", (128, 32), F32)
        s_normw = sb("s_normw", (128, 1), F32)
        s_boutc = sb("s_boutc", (128, 16), F32)
        s_gsc = sb("s_gsc", (128, NI), F32)
        s_usc = sb("s_usc", (128, NI), F32)
        s_dsc = sb("s_dsc", (128, 16), F32)
        s_i128b = sb("s_i128b", (128, 128), BF16)
        s_i16f = sb("s_i16f", (16, 16), F32)
        s_ones16f = sb("s_ones16f", (16, 128), F32)
        s_onesb = sb("s_onesb", (128, 1), BF16)
        s_onesrb = sb("s_onesrb", (1, 128), BF16)
        s_onesrf = sb("s_onesrf", (1, 128), F32)
        s_mlo = sb("s_mlo", (128, 128), BF16)
        s_mue = sb("s_mue", (128, 128), BF16)
        s_rmask = sb("s_rmask", (16, TOK), F32)
        s_e127 = sb("s_e127", (128, 1), F32)
        s_flag = sb("s_flag", (1, 1), F32)
        s_sel8 = sb("s_sel8", (1, 8), F32)
        s_flagcol = sb("s_flagcol", (128, 1), F32)
        s_selcol = sb("s_selcol", (128, 8), F32)
        s_negea = sb("s_negea", (16, 1), F32)
        s_beta = sb("s_beta", (16, TOK), F32)
        s_g = sb("s_g", (16, TOK), F32)
        s_c = sb("s_c", (16, TOK), F32)
        s_arow = sb("s_arow", (16, TOK), BF16)
        s_cT = sb("s_cT", (128, NCH * 16), F32)
        s_bT = sb("s_bT", (128, NCH * 16), F32)
        s_bTn = sb("s_bTn", (128, NCH * 16), F32)
        s_bA = sb("s_bA", (128, NCH * 16), F32)
        s_kbar = sb("s_kbar", (128, NCH * 16), F32)
        s_acb = sb("s_acb", (128, NCH * 16), F32)
        s_dg16 = sb("s_dg16", (16, 16), F32)
        s_sqd = sb("s_sqd", (128, 1024), BF16)
        wf = [sb(f"wf{i}", (128, 1024), F32) for i in range(2)]
        wb = [sb(f"wb{i}", (128, 2048), BF16) for i in range(2)]

        w8e = {"vector": sb("w8v", (128, 1024), FP8),
               "scalar": sb("w8s", (128, 1024), FP8)}
        s_rows = sb("s_rows", (1, TOK), F32)
        s_rowb = sb("s_rowb", (1, TOK), BF16)
        s_spc = sb("s_spc", (1, 8), F32)
        s_col = sb("s_col", (128, 4), F32)
        s_edf = sb("s_edf", (128, 128), F32)

        wf_free = [None, None]
        wb_free = [None, None]

        kb.spacer = {"vector": ("memset", (s_spc[0:1, 0:2], 0.0)),
                     "scalar": ("copy", (s_spc[0:1, 2:3], s_spc[0:1, 3:4]))}

        def dbg(name, ap, shape, dt):
            if not debug:
                return
            d = nc.dram_tensor("dbg_" + name, list(shape), dt,
                               kind="ExternalOutput").ap()
            kb.barrier()
            kb.do("gpsimd", "dma_start", d, ap)
            kb.mark("gpsimd", "gdma", 16)
            kb.barrier()

        def snap(name, ap, shape, dt):
            if not debug2:
                return
            d = nc.dram_tensor("snap_" + name, list(shape), dt,
                               kind="ExternalOutput").ap()
            kb.do("gpsimd", "dma_start", d, ap)
            kb.mark("gpsimd", "gdma", 16)

        lw_half = [0]

        def load_wblk(src_ap, widx, cvt_fp8, parity):
            eng = "vector" if parity == 0 else "scalar"
            sem = "dv" if parity == 0 else "ac"
            meth = "tensor_copy" if eng == "vector" else "copy"
            for hh in range(2):
                wfi = lw_half[0] & 1
                lw_half[0] += 1
                kb.wait("sync", wf_free[wfi])
                kb.do("sync", "dma_start", wf[wfi][:], src_ap[hh])
                kb.mark("sync", "wdma", 16)
                tw = kb.tok("wdma")
                kb.wait_p16(eng, tw)
                if hh == 0:
                    kb.wait(eng, wb_free[widx])
                if cvt_fp8:
                    w8 = w8e[eng]
                    kb.do(eng, meth, w8[:], wf[wfi][:])
                    kb.do(eng, meth,
                          wb[widx][:, hh * 1024:(hh + 1) * 1024], w8[:])
                else:
                    kb.do(eng, meth,
                          wb[widx][:, hh * 1024:(hh + 1) * 1024], wf[wfi][:])
                t = kb.mark(eng, sem)
                wf_free[wfi] = t
            return t

        conv_seq = []
        for ib in range(NI):
            conv_seq.append((gate_w[ib], gw_bf[ib]))
            conv_seq.append((up_w[ib], uw_bf[ib]))
            conv_seq.append((down_in[ib], dn_bf[ib]))
        conv_state = {"i": 0}

        conv_emitted = {"d1": [], "d2": 0}

        def conv_pump(nblk=1):
            for _ in range(nblk):
                i = conv_state["i"]
                n = len(conv_seq)
                if i >= n + 1:
                    return
                conv_state["i"] = i + 1
                if i < n:
                    src, dst = conv_seq[i]
                    if i >= 3:
                        kb.wait("gpsimd", conv_emitted["d1"][i - 1])
                    kb.do("gpsimd", "dma_start", f8mid[i % 3], src)
                    kb.mark("gpsimd", "gdma", 16)
                    conv_emitted["d1"].append(kb.tok("gdma"))
                j = conv_emitted["d2"]
                if j < n - 1 and j == i - 1:
                    kb.wait("gpsimd", conv_emitted["d1"][j + 1])
                    kb.do("gpsimd", "dma_start", conv_seq[j][1],
                          f8mid[j % 3])
                    kb.mark("gpsimd", "gdma", 16)
                    conv_emitted["d2"] = j + 1
                if i == n:
                    kb.do("gpsimd", "dma_start", f8mid[(n - 1) % 3],
                          conv_seq[n - 1][0])
                    kb.mark("gpsimd", "gdma", 16)
                    kb.wait("gpsimd", kb.tok("gdma"))
                    kb.do("gpsimd", "dma_start", conv_seq[n - 1][1],
                          f8mid[(n - 1) % 3])
                    kb.mark("gpsimd", "gdma", 16)
                    conv_emitted["d2"] = n

        ln_state = {"prev": None}

        def ln_apply(xb, nrows, out_ap, stat,
                     add_ap=None, out_f32=None):
            kb.wait("scalar", ln_state["prev"])
            kb.do("vector", "reduce_sum", stat[0:nrows, 0:1],
                  xb[0:nrows, :], AX.X)
            kb.do("scalar", "activation", s_sqd[0:nrows, :],
                  xb[0:nrows, 0:1024], AF.Square,
                  accum_out=stat[0:nrows, 1:2])
            kb.do("scalar", "activation", s_sqd[0:nrows, :],
                  xb[0:nrows, 1024:2048], AF.Square,
                  accum_out=stat[0:nrows, 7:8])
            ta = kb.mark("scalar", "ac")
            kb.do("vector", "tensor_scalar_mul", stat[0:nrows, 2:3],
                  stat[0:nrows, 0:1], 1.0 / HID)
            kb.wait("vector", ta)
            kb.do("vector", "tensor_tensor", stat[0:nrows, 1:2],
                  stat[0:nrows, 1:2], stat[0:nrows, 7:8], ALU.add)
            kb.do("vector", "tensor_scalar_mul", stat[0:nrows, 3:4],
                  stat[0:nrows, 1:2], 1.0 / HID)
            kb.do("vector", "tensor_tensor", stat[0:nrows, 4:5],
                  stat[0:nrows, 2:3], stat[0:nrows, 2:3], ALU.mult)
            kb.do("vector", "tensor_tensor", stat[0:nrows, 4:5],
                  stat[0:nrows, 3:4], stat[0:nrows, 4:5], ALU.subtract)
            tv = kb.mark("vector", "dv")
            kb.wait("scalar", tv)
            kb.do("scalar", "activation", stat[0:nrows, 5:6],
                  stat[0:nrows, 4:5], AF.Sqrt, bias=s_cstc[0:nrows, 0:1])
            ta = kb.mark("scalar", "ac")
            kb.wait("vector", ta)
            kb.do("vector", "reciprocal", stat[0:nrows, 6:7],
                  stat[0:nrows, 5:6])
            if out_ap is not None:
                kb.do("vector", "tensor_scalar", out_ap,
                      xb[0:nrows, :], stat[0:nrows, 2:3], stat[0:nrows, 6:7],
                      ALU.subtract, ALU.mult)
            if add_ap is not None:
                kb.do("vector", "tensor_scalar", out_f32,
                      xb[0:nrows, :], stat[0:nrows, 2:3], stat[0:nrows, 6:7],
                      ALU.subtract, ALU.mult)
                kb.do("vector", "tensor_tensor", out_f32, out_f32, add_ap,
                      ALU.add)
            t = kb.mark("vector", "dv")
            ln_state["prev"] = t
            return t

        for dst, src in (
            (s_cst[:], csts), (s_cstc[:], cstc),
            (s_small[:], small), (s_bias[:], b_in),
            (s_convw[:], conv_w), (s_convb[:], conv_b), (s_normw[:], norm_w),
            (s_boutc[:], bout_c), (s_gsc[:], g_scale), (s_usc[:], u_scale),
            (s_dsc[:], d_scale), (s_i128b[:], i128b), (s_i16f[:], i16f),
            (s_ones16f[:], ones16f),
            (s_onesb[:], onesb), (s_onesrb[:], onesrb), (s_onesrf[:], onesrf),
            (s_mlo[:], mask_lo), (s_mue[:], mask_ue), (s_rmask[:], rmask),
            (s_e127[:], e127), (s_flag[:], flag), (s_sel8[:], sel8),
        ):
            kb.do("sync", "dma_start", dst, src)
            kb.mark("sync", "wdma", 16)
        kb.do("sync", "dma_start", s_cst[:], csts)
        kb.mark("sync", "wdma", 16)
        tc0 = kb.tok("wdma")
        for e in ("vector", "scalar", "tensor"):
            kb.wait(e, tc0)
        kb.do("scalar", "activation", s_negea[:, 0:1], s_small[:, 0:1], AF.Exp)
        ta = kb.mark("scalar", "ac")
        kb.wait("vector", ta)
        kb.do("vector", "tensor_scalar_mul", s_negea[:, 0:1],
              s_negea[:, 0:1], -1.0)
        kb.do("vector", "memset", s_state[0][:], 0.0)
        kb.do("vector", "memset", s_state[1][:], 0.0)
        kb.psum_wait(4)
        kb.do("tensor", "matmul", pb[4][:, 0:1], s_onesrf[:], s_flag[:],
              start=True, stop=True)
        kb.do("tensor", "matmul", pb[4][:, 8:16], s_onesrf[:], s_sel8[:],
              start=True, stop=True)
        tp = kb.mark("tensor", "pe")
        kb.wait("vector", tp)
        kb.do("vector", "tensor_copy", s_flagcol[:], pb[4][:, 0:1])
        kb.do("vector", "tensor_copy", s_selcol[:], pb[4][:, 8:16])
        tv = kb.mark("vector", "dv")
        kb.psum_consumed(4, tv)
        kb.barrier()
        kb.ckpt("p0")

        qkv_guard = nc.sbuf_tensor("qkv_s", [128, 32 * TOK], BF16)
        qkv_parent = qkv_guard.__enter__()
        qkv = [qkv_parent[:, j * TOK:(j + 1) * TOK] for j in range(32)]
        core = [qkv_parent[:, (16 + j) * TOK:(17 + j) * TOK]
                for j in range(16)]
        lnx_guards = [nc.sbuf_tensor(f"lnx{j}", [128, HALO + TOK], BF16)
                      for j in range(16)]
        lnx = [g.__enter__() for g in lnx_guards]
        if True:

            with nc.sbuf_tensor("xt0", [128, HID], F32) as xt0, \
                 nc.sbuf_tensor("xt1", [128, HID], F32) as xt1, \
                 nc.sbuf_tensor("lntok", [128, HID], BF16) as lntok, \
                 nc.sbuf_tensor("stat1", [128, 8], F32) as stat1:
                xb_ = [xt0, xt1]
                xfree = [None, None]
                lntok_free = [None]

                def p1_tile(xb, nrows, coloff):
                    kb.wait("vector", lntok_free[0])
                    tln = ln_apply(xb, nrows, lntok[0:nrows, :], stat1)
                    xb_done = (kb.tok("dv"), kb.tok("ac"))
                    for j in range(16):
                        bank = 2 + (j & 1)
                        kb.wait("tensor", tln)
                        kb.psum_wait(bank)
                        rhs = s_i128b[:] if nrows == 128 \
                            else s_i128b[0:nrows, 0:nrows]
                        kb.do("tensor", "matmul", pb[bank][0:128, 0:nrows],
                              lntok[0:nrows, j * 128:(j + 1) * 128], rhs,
                              start=True, stop=True)
                        tpj = kb.mark("tensor", "pe")
                        kb.wait("scalar", tpj)
                        kb.do("scalar", "copy",
                              lnx[j][:, coloff:coloff + nrows],
                              pb[bank][0:128, 0:nrows])
                        taj = kb.mark("scalar", "ac")
                        kb.psum_consumed(bank, taj)
                    lntok_free[0] = kb.tok("pe")
                    return xb_done

                kb.do("sync", "dma_start", xt0[0:HALO, :], x_halo)
                kb.mark("sync", "wdma", 16)
                kb.do("sync", "dma_start", xt0[0:HALO, :], x_halo)
                kb.mark("sync", "wdma", 16)
                th = kb.tok("wdma")
                kb.wait("vector", th)
                kb.wait("scalar", th)
                xfree[0] = p1_tile(xt0, HALO, 0)
                for t in range(NTT):
                    b = t & 1
                    if xfree[b] is not None:
                        for tkn in xfree[b]:
                            kb.wait("sync", tkn)
                    kb.do("sync", "dma_start", xb_[b][:],
                          x_main[t * TT:(t + 1) * TT, :])
                    kb.mark("sync", "wdma", 16)
                    tw = kb.tok("wdma")
                    kb.wait_p16("vector", tw)
                    kb.wait_p16("scalar", tw)
                    xfree[b] = p1_tile(xb_[b], 128, HALO + t * TT)
            kb.barrier()
            kb.ckpt("p1")
            snap("lnx0", lnx[0][:], (128, HALO + TOK), BF16)
            snap("lnx8", lnx[8][:], (128, HALO + TOK), BF16)
            if debug:
                dbg("lnx0", lnx[0][:], (128, HALO + TOK), BF16)
                dbg("lnx15", lnx[15][:], (128, HALO + TOK), BF16)

            with ExitStack() as sc_p2:
                E2 = sc_p2.enter_context
                mixb = [E2(nc.sbuf_tensor(f"mix{i}", [128, HALO + TOK], BF16))
                        for i in range(2)]
                cac = [E2(nc.sbuf_tensor(f"cac{i}", [128, TOK], BF16))
                       for i in range(2)]
                silu_f = E2(nc.sbuf_tensor("silu_f", [128, TOK], BF16))
                sq2 = E2(nc.sbuf_tensor("sq2", [128, TOK], BF16))
                ba_raw = E2(nc.sbuf_tensor("ba_raw", [48, TOK], F32))
                zst = [E2(nc.sbuf_tensor(f"zst{i}", [128, TOK], BF16))
                       for i in range(2)]

                mix_free = [None, None]
                rows_free = [None]
                zst_free = [None, None]
                silu_free = None
                sq2_free = None
                cac_free = [None, None]

                wtok = load_wblk(w_in[0], 0, False, 0)
                for o in range(NIN):
                    widx = o & 1
                    nxt = None
                    if o + 1 < NIN:
                        nxt = load_wblk(
                            w_in[o + 1], (o + 1) & 1, False, (o + 1) & 1)
                    kb.wait("tensor", wtok)
                    is_mix, is_z = o < 32, 32 <= o < 48
                    mt = mixb[o & 1] if is_mix else None
                    slices = ([(0, HALO)] if is_mix else []) + \
                        [(HALO, 512), (HALO + 512, 512)]
                    for si, (soff, nn) in enumerate(slices):
                        bank = (2 * o + si) % 4
                        kb.psum_wait(bank)
                        for kk in range(16):
                            kb.do("tensor", "matmul", pb[bank][0:128, 0:nn],
                                  wb[widx][:, kk * 128:(kk + 1) * 128],
                                  lnx[kk][:, soff:soff + nn],
                                  start=(kk == 0), stop=(kk == 15))
                        tpp = kb.mark("tensor", "pe")
                        wb_free[widx] = tpp
                        if is_mix:
                            kb.wait("vector", tpp)
                            if si == 0:
                                kb.wait("vector", mix_free[o & 1])
                            kb.do("vector", "tensor_scalar",
                                  mt[:, soff:soff + nn],
                                  pb[bank][0:128, 0:nn],
                                  s_bias[:, o:o + 1], None, ALU.add)
                            te = kb.mark("vector", "dv")
                        elif is_z:
                            kb.wait("scalar", tpp)
                            if si == 0:
                                kb.wait("scalar", zst_free[o & 1])
                            kb.do("scalar", "activation",
                                  zst[o & 1][:, soff - HALO:soff - HALO + nn],
                                  pb[bank][0:128, 0:nn], AF.Silu,
                                  bias=s_bias[:, o:o + 1])
                            te = kb.mark("scalar", "ac")
                        else:
                            kb.wait("vector", tpp)
                            kb.do("vector", "tensor_scalar",
                                  ba_raw[0:48, soff - HALO:soff - HALO + nn],
                                  pb[bank][0:48, 0:nn],
                                  s_bias[0:48, o:o + 1], None, ALU.add)
                            te = kb.mark("vector", "dv")
                        kb.psum_consumed(bank, te)
                    if is_z:
                        tz = kb.tok("ac")
                        kb.wait("gpsimd", tz)
                        kb.do("gpsimd", "dma_start",
                              z_hbm[o - 32],
                              zst[o & 1][:])
                        zst_free[o & 1] = kb.mark("gpsimd", "gdma", 16)
                    if is_mix:
                        kb.do("vector", "tensor_scalar_mul", mt[:, 0:HALO],
                              mt[:, 0:HALO], s_flagcol[:, 0:1])
                        kb.wait("vector", cac_free[0])
                        kb.do("vector", "tensor_scalar", cac[0][:],
                              mt[:, HALO:HALO + TOK],
                              s_convw[:, 4 * o + 3:4 * o + 4], None, ALU.mult)
                        kb.wait("vector", cac_free[1])
                        for d in (1, 2, 3):
                            src_acc = cac[(d + 1) & 1]
                            dst_acc = cac[d & 1]
                            kb.do("vector", "scalar_tensor_tensor",
                                  dst_acc[:], mt[:, HALO - d:HALO - d + TOK],
                                  s_convw[:, 4 * o + 3 - d:4 * o + 4 - d],
                                  src_acc[:], ALU.mult, ALU.add)
                        tcv = kb.mark("vector", "dv")
                        mix_free[o & 1] = tcv
                        acc = cac[1]
                        kb.wait("scalar", tcv)
                        if o >= 16:
                            kb.do("scalar", "activation", qkv[o], acc[:],
                                  AF.Silu, bias=s_convb[:, o:o + 1])
                            cac_free[1] = kb.mark("scalar", "ac")
                            cac_free[0] = cac_free[1]
                        else:
                            kb.wait("scalar", silu_free)
                            kb.do("scalar", "activation", silu_f[:], acc[:],
                                  AF.Silu, bias=s_convb[:, o:o + 1])
                            cac_free[1] = kb.mark("scalar", "ac")
                            cac_free[0] = cac_free[1]
                            kb.wait("scalar", sq2_free)
                            kb.do("scalar", "activation", sq2[:], silu_f[:],
                                  AF.Square)
                            tsq = kb.mark("scalar", "ac")
                            kb.wait("tensor", tsq)
                            kb.psum_wait(6)
                            kb.do("tensor", "matmul", pb[6][0:1, 0:512],
                                  s_onesb[:], sq2[:, 0:512],
                                  start=True, stop=True)
                            kb.psum_wait(5)
                            kb.do("tensor", "matmul", pb[5][0:1, 0:512],
                                  s_onesb[:], sq2[:, 512:1024],
                                  start=True, stop=True)
                            tp6 = kb.mark("tensor", "pe")
                            sq2_free = tp6
                            sc_i = 1 if o < 8 else 2
                            kb.wait("scalar", tp6)
                            kb.wait("scalar", rows_free[0])
                            kb.do("scalar", "activation", s_rows[0:1, 0:512],
                                  pb[6][0:1, 0:512], AF.Sqrt,
                                  scale=s_cst[0:1, sc_i:sc_i + 1],
                                  bias=s_cst[0:1, sc_i + 2:sc_i + 3])
                            kb.do("scalar", "activation", s_rows[0:1, 512:1024],
                                  pb[5][0:1, 0:512], AF.Sqrt,
                                  scale=s_cst[0:1, sc_i:sc_i + 1],
                                  bias=s_cst[0:1, sc_i + 2:sc_i + 3])
                            ta6 = kb.mark("scalar", "ac")
                            kb.psum_consumed(6, ta6)
                            kb.psum_consumed(5, ta6)
                            kb.wait("vector", ta6)
                            kb.do("vector", "reciprocal", s_rows[0:1, :],
                                  s_rows[0:1, :])
                            kb.do("vector", "tensor_copy", s_rowb[0:1, :],
                                  s_rows[0:1, :])
                            trr = kb.mark("vector", "dv")
                            rows_free[0] = trr
                            kb.wait("tensor", trr)
                            kb.psum_wait(6)
                            kb.do("tensor", "matmul", pb[6][:, 0:512],
                                  s_onesrb[:], s_rowb[0:1, 0:512],
                                  start=True, stop=True)
                            kb.psum_wait(5)
                            kb.do("tensor", "matmul", pb[5][:, 0:512],
                                  s_onesrb[:], s_rowb[0:1, 512:1024],
                                  start=True, stop=True)
                            tph = kb.mark("tensor", "pe")
                            kb.wait("vector", tph)
                            kb.do("vector", "tensor_tensor",
                                  qkv[o][:, 0:512], silu_f[:, 0:512],
                                  pb[6][:, 0:512], ALU.mult)
                            kb.do("vector", "tensor_tensor",
                                  qkv[o][:, 512:1024], silu_f[:, 512:1024],
                                  pb[5][:, 0:512], ALU.mult)
                            tqn = kb.mark("vector", "dv")
                            silu_free = tqn
                            kb.psum_consumed(6, tqn)
                            kb.psum_consumed(5, tqn)
                    wtok = nxt
                kb.barrier()

                kb.do("scalar", "activation", s_beta[:], ba_raw[0:16, :],
                      AF.Sigmoid, bias=s_small[:, 2:3])
                kb.do("vector", "tensor_scalar", s_g[:], ba_raw[32:48, :],
                      s_small[:, 1:2], None, ALU.add)
                tg = kb.mark("vector", "dv")
                kb.wait("scalar", tg)
                kb.do("scalar", "activation", ba_raw[0:16, :],
                      s_g[:], AF.Abs)
                t1 = kb.mark("scalar", "ac")
                kb.do("scalar", "activation", ba_raw[0:16, :],
                      ba_raw[0:16, :], AF.Exp, scale=s_cstc[0:16, 5:6])
                kb.do("scalar", "activation", ba_raw[0:16, :],
                      ba_raw[0:16, :], AF.Ln, bias=s_cstc[0:16, 6:7])
                t2 = kb.mark("scalar", "ac")
                kb.wait("vector", t2)
                kb.do("vector", "scalar_tensor_tensor", ba_raw[0:16, :],
                      s_g[:], 0.0, ba_raw[0:16, :], ALU.max, ALU.add)
                kb.do("vector", "tensor_scalar", s_g[:], ba_raw[0:16, :],
                      s_negea[:, 0:1], None, ALU.mult)
                kb.do("vector", "tensor_tensor_scan", s_c[:], s_rmask[:],
                      s_g[:], 0.0, ALU.mult, ALU.add)
                tcs = kb.mark("vector", "dv")
                kb.wait("scalar", tcs)
                kb.do("scalar", "activation", s_arow[:], s_c[:], AF.Exp)
                kb.mark("scalar", "ac")
                kb.barrier()
                kb.ckpt("p2")
                snap("qkv0", qkv[0], (128, TOK), BF16)
                snap("qkv8", qkv[8], (128, TOK), BF16)
                snap("qkv16", qkv[16], (128, TOK), BF16)
                snap("qkv31", qkv[31], (128, TOK), BF16)
                snap("beta", s_beta[:], (16, TOK), F32)
                snap("g", s_g[:], (16, TOK), F32)
                snap("arow", s_arow[:], (16, TOK), BF16)
                if debug:
                    dbg("beta", s_beta[:], (16, TOK), F32)
                    dbg("g", s_g[:], (16, TOK), F32)
                    dbg("c", s_c[:], (16, TOK), F32)
                    dbg("qkv0", qkv[0], (128, TOK), BF16)
                    dbg("qkv8", qkv[8], (128, TOK), BF16)
                    dbg("qkv16", qkv[16], (128, TOK), BF16)
                    dbg("qkv31", qkv[31], (128, TOK), BF16)

            for g in reversed(lnx_guards):
                g.__exit__(None, None, None)

            sel_guards = [nc.sbuf_tensor("s_selhf", [16, 16 * 128], F32),
                          nc.sbuf_tensor("s_selhb", [16, 16 * 128], BF16)]
            s_selhf, s_selhb = [g.__enter__() for g in sel_guards]
            kb.do("sync", "dma_start", s_selhf[:], selhf)
            kb.mark("sync", "wdma", 16)
            kb.do("sync", "dma_start", s_selhb[:], selhb)
            kb.mark("sync", "wdma", 16)
            tsel_ld = kb.tok("wdma")
            kb.wait_p16("tensor", tsel_ld)

            for i in range(NCH):
                conv_pump(2)
                cs = slice(i * C, (i + 1) * C)
                co = i * 16
                kb.psum_wait(2)
                kb.do("tensor", "matmul", pb[2][0:128, 0:16],
                      s_c[:, cs], s_i16f[:], start=True, stop=True)
                kb.do("tensor", "matmul", pb[2][0:128, 16:32],
                      s_beta[:, cs], s_i16f[:], start=True, stop=True)
                tpc = kb.mark("tensor", "pe")
                kb.wait("vector", tpc)
                kb.do("vector", "tensor_copy", s_cT[:, co:co + 16],
                      pb[2][0:128, 0:16])
                kb.do("vector", "tensor_copy", s_bT[:, co:co + 16],
                      pb[2][0:128, 16:32])
                kb.do("vector", "tensor_scalar_mul", s_bTn[:, co:co + 16],
                      pb[2][0:128, 16:32], -1.0)
                tvc = kb.mark("vector", "dv")
                kb.psum_consumed(2, tvc)
                kb.wait("scalar", tvc)
                kb.do("scalar", "activation", s_bA[:, co:co + 16],
                      s_cT[:, co:co + 16], AF.Exp)
                tac = kb.mark("scalar", "ac")
                kb.wait("vector", tac)
                kb.do("vector", "tensor_tensor", s_bA[:, co:co + 16],
                      s_bT[:, co:co + 16], s_bA[:, co:co + 16], ALU.mult)
                kb.wait("tensor", tvc)
                kb.psum_wait(3)
                kb.do("tensor", "matmul", pb[3][0:16, 0:1],
                      s_cT[:, co:co + 16], s_e127[:],
                      start=True, stop=True)
                tp3 = kb.mark("tensor", "pe")
                kb.wait("vector", tp3)
                kb.do("vector", "tensor_copy", s_col[0:16, 0:1],
                      pb[3][0:16, 0:1])
                tv3 = kb.mark("vector", "dv")
                kb.psum_consumed(3, tv3)
                kb.do("vector", "tensor_scalar", s_dg16[:],
                      s_i16f[:], s_col[0:16, 0:1], None, ALU.mult)
                tdg = kb.mark("vector", "dv")
                kb.wait("tensor", tdg)
                kb.psum_wait(3)
                kb.do("tensor", "matmul", pb[3][0:128, 0:16],
                      s_ones16f[:], s_dg16[:], start=True, stop=True)
                tp3b = kb.mark("tensor", "pe")
                kb.wait("scalar", tp3b)
                kb.do("scalar", "activation", s_acb[:, co:co + 16],
                      pb[3][0:128, 0:16], AF.Exp)
                ta3 = kb.mark("scalar", "ac")
                kb.wait("vector", tp3b)
                kb.do("vector", "tensor_tensor", s_kbar[:, co:co + 16],
                      pb[3][0:128, 0:16], s_cT[:, co:co + 16],
                      ALU.subtract)
                kb.do("vector", "tensor_single_scalar",
                      s_kbar[:, co:co + 16], s_kbar[:, co:co + 16],
                      0.0, ALU.min)
                tv4 = kb.mark("vector", "dv")
                kb.psum_consumed(3, ta3)
                kb.psum_consumed(3, tv4, add=True)
                kb.wait("scalar", tv4)
                kb.do("scalar", "activation", s_kbar[:, co:co + 16],
                      s_kbar[:, co:co + 16], AF.Exp)
                kb.mark("scalar", "ac")
            kb.barrier()
            kb.ckpt("p3")
            snap("cT", s_cT[:], (128, NCH * 16), F32)
            snap("kbar", s_kbar[:], (128, NCH * 16), F32)
            snap("acb", s_acb[:], (128, NCH * 16), F32)
            if debug:
                dbg("cT", s_cT[:], (128, NCH * 16), F32)
                dbg("kbar", s_kbar[:], (128, NCH * 16), F32)
                dbg("acb", s_acb[:], (128, NCH * 16), F32)

            with ExitStack() as sc4:
                E4 = sc4.enter_context
                s_nt = [E4(nc.sbuf_tensor(f"s_nt{k}", [128, 128], BF16))
                        for k in range(7)]
                s_np = [E4(nc.sbuf_tensor(f"s_np{k}", [128, 128], BF16))
                        for k in range(2)]
                s_gm = E4(nc.sbuf_tensor("s_gm", [128, 128], BF16))
                s_edN = E4(nc.sbuf_tensor("s_edN", [128, 128], BF16))
                s_edG = E4(nc.sbuf_tensor("s_edG", [128, 128], BF16))
                s_wkT = E4(nc.sbuf_tensor("s_wkT", [128, 128], BF16))
                s_wfl = E4(nc.sbuf_tensor("s_wfl", [128, 128], BF16))
                s_kb_ = E4(nc.sbuf_tensor("s_kb_", [128, 128], BF16))
                s_pan2 = [E4(nc.sbuf_tensor(f"s_pan2{k}", [128, 256],
                                            BF16)) for k in range(2)]
                panC = [E4(nc.sbuf_tensor(f"panC{k}", [128, HV * 256],
                                          BF16)) for k in range(2)]
                gtC = [E4(nc.sbuf_tensor(f"gtC{k}", [128, HV * 128],
                                         BF16)) for k in range(2)]
                ktC = [E4(nc.sbuf_tensor(f"ktC{k}", [128, HK * 128],
                                         BF16)) for k in range(2)]
                panC_free = [None, None]
                gtC_free = [None, None]
                ktC_free = [None, None]
                t_scr = None

                def chunk_pass1(i):
                    nonlocal t_scr
                    pp = i & 1
                    cs = slice(i * C, (i + 1) * C)
                    co = i * 16
                    kb.wait("vector", panC_free[pp])
                    kb.wait("vector", gtC_free[pp])
                    kb.wait("vector", ktC_free[pp])
                    for h in range(HV):
                        conv_pump(1)
                        kq = h // 2
                        gbank = 3 + (h & 1)
                        kb.wait("tensor", t_scr)
                        kb.psum_wait(gbank)
                        kb.do("tensor", "matmul", pb[gbank][0:128, 0:128],
                              qkv[8 + kq][:, cs], qkv[8 + kq][:, cs],
                              start=True, stop=True)
                        kb.do("tensor", "matmul",
                              pb[gbank][0:128, 128:256],
                              qkv[8 + kq][:, cs], qkv[kq][:, cs],
                              start=True, stop=True)
                        kb.psum_wait(5)
                        kb.do("tensor", "matmul", pb[5][0:128, 0:128],
                              s_selhf[:, h * 128:(h + 1) * 128],
                              s_c[:, cs], start=True, stop=True)
                        tpg = kb.mark("tensor", "pe")
                        kb.wait("vector", tpg)
                        kb.do("vector", "tensor_scalar", s_edf[:],
                              pb[5][0:128, 0:128],
                              s_cT[:, co + h:co + h + 1], 0.0,
                              ALU.subtract, ALU.max)
                        tv1 = kb.mark("vector", "dv")
                        kb.wait("scalar", tv1)
                        kb.do("scalar", "activation", s_edN[:], s_edf[:],
                              AF.Exp, scale=s_cstc[0:128, 5:6])
                        te1 = kb.mark("scalar", "ac")
                        kb.wait("vector", te1)
                        kb.do("vector", "tensor_scalar", s_edf[:],
                              pb[5][0:128, 0:128],
                              s_cT[:, co + h:co + h + 1], 0.0,
                              ALU.subtract, ALU.min)
                        tv2 = kb.mark("vector", "dv")
                        kb.psum_consumed(5, tv2)
                        kb.wait("scalar", tv2)
                        kb.do("scalar", "activation", s_edG[:], s_edf[:],
                              AF.Exp)
                        tEd = kb.mark("scalar", "ac")
                        kb.wait("vector", tEd)
                        kb.do("vector", "tensor_tensor", s_gm[:],
                              pb[gbank][0:128, 0:128], s_mlo[:], ALU.mult)
                        kb.do("vector", "scalar_tensor_tensor",
                              s_np[0][:], s_edN[:],
                              s_bTn[:, co + h:co + h + 1], s_gm[:],
                              ALU.mult, ALU.mult)
                        tN = kb.mark("vector", "dv")
                        kb.do("vector", "tensor_tensor", s_gm[:],
                              pb[gbank][0:128, 128:256], s_mue[:],
                              ALU.mult)
                        kb.do("vector", "tensor_tensor",
                              gtC[pp][:, h * 128:(h + 1) * 128],
                              s_gm[:], s_edG[:], ALU.mult)
                        tGt = kb.mark("vector", "dv")
                        kb.psum_consumed(gbank, tGt)
                        kb.wait("tensor", tN)
                        kb.psum_wait(6)
                        kb.do("tensor", "matmul", pb[6][0:128, 0:128],
                              s_np[0][:], s_i128b[:],
                              start=True, stop=True)
                        tpt = kb.mark("tensor", "pe")
                        kb.wait("scalar", tpt)
                        kb.do("scalar", "copy", s_nt[0][:],
                              pb[6][0:128, 0:128])
                        tnt = kb.mark("scalar", "ac")
                        kb.psum_consumed(6, tnt)
                        if h % 2 == 0:
                            kb.wait("tensor", ktC_free[pp])
                            kb.psum_wait(2)
                            kb.do("tensor", "matmul",
                                  pb[2][0:128, 0:128],
                                  qkv[8 + kq][:, cs], s_i128b[:],
                                  start=True, stop=True)
                            tkt = kb.mark("tensor", "pe")
                            kb.wait("vector", tkt)
                            kb.do("vector", "tensor_copy",
                                  ktC[pp][:, kq * 128:(kq + 1) * 128],
                                  pb[2][0:128, 0:128])
                            tke = kb.mark("vector", "dv")
                            kb.psum_consumed(2, tke)
                        kb.psum_wait(2)
                        kb.do("tensor", "matmul", pb[2][0:128, 0:128],
                              qkv[16 + h][:, cs], s_i128b[:],
                              start=True, stop=True)
                        tvt = kb.mark("tensor", "pe")
                        kb.wait("vector", tvt)
                        kb.do("vector", "tensor_scalar",
                              panC[pp][:, h * 256:h * 256 + 128],
                              pb[2][0:128, 0:128],
                              s_bT[:, co + h:co + h + 1], None, ALU.mult)
                        tuv = kb.mark("vector", "dv")
                        kb.psum_consumed(2, tuv)
                        kb.do("vector", "tensor_scalar",
                              panC[pp][:, h * 256 + 128:h * 256 + 256],
                              ktC[pp][:, kq * 128:(kq + 1) * 128],
                              s_bA[:, co + h:co + h + 1], None, ALU.mult)
                        tuk = kb.mark("vector", "dv")
                        if debug and i == 0 and h == 0:
                            dbg("pan_pre", panC[pp][:, 0:256], (128, 256),
                                BF16)
                        cur = 0
                        tnp, tns = tN, tnt
                        for k in range(1, 7):
                            kb.wait("tensor", tns)
                            kb.wait("tensor", tnp)
                            kb.psum_wait(6)
                            kb.do("tensor", "matmul",
                                  pb[6][0:128, 0:128],
                                  s_nt[k - 1][:], s_np[cur][:],
                                  start=True, stop=True)
                            kb.psum_wait(5)
                            kb.do("tensor", "matmul",
                                  pb[5][0:128, 0:128],
                                  s_np[cur][:], s_nt[k - 1][:],
                                  start=True, stop=True)
                            tsq_ = kb.mark("tensor", "pe")
                            kb.wait("vector", tsq_)
                            kb.do("vector", "tensor_copy",
                                  s_np[1 - cur][:], pb[6][0:128, 0:128])
                            tnp = kb.mark("vector", "dv")
                            kb.psum_consumed(6, tnp)
                            kb.wait("scalar", tsq_)
                            kb.do("scalar", "copy", s_nt[k][:],
                                  pb[5][0:128, 0:128])
                            tns = kb.mark("scalar", "ac")
                            kb.psum_consumed(5, tns)
                            cur = 1 - cur
                        src = panC[pp][:, h * 256:h * 256 + 256]
                        kb.wait("tensor", tuv)
                        kb.wait("tensor", tuk)
                        kb.wait("tensor", tns)
                        tev = None
                        for ai, k in enumerate(range(6, -1, -1)):
                            bank = 5 if (ai & 1) == 0 else 6
                            dst = (s_pan2[ai & 1][:] if ai < 6 else
                                   panC[pp][:, h * 256:h * 256 + 256])
                            kb.wait("tensor", tev)
                            kb.psum_wait(bank)
                            kb.do("tensor", "matmul",
                                  pb[bank][0:128, 0:256], s_nt[k][:],
                                  src, start=True, stop=True)
                            tap = kb.mark("tensor", "pe")
                            kb.wait("vector", tap)
                            kb.do("vector", "tensor_tensor",
                                  dst, pb[bank][0:128, 0:256], src, ALU.add)
                            tev = kb.mark("vector", "dv")
                            kb.psum_consumed(bank, tev)
                            src = dst
                        t_scr = kb.tok("ac")
                        if debug and i == 0 and h == 0:
                            dbg("edN", s_edN[:], (128, 128), BF16)
                            dbg("edG", s_edG[:], (128, 128), BF16)
                            dbg("np0", s_np[0][:], (128, 128), BF16)
                            dbg("nt0", s_nt[0][:], (128, 128), BF16)
                            dbg("nt6", s_nt[6][:], (128, 128), BF16)
                            dbg("ktok0", ktC[pp][:, 0:128], (128, 128), BF16)
                            dbg("gt0", gtC[pp][:, 0:128], (128, 128), BF16)
                            dbg("pan_post", panC[pp][:, 0:256],
                                (128, 256), BF16)
                    for h in range(HV):
                        kq = h // 2
                        sl = slice(h * 128, (h + 1) * 128)
                        rb = i & 1
                        wbuf = 1 - rb
                        kb.psum_wait(2)
                        kb.do("tensor", "matmul", pb[2][0:128, 0:128],
                              panC[pp][:, h * 256 + 128:h * 256 + 256],
                              s_i128b[:], start=True, stop=True)
                        tw1 = kb.mark("tensor", "pe")
                        kb.wait("vector", tw1)
                        kb.do("vector", "tensor_copy", s_wkT[:],
                              pb[2][0:128, 0:128])
                        tw2 = kb.mark("vector", "dv")
                        kb.psum_consumed(2, tw2)
                        kb.wait("tensor", tw2)
                        kb.psum_wait(6)
                        kb.do("tensor", "matmul", pb[6][0:128, 0:128],
                              s_wkT[:], s_state[rb][:, sl],
                              start=True, stop=True)
                        tw3 = kb.mark("tensor", "pe")
                        kb.wait("vector", tw3)
                        kb.do("vector", "tensor_tensor", s_wfl[:],
                              panC[pp][:, h * 256:h * 256 + 128],
                              pb[6][0:128, 0:128], ALU.subtract)
                        kb.do("vector", "tensor_scalar", s_kb_[:],
                              ktC[pp][:, kq * 128:(kq + 1) * 128],
                              s_kbar[:, co + h:co + h + 1], None,
                              ALU.mult)
                        tw4 = kb.mark("vector", "dv")
                        kb.psum_consumed(6, tw4)
                        kb.wait("tensor", tw4)
                        kb.psum_wait(5)
                        kb.do("tensor", "matmul", pb[5][0:128, 0:128],
                              s_kb_[:], s_wfl[:], start=True, stop=True)
                        tw5 = kb.mark("tensor", "pe")
                        kb.wait("vector", tw5)
                        kb.do("vector", "scalar_tensor_tensor",
                              s_state[wbuf][:, sl], s_state[rb][:, sl],
                              s_acb[:, co + h:co + h + 1],
                              pb[5][0:128, 0:128], ALU.mult, ALU.add)
                        tw6 = kb.mark("vector", "dv")
                        kb.psum_consumed(5, tw6)
                    tsp = kb.tok("dv")
                    tss = kb.tok("ac")
                    kb.wait("gpsimd", tsp)
                    kb.wait("gpsimd", tss)
                    kb.do("gpsimd", "dma_start",
                          pan_hbm[i],
                          panC[pp][:])
                    panC_free[pp] = kb.mark("gpsimd", "gdma", 16)
                    kb.do("gpsimd", "dma_start",
                          gt_hbm[i],
                          gtC[pp][:])
                    gtC_free[pp] = kb.mark("gpsimd", "gdma", 16)
                    kb.do("gpsimd", "dma_start",
                          kt_hbm[i],
                          ktC[pp][:])
                    ktC_free[pp] = kb.mark("gpsimd", "gdma", 16)

                for i in range(NCH):
                    chunk_pass1(i)
                kb.barrier()
                kb.ckpt("p4")
                snap("state_p1", s_state[NCH & 1][:], (128, HV * DV), BF16)
                snap("pan0", pan_hbm[0], (128, HV * 256), BF16)
                snap("gwbf0", gw_bf[0, 0], (128, 1024), BF16)
                snap("dnbf47", dn_bf[NI - 1, 0], (128, 1024), BF16)
                snap("gw_all", gw_bf.rearrange("a h p m -> (a h p) m"),
                     (NI * 2 * 128, 1024), BF16)
                snap("uw_all", uw_bf.rearrange("a h p m -> (a h p) m"),
                     (NI * 2 * 128, 1024), BF16)
                snap("dn_all", dn_bf.rearrange("a h p m -> (a h p) m"),
                     (NI * 2 * 128, 1024), BF16)
                if debug:
                    dbg("state_p1",
                        s_state[NCH & 1][:], (128, HV * DV), BF16)

            fin = NCH & 1
            kb.do("gpsimd", "dma_start", cc_in, s_state[fin][:])
            kb.mark("gpsimd", "gdma", 16)
            kb.wait("gpsimd", kb.tok("gdma"))
            kb.ckpt("c1")
            kb.do("gpsimd", "collective_compute",
                  "AllGather", ALU.bypass, rg, [cc_in], [cc_out])
            tcoll = kb.mark("gpsimd", "Bg", 1)
            kb.ckpt("c2")
            with nc.sbuf_tensor("s_recv", [128, 8 * HV * DV], BF16) \
                    as s_recv:
                kb.wait("gpsimd", tcoll)
                ccv = cc_out.rearrange("(r p) f -> r p f", p=128)
                for r in range(8):
                    kb.do("gpsimd", "dma_start",
                          s_recv[:, r * 2048:(r + 1) * 2048], ccv[r])
                    kb.mark("gpsimd", "gdma", 16)
                trcv = kb.tok("gdma")
                kb.wait_p16("vector", trcv)
                kb.do("vector", "tensor_scalar", s_state[0][:],
                      s_recv[:, 0:2048], s_selcol[:, 0:1], None, ALU.mult)
                for r in range(1, 8):
                    kb.do("vector", "scalar_tensor_tensor",
                          s_state[0][:],
                          s_recv[:, r * 2048:(r + 1) * 2048],
                          s_selcol[:, r:r + 1], s_state[0][:],
                          ALU.mult, ALU.add)
                tsel = kb.mark("vector", "dv")
            kb.barrier()
            kb.ckpt("coll")
            snap("state_init", s_state[0][:], (128, HV * DV), BF16)
            if debug:
                dbg("state_init", s_state[0][:], (128, HV * DV), BF16)

            with ExitStack() as sc5:
                E5 = sc5.enter_context
                panI = [E5(nc.sbuf_tensor(f"panI{k}", [128, HV * 256],
                                          BF16)) for k in range(2)]
                gtI = [E5(nc.sbuf_tensor(f"gtI{k}", [128, HV * 128],
                                         BF16)) for k in range(2)]
                ktI = [E5(nc.sbuf_tensor(f"ktI{k}", [128, HK * 128],
                                         BF16)) for k in range(2)]
                s_wkT = E5(nc.sbuf_tensor("s_wkT5", [128, 128], BF16))
                s_wfl = E5(nc.sbuf_tensor("s_wfl5", [128, 128], BF16))
                s_kb_ = E5(nc.sbuf_tensor("s_kb5", [128, 128], BF16))
                s_qt = E5(nc.sbuf_tensor("s_qt5", [128, 128], BF16))
                ld_free = [None, None]

                def load_chunk(i):
                    pp = i & 1
                    kb.wait("sync", ld_free[pp])
                    kb.do("sync", "dma_start", panI[pp][:],
                          pan_hbm[i])
                    kb.mark("sync", "wdma", 16)
                    kb.do("sync", "dma_start", gtI[pp][:],
                          gt_hbm[i])
                    kb.mark("sync", "wdma", 16)
                    kb.do("sync", "dma_start", ktI[pp][:],
                          kt_hbm[i])
                    kb.mark("sync", "wdma", 16)
                    return kb.tok("wdma")

                tld = [None] * NCH
                tld[0] = load_chunk(0)
                for i in range(NCH):
                    pp = i & 1
                    cs = slice(i * C, (i + 1) * C)
                    co = i * 16
                    if i + 1 < NCH:
                        tld[i + 1] = load_chunk(i + 1)
                        kb.wait_p16("tensor", tld[i])
                    else:
                        kb.do("sync", "dma_start", ktI[pp][:], kt_hbm[i])
                        kb.mark("sync", "wdma", 16)
                        kb.wait("tensor", kb.tok("wdma"))
                    for h in range(HV):
                        kq = h // 2
                        sl = slice(h * 128, (h + 1) * 128)
                        rb = i & 1
                        wbuf = 1 - rb
                        kb.psum_wait(4)
                        kb.do("tensor", "matmul", pb[4][:, 0:128],
                              s_selhb[:, h * 128:(h + 1) * 128],
                              s_arow[:, cs], start=True, stop=True)
                        tq1 = kb.mark("tensor", "pe")
                        kb.wait("vector", tq1)
                        kb.do("vector", "tensor_tensor", s_qt[:],
                              qkv[kq][:, cs], pb[4][:, 0:128], ALU.mult)
                        tq2 = kb.mark("vector", "dv")
                        kb.psum_consumed(4, tq2)
                        kb.psum_wait(2)
                        kb.do("tensor", "matmul", pb[2][0:128, 0:128],
                              panI[pp][:, h * 256 + 128:h * 256 + 256],
                              s_i128b[:], start=True, stop=True)
                        tw1 = kb.mark("tensor", "pe")
                        kb.wait("vector", tw1)
                        kb.do("vector", "tensor_copy", s_wkT[:],
                              pb[2][0:128, 0:128])
                        tw2 = kb.mark("vector", "dv")
                        kb.psum_consumed(2, tw2)
                        kb.wait("tensor", tw2)
                        kb.psum_wait(6)
                        kb.do("tensor", "matmul", pb[6][0:128, 0:128],
                              s_wkT[:], s_state[rb][:, sl],
                              start=True, stop=True)
                        tw3 = kb.mark("tensor", "pe")
                        kb.wait("vector", tw3)
                        kb.do("vector", "tensor_tensor", s_wfl[:],
                              panI[pp][:, h * 256:h * 256 + 128],
                              pb[6][0:128, 0:128], ALU.subtract)
                        kb.do("vector", "tensor_scalar", s_kb_[:],
                              ktI[pp][:, kq * 128:(kq + 1) * 128],
                              s_kbar[:, co + h:co + h + 1], None,
                              ALU.mult)
                        tw4 = kb.mark("vector", "dv")
                        kb.psum_consumed(6, tw4)
                        kb.wait("tensor", tq2)
                        kb.wait("tensor", tw4)
                        kb.psum_wait(3)
                        kb.do("tensor", "matmul", pb[3][0:128, 0:128],
                              s_state[rb][:, sl], s_qt[:],
                              start=True, stop=False)
                        kb.do("tensor", "matmul", pb[3][0:128, 0:128],
                              s_wfl[:],
                              gtI[pp][:, h * 128:(h + 1) * 128],
                              start=False, stop=True)
                        tO = kb.mark("tensor", "pe")
                        kb.wait("scalar", tO)
                        kb.do("scalar", "copy", core[h][:, cs],
                              pb[3][0:128, 0:128])
                        tOe = kb.mark("scalar", "ac")
                        kb.psum_consumed(3, tOe)
                        kb.psum_wait(5)
                        kb.do("tensor", "matmul", pb[5][0:128, 0:128],
                              s_kb_[:], s_wfl[:], start=True, stop=True)
                        tw5 = kb.mark("tensor", "pe")
                        kb.wait("vector", tw5)
                        kb.do("vector", "scalar_tensor_tensor",
                              s_state[wbuf][:, sl], s_state[rb][:, sl],
                              s_acb[:, co + h:co + h + 1],
                              pb[5][0:128, 0:128], ALU.mult, ALU.add)
                        tw6 = kb.mark("vector", "dv")
                        kb.psum_consumed(5, tw6)
                    ld_free[pp] = kb.tok("pe")
                kb.barrier()
                kb.ckpt("p5")
                snap("core_raw0", core[0][:], (128, TOK), BF16)
                snap("core_raw15", core[15][:], (128, TOK), BF16)
            if debug:
                dbg("core_raw0", core[0][:], (128, TOK), BF16)
                dbg("core_raw15", core[15][:], (128, TOK), BF16)

            for g in reversed(sel_guards):
                g.__exit__(None, None, None)

            with nc.sbuf_tensor("zin0", [128, TOK], BF16) as zin0, \
                 nc.sbuf_tensor("zin1", [128, TOK], BF16) as zin1, \
                 nc.sbuf_tensor("gsc", [128, TOK], BF16) as gsc:
                zin = [zin0, zin1]
                zfree = [None, None]
                gsc_free = [None]
                rows6_free = [None]
                kb.do("sync", "dma_start", zin[0][:],
                      z_hbm[0])
                kb.mark("sync", "wdma", 16)
                tz = [None] * HV
                tz[0] = kb.tok("wdma")
                for h in range(HV):
                    if h + 1 < HV:
                        kb.wait("sync", zfree[(h + 1) & 1])
                        kb.do("sync", "dma_start", zin[(h + 1) & 1][:],
                              z_hbm[h + 1])
                        kb.mark("sync", "wdma", 16)
                        tz[h + 1] = kb.tok("wdma")
                    kb.wait("scalar", gsc_free[0])
                    kb.do("scalar", "activation", gsc[:], core[h][:],
                          AF.Square)
                    tq = kb.mark("scalar", "ac")
                    kb.wait("tensor", tq)
                    kb.psum_wait(6)
                    kb.do("tensor", "matmul", pb[6][0:1, 0:512],
                          s_onesb[:], gsc[:, 0:512],
                          start=True, stop=True)
                    kb.psum_wait(5)
                    kb.do("tensor", "matmul", pb[5][0:1, 0:512],
                          s_onesb[:], gsc[:, 512:1024],
                          start=True, stop=True)
                    tsm = kb.mark("tensor", "pe")
                    kb.wait("scalar", tsm)
                    kb.wait("scalar", rows6_free[0])
                    kb.do("scalar", "activation", s_rows[0:1, 0:512],
                          pb[6][0:1, 0:512], AF.Sqrt,
                          scale=s_cst[0:1, 7:8], bias=s_cst[0:1, 8:9])
                    kb.do("scalar", "activation", s_rows[0:1, 512:1024],
                          pb[5][0:1, 0:512], AF.Sqrt,
                          scale=s_cst[0:1, 7:8], bias=s_cst[0:1, 8:9])
                    tsr = kb.mark("scalar", "ac")
                    kb.psum_consumed(6, tsr)
                    kb.psum_consumed(5, tsr)
                    kb.wait("vector", tsr)
                    kb.do("vector", "reciprocal", s_rows[0:1, :],
                          s_rows[0:1, :])
                    kb.do("vector", "tensor_copy", s_rowb[0:1, :],
                          s_rows[0:1, :])
                    trc = kb.mark("vector", "dv")
                    rows6_free[0] = trc
                    kb.wait("tensor", trc)
                    kb.psum_wait(6)
                    kb.do("tensor", "matmul", pb[6][:, 0:512],
                          s_onesrb[:], s_rowb[0:1, 0:512],
                          start=True, stop=True)
                    kb.psum_wait(5)
                    kb.do("tensor", "matmul", pb[5][:, 0:512],
                          s_onesrb[:], s_rowb[0:1, 512:1024],
                          start=True, stop=True)
                    tbc = kb.mark("tensor", "pe")
                    kb.wait("vector", tbc)
                    kb.do("vector", "tensor_tensor", gsc[:, 0:512],
                          core[h][:, 0:512], pb[6][:, 0:512], ALU.mult)
                    kb.do("vector", "tensor_tensor", gsc[:, 512:1024],
                          core[h][:, 512:1024], pb[5][:, 0:512], ALU.mult)
                    tgg = kb.mark("vector", "dv")
                    kb.psum_consumed(6, tgg)
                    kb.psum_consumed(5, tgg)
                    kb.wait_p16("vector", tz[h])
                    kb.do("vector", "scalar_tensor_tensor", gsc[:],
                          gsc[:], s_normw[:, 0:1], zin[h & 1][:],
                          ALU.mult, ALU.mult)
                    kb.do("vector", "tensor_copy", core[h][:], gsc[:])
                    zfree[h & 1] = kb.mark("vector", "dv")
                    gsc_free[0] = zfree[h & 1]
                kb.barrier()
                kb.ckpt("p6")
                snap("core_g0", core[0][:], (128, TOK), BF16)
            if debug:
                dbg("core_g0", core[0][:], (128, TOK), BF16)


        with ExitStack() as sc7:
            E7 = sc7.enter_context
            att_guards = [nc.sbuf_tensor(f"att{j}", [128, TOK], BF16)
                          for j in range(16)]
            att = [g.__enter__() for g in att_guards]
            wtok2 = load_wblk(wout_b[0], 0, False, 0)
            for o in range(16):
                widx = o & 1
                nxt = None
                if o + 1 < 16:
                    nxt = load_wblk(
                        wout_b[o + 1], (o + 1) & 1, False, (o + 1) & 1)
                kb.wait("tensor", wtok2)
                for si in range(2):
                    bank = (2 * o + si) % 2
                    kb.psum_wait(bank)
                    for kk in range(16):
                        kb.do("tensor", "matmul", pb[bank][0:128, 0:512],
                              wb[widx][:, kk * 128:(kk + 1) * 128],
                              core[kk][:, si * 512:(si + 1) * 512],
                              start=(kk == 0), stop=(kk == 15))
                    tpp = kb.mark("tensor", "pe")
                    wb_free[widx] = tpp
                    kb.wait("vector", tpp)
                    kb.do("vector", "tensor_scalar",
                          att[o][:, si * 512:(si + 1) * 512],
                          pb[bank][0:128, 0:512],
                          s_boutc[:, o:o + 1], None, ALU.add)
                    te = kb.mark("vector", "dv")
                    kb.psum_consumed(bank, te)
                wtok2 = nxt
            conv_pump(len(conv_seq))
            kb.barrier()
            kb.ckpt("p7a")
            snap("att0", att[0][:], (128, TOK), BF16)
            snap("att15", att[15][:], (128, TOK), BF16)
            if debug:
                dbg("att0", att[0][:], (128, TOK), BF16)

            with nc.sbuf_tensor("atok", [128, HID], BF16) as atok, \
                 nc.sbuf_tensor("xt", [128, HID], F32) as xt, \
                 nc.sbuf_tensor("x1t", [128, HID], F32) as x1t, \
                 nc.sbuf_tensor("l3t", [128, HID], BF16) as l3t, \
                 nc.sbuf_tensor("l3s", [128, HID], BF16) as l3s, \
                 nc.sbuf_tensor("st7", [128, 8], F32) as st7:
                xt_free = None
                atok_free = None
                x1t_free = []
                l3t_free = None
                l3s_free = None
                for t in range(NTT):
                    kb.wait("sync", xt_free)
                    kb.do("sync", "dma_start", xt[:],
                          x_main[t * TT:(t + 1) * TT, :])
                    kb.mark("sync", "wdma", 16)
                    kb.do("sync", "dma_start", xt[:],
                          x_main[t * TT:(t + 1) * TT, :])
                    kb.mark("sync", "wdma", 16)
                    txx = kb.tok("wdma")
                    kb.wait("scalar", atok_free)
                    for j in range(16):
                        bank = 2 + (j & 1)
                        kb.psum_wait(bank)
                        kb.do("tensor", "matmul", pb[bank][0:128, 0:128],
                              att[j][:, t * TT:(t + 1) * TT], s_i128b[:],
                              start=True, stop=True)
                        tpj = kb.mark("tensor", "pe")
                        kb.wait("scalar", tpj)
                        kb.do("scalar", "copy",
                              atok[:, j * 128:(j + 1) * 128],
                              pb[bank][0:128, 0:128])
                        taj = kb.mark("scalar", "ac")
                        kb.psum_consumed(bank, taj)
                    tat = kb.tok("ac")
                    kb.wait("vector", tat)
                    kb.wait("vector", txx)
                    for tkn in x1t_free:
                        kb.wait("vector", tkn)
                    tl2 = ln_apply(atok, 128, None, st7,
                                   add_ap=xt[:], out_f32=x1t[:])
                    kb.wait("gpsimd", tl2)
                    kb.do("gpsimd", "dma_start",
                          x1_hbm[t * TT:(t + 1) * TT, :], x1t[:])
                    tx1d = kb.mark("gpsimd", "gdma", 16)
                    xt_free = tl2
                    atok_free = tl2
                    kb.wait("vector", l3t_free)
                    tl3 = ln_apply(x1t, 128, l3t[:], st7)
                    x1t_free = [tx1d, kb.tok("ac"), tl3]
                    kb.wait("scalar", l3s_free)
                    for j in range(16):
                        bank = 2 + (j & 1)
                        kb.wait("tensor", tl3)
                        kb.psum_wait(bank)
                        kb.do("tensor", "matmul", pb[bank][0:128, 0:128],
                              l3t[:, j * 128:(j + 1) * 128], s_i128b[:],
                              start=True, stop=True)
                        tpj = kb.mark("tensor", "pe")
                        kb.wait("scalar", tpj)
                        kb.do("scalar", "copy",
                              l3s[:, j * 128:(j + 1) * 128],
                              pb[bank][0:128, 0:128])
                        taj = kb.mark("scalar", "ac")
                        kb.psum_consumed(bank, taj)
                    l3t_free = kb.tok("pe")
                    tl3s = kb.tok("ac")
                    kb.wait("gpsimd", tl3s)
                    for j in range(16):
                        kb.do("gpsimd", "dma_start",
                              ln3_hbm[t // 4, j][:, (t % 4) * 128:
                                                 (t % 4) * 128 + 128],
                              l3s[:, j * 128:(j + 1) * 128])
                        kb.mark("gpsimd", "gdma", 16)
                    l3s_free = kb.tok("gdma")
            kb.barrier()
            snap("x1a", x1_hbm[0:128, :], (128, HID), F32)
            snap("x1b", x1_hbm[TOK - 128:TOK, :], (128, HID), F32)
            snap("ln3a", ln3_hbm[0, 0], (128, 512), BF16)

            if debug:
                dx1 = nc.dram_tensor("dbg_x1", [TOK, HID], F32,
                                     kind="ExternalOutput").ap()
                kb.barrier()
                kb.do("gpsimd", "dma_start", dx1, x1_hbm)
                kb.mark("gpsimd", "gdma", 16)
                dl3 = nc.dram_tensor("dbg_ln3", [128, 512], BF16,
                                     kind="ExternalOutput").ap()
                kb.do("gpsimd", "dma_start", dl3, ln3_hbm[0, 0])
                kb.mark("gpsimd", "gdma", 16)
                kb.barrier()
            for g in reversed(att_guards):
                g.__exit__(None, None, None)
            qkv_guard.__exit__(None, None, None)

            with ExitStack() as sc8:
                E8 = sc8.enter_context
                fuse = E8(nc.sbuf_tensor("fuse", [128, NI * 512], BF16))
                l3in = E8(nc.sbuf_tensor("l3in", [128, 16 * 512], BF16))
                gact = [E8(nc.sbuf_tensor(f"gact{k}", [128, 512], BF16))
                        for k in range(2)]
                ffncm = E8(nc.sbuf_tensor("ffncm", [128, 16 * 512], BF16))
                gub = [E8(nc.sbuf_tensor(f"gub{k}", [128, 2048], BF16))
                       for k in range(4)]
                dnb = [E8(nc.sbuf_tensor(f"dnb{k}", [128, 2048], BF16))
                       for k in range(2)]
                ftok = E8(nc.sbuf_tensor("ftok", [128, HID], BF16))
                x1r = E8(nc.sbuf_tensor("x1r", [128, HID], F32))
                l4t = E8(nc.sbuf_tensor("l4t", [128, HID], F32))
                st8 = E8(nc.sbuf_tensor("st8", [128, 8], F32))
                gact_free = [None, None]
                gub_free = [None] * 4
                dnb_free = [None] * 2
                fuse_free = None
                ffncm_free = None
                ftok_free = None
                x1r_free = None
                l4t_free = None

                def load_gu(ib, which):
                    bi = 2 * which + (ib & 1)
                    kb.wait("sync", gub_free[bi])
                    w = gw_bf if which == 0 else uw_bf
                    for hh in range(2):
                        kb.do("sync", "dma_start",
                              gub[bi][:, hh * 1024:(hh + 1) * 1024],
                              w[ib, hh])
                        kb.mark("sync", "wdma", 16)
                    return kb.tok("wdma")

                def load_dn(ib, slot):
                    kb.wait("sync", dnb_free[slot])
                    for hh in range(2):
                        kb.do("sync", "dma_start",
                              dnb[slot][:, hh * 1024:(hh + 1) * 1024],
                              dn_bf[ib, hh])
                        kb.mark("sync", "wdma", 16)
                    return kb.tok("wdma")

                l3in_free = None
                for half in range(2):
                    kb.wait("sync", l3in_free)
                    for kk in range(16):
                        kb.do("sync", "dma_start",
                              l3in[:, kk * 512:(kk + 1) * 512],
                              ln3_hbm[half, kk])
                        kb.mark("sync", "wdma", 16)
                    tl3in = kb.tok("wdma")
                    kb.wait_p16("tensor", tl3in)
                    tg_tok = load_gu(0, 0)
                    tu_tok = load_gu(0, 1)
                    kb.wait("vector", fuse_free)
                    for ib in range(NI):
                        gb = ib & 1
                        ub = 2 + (ib & 1)
                        kb.wait_p16("tensor", tg_tok)
                        bank = ib % 2
                        kb.psum_wait(bank)
                        for kk in range(16):
                            kb.do("tensor", "matmul", pb[bank][0:128, 0:512],
                                  gub[gb][:, kk * 128:(kk + 1) * 128],
                                  l3in[:, kk * 512:(kk + 1) * 512],
                                  start=(kk == 0), stop=(kk == 15))
                        tg1 = kb.mark("tensor", "pe")
                        gub_free[gb] = tg1
                        if ib + 1 < NI:
                            tg_tok = load_gu(ib + 1, 0)
                        kb.wait("scalar", tg1)
                        kb.wait("scalar", gact_free[ib & 1])
                        kb.do("scalar", "activation", gact[ib & 1][:],
                              pb[bank][0:128, 0:512], AF.Silu,
                              scale=s_gsc[:, ib:ib + 1])
                        tg2 = kb.mark("scalar", "ac")
                        kb.psum_consumed(bank, tg2)
                        kb.wait_p16("tensor", tu_tok)
                        bank2 = 2 + (ib % 2)
                        kb.psum_wait(bank2)
                        for kk in range(16):
                            kb.do("tensor", "matmul",
                                  pb[bank2][0:128, 0:512],
                                  gub[ub][:, kk * 128:(kk + 1) * 128],
                                  l3in[:, kk * 512:(kk + 1) * 512],
                                  start=(kk == 0), stop=(kk == 15))
                        tu1 = kb.mark("tensor", "pe")
                        gub_free[ub] = tu1
                        if ib + 1 < NI:
                            tu_tok = load_gu(ib + 1, 1)
                        kb.wait("vector", tu1)
                        kb.wait("vector", tg2)
                        kb.do("vector", "scalar_tensor_tensor",
                              fuse[:, ib * 512:(ib + 1) * 512],
                              pb[bank2][0:128, 0:512],
                              s_usc[:, ib:ib + 1], gact[ib & 1][:],
                              ALU.mult, ALU.mult)
                        tf = kb.mark("vector", "dv")
                        gact_free[ib & 1] = tf
                        kb.psum_consumed(bank2, tf)
                    tfuse = kb.tok("dv")
                    l3in_free = kb.tok("pe")
                    if debug2:
                        kb.wait("gpsimd", tfuse)
                        snap(f"fuse_h{half}", fuse[:, 0:4096], (128, 4096),
                             BF16)
                        snap(f"l3in_h{half}", l3in[:], (128, 16 * 512), BF16)
                    kb.wait("tensor", tfuse)
                    tdn = [None] * 2
                    for s in range(2):
                        tdn[s] = load_dn(s, s)
                    kb.wait("vector", ffncm_free)
                    for gi in range(2 * NI):
                        og, ib = divmod(gi, NI)
                        slot = gi & 1
                        if ib == 0:
                            for oo in range(8):
                                kb.psum_wait(oo)
                        kb.wait_p16("tensor", tdn[slot])
                        for oo in range(8):
                            o = og * 8 + oo
                            kb.do("tensor", "matmul",
                                  pb[oo][0:128, 0:512],
                                  dnb[slot][:, o * 128:(o + 1) * 128],
                                  fuse[:, ib * 512:(ib + 1) * 512],
                                  start=(ib == 0), stop=(ib == NI - 1))
                        tdd = kb.mark("tensor", "pe")
                        dnb_free[slot] = tdd
                        if gi + 2 < 2 * NI:
                            tdn[slot] = load_dn((gi + 2) % NI, slot)
                        if ib == NI - 1:
                            tdall = kb.tok("pe")
                            kb.wait("vector", tdall)
                            for oo in range(8):
                                o = og * 8 + oo
                                kb.do("vector", "tensor_scalar",
                                      ffncm[:, o * 512:(o + 1) * 512],
                                      pb[oo][0:128, 0:512],
                                      s_dsc[:, o:o + 1], None, ALU.mult)
                                tev = kb.mark("vector", "dv")
                                kb.psum_consumed(oo, tev)
                    tffn = kb.tok("dv")
                    fuse_free = kb.tok("pe")
                    if debug2:
                        kb.wait("gpsimd", tffn)
                        snap(f"ffncm_h{half}", ffncm[:], (128, 16 * 512),
                             BF16)
                    if debug and half == 0:
                        dbg("fuse0", fuse[:, 0:512], (128, 512), BF16)
                        dbg("ffncm0", ffncm[:, 0:512], (128, 512), BF16)
                        dbg("l3in0", l3in[:, 0:512], (128, 512), BF16)
                    for sub in range(4):
                        tglob = half * 512 + sub * 128
                        kb.wait("sync", x1r_free)
                        kb.do("sync", "dma_start", x1r[:],
                              x1_hbm[tglob:tglob + 128, :])
                        kb.mark("sync", "wdma", 16)
                        kb.do("sync", "dma_start", x1r[:],
                              x1_hbm[tglob:tglob + 128, :])
                        kb.mark("sync", "wdma", 16)
                        tx1 = kb.tok("wdma")
                        kb.wait("scalar", ftok_free)
                        for j in range(16):
                            bank = 6 + (j & 1)
                            kb.wait("tensor", tffn)
                            kb.psum_wait(bank)
                            kb.do("tensor", "matmul",
                                  pb[bank][0:128, 0:128],
                                  ffncm[:, j * 512 + sub * 128:
                                        j * 512 + sub * 128 + 128],
                                  s_i128b[:], start=True, stop=True)
                            tpj = kb.mark("tensor", "pe")
                            kb.wait("scalar", tpj)
                            kb.do("scalar", "copy",
                                  ftok[:, j * 128:(j + 1) * 128],
                                  pb[bank][0:128, 0:128])
                            taj = kb.mark("scalar", "ac")
                            kb.psum_consumed(bank, taj)
                        tft = kb.tok("ac")
                        kb.wait("vector", tft)
                        kb.wait("vector", tx1)
                        kb.wait("vector", l4t_free)
                        tl4 = ln_apply(ftok, 128, None, st8,
                                       add_ap=x1r[:], out_f32=l4t[:])
                        ftok_free = tl4
                        x1r_free = tl4
                        kb.wait("gpsimd", tl4)
                        kb.do("gpsimd", "dma_start",
                              out_x[tglob:tglob + 128, :], l4t[:])
                        l4t_free = kb.mark("gpsimd", "gdma", 16)
                    ffncm_free = kb.tok("pe")
                kb.barrier()
        kb.barrier()
        kb.ckpt("full")
        if cut is not None:
            kb.cut_at(cut)
        kb.wait("gpsimd", kb.tok("gdma"))

        if kb_sink is not None:
            kb_sink.append(kb)
        with nc.Block() as block:
            kb.emit(block)
    return nc


_NC_CACHE = {}


def _get_nc(debug=False):
    key = bool(debug)
    if key not in _NC_CACHE:
        _NC_CACHE[key] = build_nc(debug=key)
    return _NC_CACHE[key]


def _host_inputs(inp):
    f32 = np.float32
    bf16 = ml_dtypes.bfloat16
    x = np.ascontiguousarray(inp["L0_attn_hidden_states"], dtype=f32)
    Wqkv = np.asarray(inp["L0_attn_Wqkv"], f32)
    Wz = np.asarray(inp["L0_attn_Wz"], f32)
    Wb = np.asarray(inp["L0_attn_Wb"], f32)
    Wa = np.asarray(inp["L0_attn_Wa"], f32)
    bqkv = np.asarray(inp["L0_attn_bqkv"], f32)
    bz = np.asarray(inp["L0_attn_bz"], f32)
    bb = np.asarray(inp["L0_attn_bb"], f32)
    ba = np.asarray(inp["L0_attn_ba"], f32)
    convw = np.asarray(inp["L0_attn_conv_w"], f32)
    convb = np.asarray(inp["L0_attn_conv_b"], f32)
    dtb = np.asarray(inp["L0_attn_dt_bias"], f32)
    alog = np.asarray(inp["L0_attn_A_log"], f32)
    normw = np.asarray(inp["L0_attn_norm_w"], f32)
    Wout = np.asarray(inp["L0_attn_Wout"], f32)
    bout = np.asarray(inp["L0_attn_bout"], f32)
    gw = np.asarray(inp["L0_ffn_gate_w"], f32)
    uw = np.asarray(inp["L0_ffn_up_w"], f32)
    dw = np.asarray(inp["L0_ffn_down_w"], f32)
    gs = np.asarray(inp["L0_ffn_gate_scale"], f32)
    us = np.asarray(inp["L0_ffn_up_scale"], f32)
    ds = np.asarray(inp["L0_ffn_down_scale"], f32)

    Wcat = np.zeros((NIN * 128, HID), f32)
    Wcat[0:CONVD] = Wqkv
    Wcat[CONVD:CONVD + HID] = Wz
    Wcat[6144:6160] = Wb
    Wcat[6176:6192] = Wa
    WinT = Wcat.T
    w_in = np.ascontiguousarray(
        WinT.reshape(16, 128, NIN, 128).transpose(2, 1, 0, 3)
        .reshape(NIN, 128, 2, 1024).transpose(0, 2, 1, 3))
    b_cat = np.zeros((NIN * 128,), f32)
    b_cat[0:CONVD] = bqkv
    b_cat[CONVD:CONVD + HID] = bz
    b_cat[6144:6160] = bb * 0.0
    b_cat[6176:6192] = ba
    b_in = np.ascontiguousarray(b_cat.reshape(NIN, 128).T)

    conv_w_d = np.ascontiguousarray(
        convw.reshape(32, 128, KC).transpose(1, 0, 2).reshape(128, 32 * KC))
    conv_b_d = np.ascontiguousarray(convb.reshape(32, 128).T)
    small = np.zeros((16, 8), f32)
    small[:, 0] = alog
    small[:, 1] = dtb
    small[:, 2] = bb
    norm_w_d = normw.reshape(128, 1)

    WoutT = np.ascontiguousarray(Wout.T)
    wout_b = np.ascontiguousarray(
        WoutT.reshape(16, 128, 16, 128).transpose(2, 1, 0, 3)
        .reshape(16, 128, 2, 1024).transpose(0, 2, 1, 3))
    bout_c = np.ascontiguousarray(bout.reshape(16, 128).T)

    gate_b = np.ascontiguousarray(
        gw.reshape(16, 128, NI, 128).transpose(2, 1, 0, 3)
        .reshape(NI, 128, 2, 1024).transpose(0, 2, 1, 3))
    up_b = np.ascontiguousarray(
        uw.reshape(16, 128, NI, 128).transpose(2, 1, 0, 3)
        .reshape(NI, 128, 2, 1024).transpose(0, 2, 1, 3))
    down_b = np.ascontiguousarray(
        dw.reshape(NI, 128, 2, 1024).transpose(0, 2, 1, 3))
    g_sc = np.ascontiguousarray(gs.reshape(NI, 128).T)
    u_sc = np.ascontiguousarray(us.reshape(NI, 128).T)
    d_sc = np.ascontiguousarray(ds.reshape(16, 128).T)

    i128b = np.eye(128, dtype=bf16)
    i16f = np.eye(16, dtype=f32)
    selh = np.zeros((16, 16 * 128), f32)
    for h in range(16):
        selh[h, h * 128:(h + 1) * 128] = 1.0
    selhb = selh.astype(bf16)
    ones16f = np.ones((16, 128), f32)
    onesb = np.ones((128, 1), bf16)
    onesrb = np.ones((1, 128), bf16)
    onesrf = np.ones((1, 128), f32)
    jj, ff = np.mgrid[0:128, 0:128]
    mask_lo = (ff < jj).astype(bf16)
    mask_ue = (ff >= jj).astype(bf16)
    rmask = np.ones((16, TOK), f32)
    rmask[:, ::C] = 0.0
    e127 = np.zeros((128, 1), f32)
    e127[127, 0] = 1.0
    csts = np.zeros((1, 16), f32)
    csts[0, 0] = LN_EPS
    csts[0, 1] = 128.0
    csts[0, 2] = 1.0
    csts[0, 3] = 128.0 * L2_EPS
    csts[0, 4] = L2_EPS
    csts[0, 5] = -1.0
    csts[0, 6] = 1.0
    csts[0, 7] = 1.0 / 128.0
    csts[0, 8] = RMS_EPS
    cstc = np.broadcast_to(csts, (128, 16)).copy()

    shared = dict(w_in=w_in, b_in=b_in, conv_w=conv_w_d, conv_b=conv_b_d,
                  small=small, norm_w=norm_w_d, wout_b=wout_b, bout_c=bout_c,
                  gate_w=gate_b, up_w=up_b, down_in=down_b, g_scale=g_sc,
                  u_scale=u_sc, d_scale=d_sc, i128b=i128b, i16f=i16f,
                  selhb=selhb, selhf=selh, ones16f=ones16f, onesb=onesb,
                  onesrb=onesrb, onesrf=onesrf, mask_lo=mask_lo,
                  mask_ue=mask_ue, rmask=rmask, e127=e127, csts=csts, cstc=cstc)

    in_maps = []
    for c in range(8):
        b, half = c // 2, c % 2
        xm = np.ascontiguousarray(x[b, half * TOK:(half + 1) * TOK])
        if half == 1:
            xh = np.ascontiguousarray(x[b, TOK - HALO:TOK])
            fl = 1.0
        else:
            xh = np.zeros((HALO, HID), np.float32)
            fl = 0.0
        sel = np.zeros((1, 8), np.float32)
        if half == 1:
            sel[0, c - 1] = 1.0
        m = dict(shared)
        m.update(x_main=xm, x_halo=xh,
                 flag=np.full((1, 1), fl, np.float32), sel8=sel)
        in_maps.append(m)
    return in_maps


def kernel(**inputs):
    nc = _get_nc(debug=False)
    in_maps = _host_inputs(inputs)
    res = run_bass_kernel_spmd(nc, in_maps, core_ids=list(range(8)))
    out = np.empty((4, 2048, HID), np.float32)
    for c in range(8):
        b, half = c // 2, c % 2
        out[b, half * TOK:(half + 1) * TOK] = res.results[c]["out_x"]
    return out


if __name__ == "__main__":
    print("kernel module OK")


# version 19 (final)
# speedup vs baseline: 1.0309x; 1.0059x over previous
from contextlib import ExitStack

import numpy as np
import ml_dtypes

import concourse.bass as bass
import concourse.mybir as mybir
from concourse.bass_utils import run_bass_kernel_spmd
from concourse.replica_groups import maybe_share_collective_output_space

F32 = mybir.dt.float32
BF16 = mybir.dt.bfloat16
FP8 = mybir.dt.float8e4
AF = mybir.ActivationFunctionType
ALU = mybir.AluOpType
AX = mybir.AxisListType

HID = 2048
HK, HV, DK, DV, KC = 8, 16, 128, 128, 4
KEY, VAL, CONVD = HK * DK, HV * DV, 4096
TOK, HALO, TT, NTT = 1024, 8, 128, 8
C, NCH = 128, 8
NIN = 49
NI = 48
LN_EPS, L2_EPS, RMS_EPS = 1e-5, 1e-6, 1e-6

ENGS = ("sync", "gpsimd", "tensor", "vector", "scalar")
BSEM = {"sync": "Bs", "gpsimd": "Bg", "tensor": "Bt", "vector": "Bv",
        "scalar": "Ba"}


class KB:
    ESEM = {"vector": "dv", "scalar": "ac"}

    def __init__(self, nc):
        self.nc = nc
        self.streams = {e: [] for e in ENGS}
        self.cnt = {}
        self.hwm = {}
        self.sems = {}
        self.psum_free = {}
        self.wr_hist = {e: [] for e in ENGS}

    @staticmethod
    def _ap_names(vals):
        out = set()
        for v in vals:
            t = getattr(v, "tensor", None)
            if t is not None:
                lo = v.offset
                hi = lo + sum((n - 1) * st for st, n in v.ap) + 1
                out.add((t.name, lo, hi))
        return out

    @staticmethod
    def _overlaps(s1, s2):
        for (n1, a1, b1) in s1:
            for (n2, a2, b2) in s2:
                if n1 == n2 and a1 < b2 and a2 < b1:
                    return True
        return False

    def _hazard_guard(self, eng, meth, a, k):
        if eng not in self.ESEM:
            return
        writes = self._ap_names(a[:1])
        if "accum_out" in k and k["accum_out"] is not None:
            writes |= self._ap_names([k["accum_out"]])
        reads = self._ap_names(a[1:]) | self._ap_names(
            v for kk, v in k.items() if kk != "accum_out")
        hist = self.wr_hist[eng]
        need = False
        for dist in (1, 2):
            if len(hist) >= dist:
                idx, wset, rset = hist[-dist]
                if (self._overlaps(reads, wset) or
                        self._overlaps(writes, wset) or
                        self._overlaps(writes, rset)):
                    need = True
                    writer_idx = idx
                    break
        if need:
            ent = self.streams[eng][writer_idx]
            tail = self.streams[eng][writer_idx + 1:]
            if ent[4]:
                sem = ent[4][-1][0]
            else:
                sem = "hz" + eng[0]
                ent[4].append((sem, 1))
                self.cnt[sem] += 1
            after = sum(n for it in tail if it[0] == "op"
                        for (sm, n) in it[4] if sm == sem)
            self.wait(eng, (sem, self.cnt[sem] - after))
            sp = getattr(self, "spacer", None)
            if sp and eng in sp:
                meth_s, args_s = sp[eng]
                for _ in range(2):
                    self.streams[eng].append(["op", meth_s, args_s, {}, []])
            self.wr_hist[eng] = []
        self.wr_hist[eng].append((len(self.streams[eng]), writes, reads))
        if len(self.wr_hist[eng]) > 2:
            self.wr_hist[eng] = self.wr_hist[eng][-2:]

    def ckpt(self, name):
        if not hasattr(self, "ckpts"):
            self.ckpts = {}
        self.ckpts[name] = {e: len(s) for e, s in self.streams.items()}

    def cut_at(self, name):
        marks = self.ckpts[name]
        for e in ENGS:
            self.streams[e] = self.streams[e][:marks[e]]

    def new_sem(self, stack, name):
        self.sems[name] = stack.enter_context(self.nc.semaphore(name))
        self.cnt[name] = 0

    def do(self, eng, meth, *a, **k):
        self._hazard_guard(eng, meth, a, k)
        self.streams[eng].append(["op", meth, a, k, []])

    def mark(self, eng, sem, n=1):
        assert self.streams[eng] and self.streams[eng][-1][0] == "op", \
            f"mark on empty/wait stream {eng}"
        self.streams[eng][-1][4].append((sem, n))
        self.cnt[sem] += n
        return (sem, self.cnt[sem])

    def tok(self, sem):
        return (sem, self.cnt[sem])

    def wait_p16(self, eng, token):
        if token is None:
            return
        self.wait(eng, (token[0], token[1] + 16))

    def wait(self, eng, token):
        if token is None:
            return
        sem, v = token
        if v <= 0 or self.hwm.get((eng, sem), 0) >= v:
            return
        self.hwm[(eng, sem)] = v
        self.streams[eng].append(["wait", sem, v])

    def barrier(self):
        for e in ("sync", "gpsimd"):
            st = self.streams[e]
            idx = len(st) - 1
            while idx >= 0 and st[idx][0] != "op":
                idx -= 1
            if idx >= 0 and st[idx][1] == "dma_start" and st[idx][4]:
                sem = st[idx][4][-1][0]
                self.streams[e].append(
                    ["op", "dma_start", st[idx][2], dict(st[idx][3]), []])
                self.mark(e, sem, 16)
        toks = []
        for e in ENGS:
            idx = len(self.streams[e]) - 1
            while idx >= 0 and self.streams[e][idx][0] != "op":
                idx -= 1
            if idx < 0:
                toks.append(self.tok(BSEM[e]))
                continue
            ent = self.streams[e][idx]
            if ent[4]:
                toks.append((ent[4][-1][0], self.cnt[ent[4][-1][0]]))
            else:
                n = 16 if e in ("sync", "gpsimd") else 1
                ent[4].append((BSEM[e], n))
                self.cnt[BSEM[e]] += n
                toks.append((BSEM[e], self.cnt[BSEM[e]]))
        for e in ENGS:
            for t in toks:
                self.wait(e, t)

    def psum_wait(self, bank):
        for t in self.psum_free.get(bank, []):
            self.wait("tensor", t)

    def psum_consumed(self, bank, token, add=False):
        if add:
            self.psum_free.setdefault(bank, []).append(token)
        else:
            self.psum_free[bank] = [token]

    def emit(self, block):
        def mk(stream):
            def body(e):
                for it in stream:
                    if it[0] == "wait":
                        e.wait_ge(self.sems[it[1]], it[2])
                    else:
                        _, meth, a, k, incs = it
                        inst = getattr(e, meth)(*a, **k)
                        for (s, n) in incs:
                            inst = inst.then_inc(self.sems[s], n)
            return body
        for e in ENGS:
            getattr(block, e)(mk(self.streams[e]))


def build_nc(debug=False, cut=None, kb_sink=None, debug2=False):
    nc = bass.Bass("TRN2", target_bir_lowering=False)
    dbg_list = []

    def din(name, shape, dt=F32):
        return nc.dram_tensor(name, list(shape), dt, kind="ExternalInput").ap()

    x_main = din("x_main", (TOK, HID))
    x_halo = din("x_halo", (HALO, HID))
    flag = din("flag", (1, 1))
    sel8 = din("sel8", (1, 8))
    w_in = din("w_in", (NIN, 2, 128, 1024))
    b_in = din("b_in", (128, NIN))
    conv_w = din("conv_w", (128, 32 * KC))
    conv_b = din("conv_b", (128, 32))
    small = din("small", (16, 8))
    norm_w = din("norm_w", (128, 1))
    wout_b = din("wout_b", (16, 2, 128, 1024))
    bout_c = din("bout_c", (128, 16))
    gate_w = din("gate_w", (NI, 2, 128, 1024))
    up_w = din("up_w", (NI, 2, 128, 1024))
    down_in = din("down_in", (NI, 2, 128, 1024))
    g_scale = din("g_scale", (128, NI))
    u_scale = din("u_scale", (128, NI))
    d_scale = din("d_scale", (128, 16))
    i128b = din("i128b", (128, 128), BF16)
    i16f = din("i16f", (16, 16))
    selhb = din("selhb", (16, 16 * 128), BF16)
    selhf = din("selhf", (16, 16 * 128))
    ones16f = din("ones16f", (16, 128))
    onesb = din("onesb", (128, 1), BF16)
    onesrb = din("onesrb", (1, 128), BF16)
    onesrf = din("onesrf", (1, 128))
    mask_lo = din("mask_lo", (128, 128), BF16)
    mask_ue = din("mask_ue", (128, 128), BF16)
    rmask = din("rmask", (16, TOK))
    e127 = din("e127", (128, 1))
    csts = din("csts", (1, 16))
    cstc = din("cstc", (128, 16))

    out_x = nc.dram_tensor("out_x", [TOK, HID], F32, kind="ExternalOutput").ap()

    def dscr(name, shape, dt=F32, **kw):
        return nc.dram_tensor(name, list(shape), dt, kind="Internal", **kw).ap()

    pan_hbm = dscr("pan_hbm", (NCH, 128, HV * 256), BF16)
    gt_hbm = dscr("gt_hbm", (NCH, 128, HV * 128), BF16)
    kt_hbm = dscr("kt_hbm", (NCH, 128, HK * 128), BF16)
    z_hbm = dscr("z_hbm", (HV, 128, TOK), BF16)
    x1_hbm = dscr("x1_hbm", (TOK, HID), F32)
    ln3_hbm = dscr("ln3_hbm", (2, 16, 128, 512), BF16)
    gw_bf = dscr("gw_bf", (NI, 2, 128, 1024), BF16)
    uw_bf = dscr("uw_bf", (NI, 2, 128, 1024), BF16)
    dn_bf = dscr("dn_bf", (NI, 2, 128, 1024), BF16)
    f8mid = dscr("f8mid", (3, 2, 128, 1024), FP8)
    rg = [[0, 1, 2, 3, 4, 5, 6, 7]]
    aspace = "Local"
    cc_in = dscr("cc_in", (128, HV * DV), BF16)
    cc_out = dscr("cc_out", (8 * 128, HV * DV), BF16, addr_space=aspace)

    with ExitStack() as st:
        E = st.enter_context
        kb = KB(nc)
        for s in ("wdma", "gdma", "pe", "dv", "ac",
                  "Bs", "Bg", "Bt", "Bv", "Ba", "hzv", "hzs"):
            kb.new_sem(st, s)

        pb = [E(nc.psum_tensor(f"pb{i}", [128, 512], F32)) for i in range(8)]

        def sb(n, sh, dt):
            return E(nc.sbuf_tensor(n, list(sh), dt))

        s_state = [sb(f"s_state{i}", (128, HV * DV), BF16) for i in range(2)]
        s_cst = sb("s_cst", (1, 16), F32)
        s_cstc = sb("s_cstc", (128, 16), F32)
        s_small = sb("s_small", (16, 8), F32)
        s_bias = sb("s_bias", (128, NIN), F32)
        s_convw = sb("s_convw", (128, 32 * KC), F32)
        s_convb = sb("s_convb", (128, 32), F32)
        s_normw = sb("s_normw", (128, 1), F32)
        s_boutc = sb("s_boutc", (128, 16), F32)
        s_gsc = sb("s_gsc", (128, NI), F32)
        s_usc = sb("s_usc", (128, NI), F32)
        s_dsc = sb("s_dsc", (128, 16), F32)
        s_i128b = sb("s_i128b", (128, 128), BF16)
        s_i16f = sb("s_i16f", (16, 16), F32)
        s_ones16f = sb("s_ones16f", (16, 128), F32)
        s_onesb = sb("s_onesb", (128, 1), BF16)
        s_onesrb = sb("s_onesrb", (1, 128), BF16)
        s_onesrf = sb("s_onesrf", (1, 128), F32)
        s_mlo = sb("s_mlo", (128, 128), BF16)
        s_mue = sb("s_mue", (128, 128), BF16)
        s_rmask = sb("s_rmask", (16, TOK), F32)
        s_e127 = sb("s_e127", (128, 1), F32)
        s_flag = sb("s_flag", (1, 1), F32)
        s_sel8 = sb("s_sel8", (1, 8), F32)
        s_flagcol = sb("s_flagcol", (128, 1), F32)
        s_selcol = sb("s_selcol", (128, 8), F32)
        s_negea = sb("s_negea", (16, 1), F32)
        s_beta = sb("s_beta", (16, TOK), F32)
        s_g = sb("s_g", (16, TOK), F32)
        s_c = sb("s_c", (16, TOK), F32)
        s_arow = sb("s_arow", (16, TOK), BF16)
        s_cT = sb("s_cT", (128, NCH * 16), F32)
        s_bT = sb("s_bT", (128, NCH * 16), F32)
        s_bTn = sb("s_bTn", (128, NCH * 16), F32)
        s_bA = sb("s_bA", (128, NCH * 16), F32)
        s_kbar = sb("s_kbar", (128, NCH * 16), F32)
        s_acb = sb("s_acb", (128, NCH * 16), F32)
        s_dg16 = sb("s_dg16", (16, 16), F32)
        s_sqd = sb("s_sqd", (128, 1024), BF16)
        wf = [sb(f"wf{i}", (128, 1024), F32) for i in range(2)]
        wb = [sb(f"wb{i}", (128, 2048), BF16) for i in range(2)]

        w8e = {"vector": sb("w8v", (128, 1024), FP8),
               "scalar": sb("w8s", (128, 1024), FP8)}
        s_rows = sb("s_rows", (1, TOK), F32)
        s_rowb = sb("s_rowb", (1, TOK), BF16)
        s_spc = sb("s_spc", (1, 8), F32)
        s_col = sb("s_col", (128, 4), F32)
        s_edf = sb("s_edf", (128, 128), F32)
        s_edf2 = sb("s_edf2", (128, 128), F32)

        wf_free = [None, None]
        wb_free = [None, None]

        kb.spacer = {"vector": ("memset", (s_spc[0:1, 0:2], 0.0)),
                     "scalar": ("copy", (s_spc[0:1, 2:3], s_spc[0:1, 3:4]))}

        def dbg(name, ap, shape, dt):
            if not debug:
                return
            d = nc.dram_tensor("dbg_" + name, list(shape), dt,
                               kind="ExternalOutput").ap()
            kb.barrier()
            kb.do("gpsimd", "dma_start", d, ap)
            kb.mark("gpsimd", "gdma", 16)
            kb.barrier()

        def snap(name, ap, shape, dt):
            if not debug2:
                return
            d = nc.dram_tensor("snap_" + name, list(shape), dt,
                               kind="ExternalOutput").ap()
            kb.do("gpsimd", "dma_start", d, ap)
            kb.mark("gpsimd", "gdma", 16)

        lw_half = [0]

        def load_wblk(src_ap, widx, cvt_fp8, parity):
            eng = "vector" if parity == 0 else "scalar"
            sem = "dv" if parity == 0 else "ac"
            meth = "tensor_copy" if eng == "vector" else "copy"
            for hh in range(2):
                wfi = lw_half[0] & 1
                lw_half[0] += 1
                kb.wait("sync", wf_free[wfi])
                kb.do("sync", "dma_start", wf[wfi][:], src_ap[hh])
                kb.mark("sync", "wdma", 16)
                tw = kb.tok("wdma")
                kb.wait_p16(eng, tw)
                if hh == 0:
                    kb.wait(eng, wb_free[widx])
                if cvt_fp8:
                    w8 = w8e[eng]
                    kb.do(eng, meth, w8[:], wf[wfi][:])
                    kb.do(eng, meth,
                          wb[widx][:, hh * 1024:(hh + 1) * 1024], w8[:])
                else:
                    kb.do(eng, meth,
                          wb[widx][:, hh * 1024:(hh + 1) * 1024], wf[wfi][:])
                t = kb.mark(eng, sem)
                wf_free[wfi] = t
            return t

        conv_seq = []
        for ib in range(NI):
            conv_seq.append((gate_w[ib], gw_bf[ib]))
            conv_seq.append((up_w[ib], uw_bf[ib]))
            conv_seq.append((down_in[ib], dn_bf[ib]))
        conv_state = {"i": 0}

        conv_emitted = {"d1": [], "d2": 0}

        def conv_pump(nblk=1):
            for _ in range(nblk):
                i = conv_state["i"]
                n = len(conv_seq)
                if i >= n + 1:
                    return
                conv_state["i"] = i + 1
                if i < n:
                    src, dst = conv_seq[i]
                    if i >= 3:
                        kb.wait("gpsimd", conv_emitted["d1"][i - 1])
                    kb.do("gpsimd", "dma_start", f8mid[i % 3], src)
                    kb.mark("gpsimd", "gdma", 16)
                    conv_emitted["d1"].append(kb.tok("gdma"))
                j = conv_emitted["d2"]
                if j < n - 1 and j == i - 1:
                    kb.wait("gpsimd", conv_emitted["d1"][j + 1])
                    kb.do("gpsimd", "dma_start", conv_seq[j][1],
                          f8mid[j % 3])
                    kb.mark("gpsimd", "gdma", 16)
                    conv_emitted["d2"] = j + 1
                if i == n:
                    kb.do("gpsimd", "dma_start", f8mid[(n - 1) % 3],
                          conv_seq[n - 1][0])
                    kb.mark("gpsimd", "gdma", 16)
                    kb.wait("gpsimd", kb.tok("gdma"))
                    kb.do("gpsimd", "dma_start", conv_seq[n - 1][1],
                          f8mid[(n - 1) % 3])
                    kb.mark("gpsimd", "gdma", 16)
                    conv_emitted["d2"] = n

        ln_state = {"prev": None}

        def ln_apply(xb, nrows, out_ap, stat,
                     add_ap=None, out_f32=None):
            kb.wait("scalar", ln_state["prev"])
            kb.do("vector", "reduce_sum", stat[0:nrows, 0:1],
                  xb[0:nrows, :], AX.X)
            kb.do("scalar", "activation", s_sqd[0:nrows, :],
                  xb[0:nrows, 0:1024], AF.Square,
                  accum_out=stat[0:nrows, 1:2])
            kb.do("scalar", "activation", s_sqd[0:nrows, :],
                  xb[0:nrows, 1024:2048], AF.Square,
                  accum_out=stat[0:nrows, 7:8])
            ta = kb.mark("scalar", "ac")
            kb.do("vector", "tensor_scalar_mul", stat[0:nrows, 2:3],
                  stat[0:nrows, 0:1], 1.0 / HID)
            kb.wait("vector", ta)
            kb.do("vector", "tensor_tensor", stat[0:nrows, 1:2],
                  stat[0:nrows, 1:2], stat[0:nrows, 7:8], ALU.add)
            kb.do("vector", "tensor_scalar_mul", stat[0:nrows, 3:4],
                  stat[0:nrows, 1:2], 1.0 / HID)
            kb.do("vector", "tensor_tensor", stat[0:nrows, 4:5],
                  stat[0:nrows, 2:3], stat[0:nrows, 2:3], ALU.mult)
            kb.do("vector", "tensor_tensor", stat[0:nrows, 4:5],
                  stat[0:nrows, 3:4], stat[0:nrows, 4:5], ALU.subtract)
            tv = kb.mark("vector", "dv")
            kb.wait("scalar", tv)
            kb.do("scalar", "activation", stat[0:nrows, 5:6],
                  stat[0:nrows, 4:5], AF.Sqrt, bias=s_cstc[0:nrows, 0:1])
            ta = kb.mark("scalar", "ac")
            kb.wait("vector", ta)
            kb.do("vector", "reciprocal", stat[0:nrows, 6:7],
                  stat[0:nrows, 5:6])
            if out_ap is not None:
                kb.do("vector", "tensor_scalar", out_ap,
                      xb[0:nrows, :], stat[0:nrows, 2:3], stat[0:nrows, 6:7],
                      ALU.subtract, ALU.mult)
            if add_ap is not None:
                kb.do("vector", "tensor_scalar", out_f32,
                      xb[0:nrows, :], stat[0:nrows, 2:3], stat[0:nrows, 6:7],
                      ALU.subtract, ALU.mult)
                kb.do("vector", "tensor_tensor", out_f32, out_f32, add_ap,
                      ALU.add)
            t = kb.mark("vector", "dv")
            ln_state["prev"] = t
            return t

        for dst, src in (
            (s_cst[:], csts), (s_cstc[:], cstc),
            (s_small[:], small), (s_bias[:], b_in),
            (s_convw[:], conv_w), (s_convb[:], conv_b), (s_normw[:], norm_w),
            (s_boutc[:], bout_c), (s_gsc[:], g_scale), (s_usc[:], u_scale),
            (s_dsc[:], d_scale), (s_i128b[:], i128b), (s_i16f[:], i16f),
            (s_ones16f[:], ones16f),
            (s_onesb[:], onesb), (s_onesrb[:], onesrb), (s_onesrf[:], onesrf),
            (s_mlo[:], mask_lo), (s_mue[:], mask_ue), (s_rmask[:], rmask),
            (s_e127[:], e127), (s_flag[:], flag), (s_sel8[:], sel8),
        ):
            kb.do("sync", "dma_start", dst, src)
            kb.mark("sync", "wdma", 16)
        kb.do("sync", "dma_start", s_cst[:], csts)
        kb.mark("sync", "wdma", 16)
        tc0 = kb.tok("wdma")
        for e in ("vector", "scalar", "tensor"):
            kb.wait(e, tc0)
        kb.do("scalar", "activation", s_negea[:, 0:1], s_small[:, 0:1], AF.Exp)
        ta = kb.mark("scalar", "ac")
        kb.wait("vector", ta)
        kb.do("vector", "tensor_scalar_mul", s_negea[:, 0:1],
              s_negea[:, 0:1], -1.0)
        kb.do("vector", "memset", s_state[0][:], 0.0)
        kb.do("vector", "memset", s_state[1][:], 0.0)
        kb.psum_wait(4)
        kb.do("tensor", "matmul", pb[4][:, 0:1], s_onesrf[:], s_flag[:],
              start=True, stop=True)
        kb.do("tensor", "matmul", pb[4][:, 8:16], s_onesrf[:], s_sel8[:],
              start=True, stop=True)
        tp = kb.mark("tensor", "pe")
        kb.wait("vector", tp)
        kb.do("vector", "tensor_copy", s_flagcol[:], pb[4][:, 0:1])
        kb.do("vector", "tensor_copy", s_selcol[:], pb[4][:, 8:16])
        tv = kb.mark("vector", "dv")
        kb.psum_consumed(4, tv)
        kb.barrier()
        kb.ckpt("p0")

        qkv_guard = nc.sbuf_tensor("qkv_s", [128, 32 * TOK], BF16)
        qkv_parent = qkv_guard.__enter__()
        qkv = [qkv_parent[:, j * TOK:(j + 1) * TOK] for j in range(32)]
        core = [qkv_parent[:, (16 + j) * TOK:(17 + j) * TOK]
                for j in range(16)]
        lnx_guards = [nc.sbuf_tensor(f"lnx{j}", [128, HALO + TOK], BF16)
                      for j in range(16)]
        lnx = [g.__enter__() for g in lnx_guards]
        if True:

            with nc.sbuf_tensor("xt0", [128, HID], F32) as xt0, \
                 nc.sbuf_tensor("xt1", [128, HID], F32) as xt1, \
                 nc.sbuf_tensor("lntok", [128, HID], BF16) as lntok, \
                 nc.sbuf_tensor("stat1", [128, 8], F32) as stat1:
                xb_ = [xt0, xt1]
                xfree = [None, None]
                lntok_free = [None]

                def p1_tile(xb, nrows, coloff):
                    kb.wait("vector", lntok_free[0])
                    tln = ln_apply(xb, nrows, lntok[0:nrows, :], stat1)
                    xb_done = (kb.tok("dv"), kb.tok("ac"))
                    for j in range(16):
                        bank = 2 + (j & 1)
                        kb.wait("tensor", tln)
                        kb.psum_wait(bank)
                        rhs = s_i128b[:] if nrows == 128 \
                            else s_i128b[0:nrows, 0:nrows]
                        kb.do("tensor", "matmul", pb[bank][0:128, 0:nrows],
                              lntok[0:nrows, j * 128:(j + 1) * 128], rhs,
                              start=True, stop=True)
                        tpj = kb.mark("tensor", "pe")
                        kb.wait("scalar", tpj)
                        kb.do("scalar", "copy",
                              lnx[j][:, coloff:coloff + nrows],
                              pb[bank][0:128, 0:nrows])
                        taj = kb.mark("scalar", "ac")
                        kb.psum_consumed(bank, taj)
                    lntok_free[0] = kb.tok("pe")
                    return xb_done

                kb.do("sync", "dma_start", xt0[0:HALO, :], x_halo)
                kb.mark("sync", "wdma", 16)
                kb.do("sync", "dma_start", xt0[0:HALO, :], x_halo)
                kb.mark("sync", "wdma", 16)
                th = kb.tok("wdma")
                kb.wait("vector", th)
                kb.wait("scalar", th)
                xfree[0] = p1_tile(xt0, HALO, 0)
                for t in range(NTT):
                    b = t & 1
                    if xfree[b] is not None:
                        for tkn in xfree[b]:
                            kb.wait("sync", tkn)
                    kb.do("sync", "dma_start", xb_[b][:],
                          x_main[t * TT:(t + 1) * TT, :])
                    kb.mark("sync", "wdma", 16)
                    tw = kb.tok("wdma")
                    kb.wait_p16("vector", tw)
                    kb.wait_p16("scalar", tw)
                    xfree[b] = p1_tile(xb_[b], 128, HALO + t * TT)
            kb.barrier()
            kb.ckpt("p1")
            snap("lnx0", lnx[0][:], (128, HALO + TOK), BF16)
            snap("lnx8", lnx[8][:], (128, HALO + TOK), BF16)
            if debug:
                dbg("lnx0", lnx[0][:], (128, HALO + TOK), BF16)
                dbg("lnx15", lnx[15][:], (128, HALO + TOK), BF16)

            with ExitStack() as sc_p2:
                E2 = sc_p2.enter_context
                mixb = [E2(nc.sbuf_tensor(f"mix{i}", [128, HALO + TOK], BF16))
                        for i in range(2)]
                cac = [E2(nc.sbuf_tensor(f"cac{i}", [128, TOK], BF16))
                       for i in range(2)]
                silu_f = E2(nc.sbuf_tensor("silu_f", [128, TOK], BF16))
                sq2 = E2(nc.sbuf_tensor("sq2", [128, TOK], BF16))
                ba_raw = E2(nc.sbuf_tensor("ba_raw", [48, TOK], F32))
                zst = [E2(nc.sbuf_tensor(f"zst{i}", [128, TOK], BF16))
                       for i in range(2)]

                mix_free = [None, None]
                rows_free = [None]
                zst_free = [None, None]
                silu_free = None
                sq2_free = None
                cac_free = [None, None]

                wtok = load_wblk(w_in[0], 0, False, 0)
                for o in range(NIN):
                    widx = o & 1
                    nxt = None
                    if o + 1 < NIN:
                        nxt = load_wblk(
                            w_in[o + 1], (o + 1) & 1, False, (o + 1) & 1)
                    kb.wait("tensor", wtok)
                    is_mix, is_z = o < 32, 32 <= o < 48
                    mt = mixb[o & 1] if is_mix else None
                    slices = ([(0, HALO)] if is_mix else []) + \
                        [(HALO, 512), (HALO + 512, 512)]
                    for si, (soff, nn) in enumerate(slices):
                        bank = (2 * o + si) % 4
                        kb.psum_wait(bank)
                        for kk in range(16):
                            kb.do("tensor", "matmul", pb[bank][0:128, 0:nn],
                                  wb[widx][:, kk * 128:(kk + 1) * 128],
                                  lnx[kk][:, soff:soff + nn],
                                  start=(kk == 0), stop=(kk == 15))
                        tpp = kb.mark("tensor", "pe")
                        wb_free[widx] = tpp
                        if is_mix:
                            kb.wait("vector", tpp)
                            if si == 0:
                                kb.wait("vector", mix_free[o & 1])
                            kb.do("vector", "tensor_scalar",
                                  mt[:, soff:soff + nn],
                                  pb[bank][0:128, 0:nn],
                                  s_bias[:, o:o + 1], None, ALU.add)
                            te = kb.mark("vector", "dv")
                        elif is_z:
                            kb.wait("scalar", tpp)
                            if si == 0:
                                kb.wait("scalar", zst_free[o & 1])
                            kb.do("scalar", "activation",
                                  zst[o & 1][:, soff - HALO:soff - HALO + nn],
                                  pb[bank][0:128, 0:nn], AF.Silu,
                                  bias=s_bias[:, o:o + 1])
                            te = kb.mark("scalar", "ac")
                        else:
                            kb.wait("vector", tpp)
                            kb.do("vector", "tensor_scalar",
                                  ba_raw[0:48, soff - HALO:soff - HALO + nn],
                                  pb[bank][0:48, 0:nn],
                                  s_bias[0:48, o:o + 1], None, ALU.add)
                            te = kb.mark("vector", "dv")
                        kb.psum_consumed(bank, te)
                    if is_z:
                        tz = kb.tok("ac")
                        kb.wait("gpsimd", tz)
                        kb.do("gpsimd", "dma_start",
                              z_hbm[o - 32],
                              zst[o & 1][:])
                        zst_free[o & 1] = kb.mark("gpsimd", "gdma", 16)
                    if is_mix:
                        kb.do("vector", "tensor_scalar_mul", mt[:, 0:HALO],
                              mt[:, 0:HALO], s_flagcol[:, 0:1])
                        kb.wait("vector", cac_free[0])
                        kb.do("vector", "tensor_scalar", cac[0][:],
                              mt[:, HALO:HALO + TOK],
                              s_convw[:, 4 * o + 3:4 * o + 4], None, ALU.mult)
                        kb.wait("vector", cac_free[1])
                        for d in (1, 2, 3):
                            src_acc = cac[(d + 1) & 1]
                            dst_acc = cac[d & 1]
                            kb.do("vector", "scalar_tensor_tensor",
                                  dst_acc[:], mt[:, HALO - d:HALO - d + TOK],
                                  s_convw[:, 4 * o + 3 - d:4 * o + 4 - d],
                                  src_acc[:], ALU.mult, ALU.add)
                        tcv = kb.mark("vector", "dv")
                        mix_free[o & 1] = tcv
                        acc = cac[1]
                        kb.wait("scalar", tcv)
                        if o >= 16:
                            kb.do("scalar", "activation", qkv[o], acc[:],
                                  AF.Silu, bias=s_convb[:, o:o + 1])
                            cac_free[1] = kb.mark("scalar", "ac")
                            cac_free[0] = cac_free[1]
                        else:
                            kb.wait("scalar", silu_free)
                            kb.do("scalar", "activation", silu_f[:], acc[:],
                                  AF.Silu, bias=s_convb[:, o:o + 1])
                            cac_free[1] = kb.mark("scalar", "ac")
                            cac_free[0] = cac_free[1]
                            kb.wait("scalar", sq2_free)
                            kb.do("scalar", "activation", sq2[:], silu_f[:],
                                  AF.Square)
                            tsq = kb.mark("scalar", "ac")
                            kb.wait("tensor", tsq)
                            kb.psum_wait(6)
                            kb.do("tensor", "matmul", pb[6][0:1, 0:512],
                                  s_onesb[:], sq2[:, 0:512],
                                  start=True, stop=True)
                            kb.psum_wait(5)
                            kb.do("tensor", "matmul", pb[5][0:1, 0:512],
                                  s_onesb[:], sq2[:, 512:1024],
                                  start=True, stop=True)
                            tp6 = kb.mark("tensor", "pe")
                            sq2_free = tp6
                            sc_i = 1 if o < 8 else 2
                            kb.wait("scalar", tp6)
                            kb.wait("scalar", rows_free[0])
                            kb.do("scalar", "activation", s_rows[0:1, 0:512],
                                  pb[6][0:1, 0:512], AF.Sqrt,
                                  scale=s_cst[0:1, sc_i:sc_i + 1],
                                  bias=s_cst[0:1, sc_i + 2:sc_i + 3])
                            kb.do("scalar", "activation", s_rows[0:1, 512:1024],
                                  pb[5][0:1, 0:512], AF.Sqrt,
                                  scale=s_cst[0:1, sc_i:sc_i + 1],
                                  bias=s_cst[0:1, sc_i + 2:sc_i + 3])
                            ta6 = kb.mark("scalar", "ac")
                            kb.psum_consumed(6, ta6)
                            kb.psum_consumed(5, ta6)
                            kb.wait("vector", ta6)
                            kb.do("vector", "reciprocal", s_rows[0:1, :],
                                  s_rows[0:1, :])
                            kb.do("vector", "tensor_copy", s_rowb[0:1, :],
                                  s_rows[0:1, :])
                            trr = kb.mark("vector", "dv")
                            rows_free[0] = trr
                            kb.wait("tensor", trr)
                            kb.psum_wait(6)
                            kb.do("tensor", "matmul", pb[6][:, 0:512],
                                  s_onesrb[:], s_rowb[0:1, 0:512],
                                  start=True, stop=True)
                            kb.psum_wait(5)
                            kb.do("tensor", "matmul", pb[5][:, 0:512],
                                  s_onesrb[:], s_rowb[0:1, 512:1024],
                                  start=True, stop=True)
                            tph = kb.mark("tensor", "pe")
                            kb.wait("vector", tph)
                            kb.do("vector", "tensor_tensor",
                                  qkv[o][:, 0:512], silu_f[:, 0:512],
                                  pb[6][:, 0:512], ALU.mult)
                            kb.do("vector", "tensor_tensor",
                                  qkv[o][:, 512:1024], silu_f[:, 512:1024],
                                  pb[5][:, 0:512], ALU.mult)
                            tqn = kb.mark("vector", "dv")
                            silu_free = tqn
                            kb.psum_consumed(6, tqn)
                            kb.psum_consumed(5, tqn)
                    wtok = nxt
                kb.barrier()

                kb.do("scalar", "activation", s_beta[:], ba_raw[0:16, :],
                      AF.Sigmoid, bias=s_small[:, 2:3])
                kb.do("vector", "tensor_scalar", s_g[:], ba_raw[32:48, :],
                      s_small[:, 1:2], None, ALU.add)
                tg = kb.mark("vector", "dv")
                kb.wait("scalar", tg)
                kb.do("scalar", "activation", ba_raw[0:16, :],
                      s_g[:], AF.Abs)
                t1 = kb.mark("scalar", "ac")
                kb.do("scalar", "activation", ba_raw[0:16, :],
                      ba_raw[0:16, :], AF.Exp, scale=s_cstc[0:16, 5:6])
                kb.do("scalar", "activation", ba_raw[0:16, :],
                      ba_raw[0:16, :], AF.Ln, bias=s_cstc[0:16, 6:7])
                t2 = kb.mark("scalar", "ac")
                kb.wait("vector", t2)
                kb.do("vector", "scalar_tensor_tensor", ba_raw[0:16, :],
                      s_g[:], 0.0, ba_raw[0:16, :], ALU.max, ALU.add)
                kb.do("vector", "tensor_scalar", s_g[:], ba_raw[0:16, :],
                      s_negea[:, 0:1], None, ALU.mult)
                kb.do("vector", "tensor_tensor_scan", s_c[:], s_rmask[:],
                      s_g[:], 0.0, ALU.mult, ALU.add)
                tcs = kb.mark("vector", "dv")
                kb.wait("scalar", tcs)
                kb.do("scalar", "activation", s_arow[:], s_c[:], AF.Exp)
                kb.mark("scalar", "ac")
                kb.barrier()
                kb.ckpt("p2")
                snap("qkv0", qkv[0], (128, TOK), BF16)
                snap("qkv8", qkv[8], (128, TOK), BF16)
                snap("qkv16", qkv[16], (128, TOK), BF16)
                snap("qkv31", qkv[31], (128, TOK), BF16)
                snap("beta", s_beta[:], (16, TOK), F32)
                snap("g", s_g[:], (16, TOK), F32)
                snap("arow", s_arow[:], (16, TOK), BF16)
                if debug:
                    dbg("beta", s_beta[:], (16, TOK), F32)
                    dbg("g", s_g[:], (16, TOK), F32)
                    dbg("c", s_c[:], (16, TOK), F32)
                    dbg("qkv0", qkv[0], (128, TOK), BF16)
                    dbg("qkv8", qkv[8], (128, TOK), BF16)
                    dbg("qkv16", qkv[16], (128, TOK), BF16)
                    dbg("qkv31", qkv[31], (128, TOK), BF16)

            for g in reversed(lnx_guards):
                g.__exit__(None, None, None)

            sel_guards = [nc.sbuf_tensor("s_selhf", [16, 16 * 128], F32),
                          nc.sbuf_tensor("s_selhb", [16, 16 * 128], BF16)]
            s_selhf, s_selhb = [g.__enter__() for g in sel_guards]
            kb.do("sync", "dma_start", s_selhf[:], selhf)
            kb.mark("sync", "wdma", 16)
            kb.do("sync", "dma_start", s_selhb[:], selhb)
            kb.mark("sync", "wdma", 16)
            tsel_ld = kb.tok("wdma")
            kb.wait_p16("tensor", tsel_ld)

            for i in range(NCH):
                conv_pump(2)
                cs = slice(i * C, (i + 1) * C)
                co = i * 16
                kb.psum_wait(2)
                kb.do("tensor", "matmul", pb[2][0:128, 0:16],
                      s_c[:, cs], s_i16f[:], start=True, stop=True)
                kb.do("tensor", "matmul", pb[2][0:128, 16:32],
                      s_beta[:, cs], s_i16f[:], start=True, stop=True)
                tpc = kb.mark("tensor", "pe")
                kb.wait("vector", tpc)
                kb.do("vector", "tensor_copy", s_cT[:, co:co + 16],
                      pb[2][0:128, 0:16])
                kb.do("vector", "tensor_copy", s_bT[:, co:co + 16],
                      pb[2][0:128, 16:32])
                kb.do("vector", "tensor_scalar_mul", s_bTn[:, co:co + 16],
                      pb[2][0:128, 16:32], -1.0)
                tvc = kb.mark("vector", "dv")
                kb.psum_consumed(2, tvc)
                kb.wait("scalar", tvc)
                kb.do("scalar", "activation", s_bA[:, co:co + 16],
                      s_cT[:, co:co + 16], AF.Exp)
                tac = kb.mark("scalar", "ac")
                kb.wait("vector", tac)
                kb.do("vector", "tensor_tensor", s_bA[:, co:co + 16],
                      s_bT[:, co:co + 16], s_bA[:, co:co + 16], ALU.mult)
                kb.wait("tensor", tvc)
                kb.psum_wait(3)
                kb.do("tensor", "matmul", pb[3][0:16, 0:1],
                      s_cT[:, co:co + 16], s_e127[:],
                      start=True, stop=True)
                tp3 = kb.mark("tensor", "pe")
                kb.wait("vector", tp3)
                kb.do("vector", "tensor_copy", s_col[0:16, 0:1],
                      pb[3][0:16, 0:1])
                tv3 = kb.mark("vector", "dv")
                kb.psum_consumed(3, tv3)
                kb.do("vector", "tensor_scalar", s_dg16[:],
                      s_i16f[:], s_col[0:16, 0:1], None, ALU.mult)
                tdg = kb.mark("vector", "dv")
                kb.wait("tensor", tdg)
                kb.psum_wait(3)
                kb.do("tensor", "matmul", pb[3][0:128, 0:16],
                      s_ones16f[:], s_dg16[:], start=True, stop=True)
                tp3b = kb.mark("tensor", "pe")
                kb.wait("scalar", tp3b)
                kb.do("scalar", "activation", s_acb[:, co:co + 16],
                      pb[3][0:128, 0:16], AF.Exp)
                ta3 = kb.mark("scalar", "ac")
                kb.wait("vector", tp3b)
                kb.do("vector", "tensor_tensor", s_kbar[:, co:co + 16],
                      pb[3][0:128, 0:16], s_cT[:, co:co + 16],
                      ALU.subtract)
                kb.do("vector", "tensor_single_scalar",
                      s_kbar[:, co:co + 16], s_kbar[:, co:co + 16],
                      0.0, ALU.min)
                tv4 = kb.mark("vector", "dv")
                kb.psum_consumed(3, ta3)
                kb.psum_consumed(3, tv4, add=True)
                kb.wait("scalar", tv4)
                kb.do("scalar", "activation", s_kbar[:, co:co + 16],
                      s_kbar[:, co:co + 16], AF.Exp)
                kb.mark("scalar", "ac")
            kb.barrier()
            kb.ckpt("p3")
            snap("cT", s_cT[:], (128, NCH * 16), F32)
            snap("kbar", s_kbar[:], (128, NCH * 16), F32)
            snap("acb", s_acb[:], (128, NCH * 16), F32)
            if debug:
                dbg("cT", s_cT[:], (128, NCH * 16), F32)
                dbg("kbar", s_kbar[:], (128, NCH * 16), F32)
                dbg("acb", s_acb[:], (128, NCH * 16), F32)

            with ExitStack() as sc4:
                E4 = sc4.enter_context
                s_nt = [E4(nc.sbuf_tensor(f"s_nt{k}", [128, 128], BF16))
                        for k in range(7)]
                s_np = [E4(nc.sbuf_tensor(f"s_np{k}", [128, 128], BF16))
                        for k in range(2)]
                s_gm = E4(nc.sbuf_tensor("s_gm", [128, 128], BF16))
                s_edN = E4(nc.sbuf_tensor("s_edN", [128, 128], BF16))
                s_edG = E4(nc.sbuf_tensor("s_edG", [128, 128], BF16))
                s_wkT = E4(nc.sbuf_tensor("s_wkT", [128, 128], BF16))
                s_wfl = E4(nc.sbuf_tensor("s_wfl", [128, 128], BF16))
                s_kb_ = E4(nc.sbuf_tensor("s_kb_", [128, 128], BF16))
                s_pan2 = [E4(nc.sbuf_tensor(f"s_pan2{k}", [128, 256],
                                            BF16)) for k in range(2)]
                panC = [E4(nc.sbuf_tensor(f"panC{k}", [128, HV * 256],
                                          BF16)) for k in range(2)]
                gtC = [E4(nc.sbuf_tensor(f"gtC{k}", [128, HV * 128],
                                         BF16)) for k in range(2)]
                ktC = [E4(nc.sbuf_tensor(f"ktC{k}", [128, HK * 128],
                                         BF16)) for k in range(2)]
                panC_free = [None, None]
                gtC_free = [None, None]
                ktC_free = [None, None]
                t_scr = None
                edf_prev = [None]

                def chunk_pass1(i):
                    nonlocal t_scr
                    pp = i & 1
                    cs = slice(i * C, (i + 1) * C)
                    co = i * 16
                    kb.wait("vector", panC_free[pp])
                    kb.wait("vector", gtC_free[pp])
                    kb.wait("vector", ktC_free[pp])
                    for h in range(HV):
                        conv_pump(1)
                        kq = h // 2
                        gbank = 3 + (h & 1)
                        kb.wait("tensor", t_scr)
                        kb.psum_wait(gbank)
                        kb.do("tensor", "matmul", pb[gbank][0:128, 0:128],
                              qkv[8 + kq][:, cs], qkv[8 + kq][:, cs],
                              start=True, stop=True)
                        kb.do("tensor", "matmul",
                              pb[gbank][0:128, 128:256],
                              qkv[8 + kq][:, cs], qkv[kq][:, cs],
                              start=True, stop=True)
                        kb.psum_wait(5)
                        kb.do("tensor", "matmul", pb[5][0:128, 0:128],
                              s_selhf[:, h * 128:(h + 1) * 128],
                              s_c[:, cs], start=True, stop=True)
                        tpg = kb.mark("tensor", "pe")
                        kb.wait("vector", tpg)
                        kb.wait("vector", edf_prev[0])
                        kb.do("vector", "tensor_scalar", s_edf[:],
                              pb[5][0:128, 0:128],
                              s_cT[:, co + h:co + h + 1], 0.0,
                              ALU.subtract, ALU.max)
                        tv1 = kb.mark("vector", "dv")
                        kb.do("vector", "tensor_scalar", s_edf2[:],
                              pb[5][0:128, 0:128],
                              s_cT[:, co + h:co + h + 1], 0.0,
                              ALU.subtract, ALU.min)
                        tv2 = kb.mark("vector", "dv")
                        kb.psum_consumed(5, tv2)
                        kb.wait("scalar", tv1)
                        kb.do("scalar", "activation", s_edN[:], s_edf[:],
                              AF.Exp, scale=s_cstc[0:128, 5:6])
                        te1 = kb.mark("scalar", "ac")
                        edf_prev[0] = te1
                        kb.wait("scalar", tv2)
                        kb.do("scalar", "activation", s_edG[:], s_edf2[:],
                              AF.Exp)
                        tEd = kb.mark("scalar", "ac")
                        kb.wait("vector", tEd)
                        kb.do("vector", "tensor_tensor", s_gm[:],
                              pb[gbank][0:128, 0:128], s_mlo[:], ALU.mult)
                        kb.do("vector", "scalar_tensor_tensor",
                              s_np[0][:], s_edN[:],
                              s_bTn[:, co + h:co + h + 1], s_gm[:],
                              ALU.mult, ALU.mult)
                        tN = kb.mark("vector", "dv")
                        kb.do("vector", "tensor_tensor", s_gm[:],
                              pb[gbank][0:128, 128:256], s_mue[:],
                              ALU.mult)
                        kb.do("vector", "tensor_tensor",
                              gtC[pp][:, h * 128:(h + 1) * 128],
                              s_gm[:], s_edG[:], ALU.mult)
                        tGt = kb.mark("vector", "dv")
                        kb.psum_consumed(gbank, tGt)
                        kb.wait("tensor", tN)
                        kb.psum_wait(6)
                        kb.do("tensor", "matmul", pb[6][0:128, 0:128],
                              s_np[0][:], s_i128b[:],
                              start=True, stop=True)
                        tpt = kb.mark("tensor", "pe")
                        kb.wait("scalar", tpt)
                        kb.do("scalar", "copy", s_nt[0][:],
                              pb[6][0:128, 0:128])
                        tnt = kb.mark("scalar", "ac")
                        kb.psum_consumed(6, tnt)
                        if h % 2 == 0:
                            kb.wait("tensor", ktC_free[pp])
                            kb.psum_wait(2)
                            kb.do("tensor", "matmul",
                                  pb[2][0:128, 0:128],
                                  qkv[8 + kq][:, cs], s_i128b[:],
                                  start=True, stop=True)
                            tkt = kb.mark("tensor", "pe")
                            kb.wait("vector", tkt)
                            kb.do("vector", "tensor_copy",
                                  ktC[pp][:, kq * 128:(kq + 1) * 128],
                                  pb[2][0:128, 0:128])
                            tke = kb.mark("vector", "dv")
                            kb.psum_consumed(2, tke)
                        kb.psum_wait(2)
                        kb.do("tensor", "matmul", pb[2][0:128, 0:128],
                              qkv[16 + h][:, cs], s_i128b[:],
                              start=True, stop=True)
                        tvt = kb.mark("tensor", "pe")
                        kb.wait("vector", tvt)
                        kb.do("vector", "tensor_scalar",
                              panC[pp][:, h * 256:h * 256 + 128],
                              pb[2][0:128, 0:128],
                              s_bT[:, co + h:co + h + 1], None, ALU.mult)
                        tuv = kb.mark("vector", "dv")
                        kb.psum_consumed(2, tuv)
                        kb.do("vector", "tensor_scalar",
                              panC[pp][:, h * 256 + 128:h * 256 + 256],
                              ktC[pp][:, kq * 128:(kq + 1) * 128],
                              s_bA[:, co + h:co + h + 1], None, ALU.mult)
                        tuk = kb.mark("vector", "dv")
                        if debug and i == 0 and h == 0:
                            dbg("pan_pre", panC[pp][:, 0:256], (128, 256),
                                BF16)
                        cur = 0
                        tnp, tns = tN, tnt
                        for k in range(1, 7):
                            kb.wait("tensor", tns)
                            kb.wait("tensor", tnp)
                            kb.psum_wait(6)
                            kb.do("tensor", "matmul",
                                  pb[6][0:128, 0:128],
                                  s_nt[k - 1][:], s_np[cur][:],
                                  start=True, stop=True)
                            kb.psum_wait(5)
                            kb.do("tensor", "matmul",
                                  pb[5][0:128, 0:128],
                                  s_np[cur][:], s_nt[k - 1][:],
                                  start=True, stop=True)
                            tsq_ = kb.mark("tensor", "pe")
                            kb.wait("vector", tsq_)
                            kb.do("vector", "tensor_copy",
                                  s_np[1 - cur][:], pb[6][0:128, 0:128])
                            tnp = kb.mark("vector", "dv")
                            kb.psum_consumed(6, tnp)
                            kb.wait("scalar", tsq_)
                            kb.do("scalar", "copy", s_nt[k][:],
                                  pb[5][0:128, 0:128])
                            tns = kb.mark("scalar", "ac")
                            kb.psum_consumed(5, tns)
                            cur = 1 - cur
                        src = panC[pp][:, h * 256:h * 256 + 256]
                        kb.wait("tensor", tuv)
                        kb.wait("tensor", tuk)
                        kb.wait("tensor", tns)
                        tev = None
                        for ai, k in enumerate(range(6, -1, -1)):
                            bank = 5 if (ai & 1) == 0 else 6
                            dst = (s_pan2[ai & 1][:] if ai < 6 else
                                   panC[pp][:, h * 256:h * 256 + 256])
                            kb.wait("tensor", tev)
                            kb.psum_wait(bank)
                            kb.do("tensor", "matmul",
                                  pb[bank][0:128, 0:256], s_nt[k][:],
                                  src, start=True, stop=True)
                            tap = kb.mark("tensor", "pe")
                            kb.wait("vector", tap)
                            kb.do("vector", "tensor_tensor",
                                  dst, pb[bank][0:128, 0:256], src, ALU.add)
                            tev = kb.mark("vector", "dv")
                            kb.psum_consumed(bank, tev)
                            src = dst
                        t_scr = kb.tok("ac")
                        if debug and i == 0 and h == 0:
                            dbg("edN", s_edN[:], (128, 128), BF16)
                            dbg("edG", s_edG[:], (128, 128), BF16)
                            dbg("np0", s_np[0][:], (128, 128), BF16)
                            dbg("nt0", s_nt[0][:], (128, 128), BF16)
                            dbg("nt6", s_nt[6][:], (128, 128), BF16)
                            dbg("ktok0", ktC[pp][:, 0:128], (128, 128), BF16)
                            dbg("gt0", gtC[pp][:, 0:128], (128, 128), BF16)
                            dbg("pan_post", panC[pp][:, 0:256],
                                (128, 256), BF16)
                    for h in range(HV):
                        kq = h // 2
                        sl = slice(h * 128, (h + 1) * 128)
                        rb = i & 1
                        wbuf = 1 - rb
                        kb.psum_wait(2)
                        kb.do("tensor", "matmul", pb[2][0:128, 0:128],
                              panC[pp][:, h * 256 + 128:h * 256 + 256],
                              s_i128b[:], start=True, stop=True)
                        tw1 = kb.mark("tensor", "pe")
                        kb.wait("vector", tw1)
                        kb.do("vector", "tensor_copy", s_wkT[:],
                              pb[2][0:128, 0:128])
                        tw2 = kb.mark("vector", "dv")
                        kb.psum_consumed(2, tw2)
                        kb.wait("tensor", tw2)
                        kb.psum_wait(6)
                        kb.do("tensor", "matmul", pb[6][0:128, 0:128],
                              s_wkT[:], s_state[rb][:, sl],
                              start=True, stop=True)
                        tw3 = kb.mark("tensor", "pe")
                        kb.wait("vector", tw3)
                        kb.do("vector", "tensor_tensor", s_wfl[:],
                              panC[pp][:, h * 256:h * 256 + 128],
                              pb[6][0:128, 0:128], ALU.subtract)
                        kb.do("vector", "tensor_scalar", s_kb_[:],
                              ktC[pp][:, kq * 128:(kq + 1) * 128],
                              s_kbar[:, co + h:co + h + 1], None,
                              ALU.mult)
                        tw4 = kb.mark("vector", "dv")
                        kb.psum_consumed(6, tw4)
                        kb.wait("tensor", tw4)
                        kb.psum_wait(5)
                        kb.do("tensor", "matmul", pb[5][0:128, 0:128],
                              s_kb_[:], s_wfl[:], start=True, stop=True)
                        tw5 = kb.mark("tensor", "pe")
                        kb.wait("vector", tw5)
                        kb.do("vector", "scalar_tensor_tensor",
                              s_state[wbuf][:, sl], s_state[rb][:, sl],
                              s_acb[:, co + h:co + h + 1],
                              pb[5][0:128, 0:128], ALU.mult, ALU.add)
                        tw6 = kb.mark("vector", "dv")
                        kb.psum_consumed(5, tw6)
                    tsp = kb.tok("dv")
                    tss = kb.tok("ac")
                    kb.wait("gpsimd", tsp)
                    kb.wait("gpsimd", tss)
                    kb.do("gpsimd", "dma_start",
                          pan_hbm[i],
                          panC[pp][:])
                    panC_free[pp] = kb.mark("gpsimd", "gdma", 16)
                    kb.do("gpsimd", "dma_start",
                          gt_hbm[i],
                          gtC[pp][:])
                    gtC_free[pp] = kb.mark("gpsimd", "gdma", 16)
                    kb.do("gpsimd", "dma_start",
                          kt_hbm[i],
                          ktC[pp][:])
                    ktC_free[pp] = kb.mark("gpsimd", "gdma", 16)

                for i in range(NCH):
                    chunk_pass1(i)
                kb.barrier()
                kb.ckpt("p4")
                snap("state_p1", s_state[NCH & 1][:], (128, HV * DV), BF16)
                snap("pan0", pan_hbm[0], (128, HV * 256), BF16)
                snap("gwbf0", gw_bf[0, 0], (128, 1024), BF16)
                snap("dnbf47", dn_bf[NI - 1, 0], (128, 1024), BF16)
                snap("gw_all", gw_bf.rearrange("a h p m -> (a h p) m"),
                     (NI * 2 * 128, 1024), BF16)
                snap("uw_all", uw_bf.rearrange("a h p m -> (a h p) m"),
                     (NI * 2 * 128, 1024), BF16)
                snap("dn_all", dn_bf.rearrange("a h p m -> (a h p) m"),
                     (NI * 2 * 128, 1024), BF16)
                if debug:
                    dbg("state_p1",
                        s_state[NCH & 1][:], (128, HV * DV), BF16)

            fin = NCH & 1
            kb.do("gpsimd", "dma_start", cc_in, s_state[fin][:])
            kb.mark("gpsimd", "gdma", 16)
            kb.wait("gpsimd", kb.tok("gdma"))
            kb.ckpt("c1")
            kb.do("gpsimd", "collective_compute",
                  "AllGather", ALU.bypass, rg, [cc_in], [cc_out])
            tcoll = kb.mark("gpsimd", "Bg", 1)
            kb.ckpt("c2")
            with nc.sbuf_tensor("s_recv", [128, 8 * HV * DV], BF16) \
                    as s_recv:
                kb.wait("gpsimd", tcoll)
                ccv = cc_out.rearrange("(r p) f -> r p f", p=128)
                for r in range(8):
                    kb.do("gpsimd", "dma_start",
                          s_recv[:, r * 2048:(r + 1) * 2048], ccv[r])
                    kb.mark("gpsimd", "gdma", 16)
                trcv = kb.tok("gdma")
                kb.wait_p16("vector", trcv)
                kb.do("vector", "tensor_scalar", s_state[0][:],
                      s_recv[:, 0:2048], s_selcol[:, 0:1], None, ALU.mult)
                for r in range(1, 8):
                    kb.do("vector", "scalar_tensor_tensor",
                          s_state[0][:],
                          s_recv[:, r * 2048:(r + 1) * 2048],
                          s_selcol[:, r:r + 1], s_state[0][:],
                          ALU.mult, ALU.add)
                tsel = kb.mark("vector", "dv")
            kb.barrier()
            kb.ckpt("coll")
            snap("state_init", s_state[0][:], (128, HV * DV), BF16)
            if debug:
                dbg("state_init", s_state[0][:], (128, HV * DV), BF16)

            with ExitStack() as sc5:
                E5 = sc5.enter_context
                panI = [E5(nc.sbuf_tensor(f"panI{k}", [128, HV * 256],
                                          BF16)) for k in range(2)]
                gtI = [E5(nc.sbuf_tensor(f"gtI{k}", [128, HV * 128],
                                         BF16)) for k in range(2)]
                ktI = [E5(nc.sbuf_tensor(f"ktI{k}", [128, HK * 128],
                                         BF16)) for k in range(2)]
                s_wkT = E5(nc.sbuf_tensor("s_wkT5", [128, 128], BF16))
                s_wfl = E5(nc.sbuf_tensor("s_wfl5", [128, 128], BF16))
                s_kb_ = E5(nc.sbuf_tensor("s_kb5", [128, 128], BF16))
                s_qt = E5(nc.sbuf_tensor("s_qt5", [128, 128], BF16))
                ld_free = [None, None]

                def load_chunk(i):
                    pp = i & 1
                    kb.wait("sync", ld_free[pp])
                    kb.do("sync", "dma_start", panI[pp][:],
                          pan_hbm[i])
                    kb.mark("sync", "wdma", 16)
                    kb.do("sync", "dma_start", gtI[pp][:],
                          gt_hbm[i])
                    kb.mark("sync", "wdma", 16)
                    kb.do("sync", "dma_start", ktI[pp][:],
                          kt_hbm[i])
                    kb.mark("sync", "wdma", 16)
                    return kb.tok("wdma")

                tld = [None] * NCH
                tld[0] = load_chunk(0)
                for i in range(NCH):
                    pp = i & 1
                    cs = slice(i * C, (i + 1) * C)
                    co = i * 16
                    if i + 1 < NCH:
                        tld[i + 1] = load_chunk(i + 1)
                        kb.wait_p16("tensor", tld[i])
                    else:
                        kb.do("sync", "dma_start", ktI[pp][:], kt_hbm[i])
                        kb.mark("sync", "wdma", 16)
                        kb.wait("tensor", kb.tok("wdma"))
                    kb.wait("vector", tld[i])
                    for h in range(HV):
                        kq = h // 2
                        sl = slice(h * 128, (h + 1) * 128)
                        rb = i & 1
                        wbuf = 1 - rb
                        kb.psum_wait(4)
                        kb.do("tensor", "matmul", pb[4][:, 0:128],
                              s_selhb[:, h * 128:(h + 1) * 128],
                              s_arow[:, cs], start=True, stop=True)
                        tq1 = kb.mark("tensor", "pe")
                        kb.wait("vector", tq1)
                        kb.do("vector", "tensor_tensor", s_qt[:],
                              qkv[kq][:, cs], pb[4][:, 0:128], ALU.mult)
                        tq2 = kb.mark("vector", "dv")
                        kb.psum_consumed(4, tq2)
                        kb.psum_wait(2)
                        kb.do("tensor", "matmul", pb[2][0:128, 0:128],
                              panI[pp][:, h * 256 + 128:h * 256 + 256],
                              s_i128b[:], start=True, stop=True)
                        tw1 = kb.mark("tensor", "pe")
                        kb.wait("vector", tw1)
                        kb.do("vector", "tensor_copy", s_wkT[:],
                              pb[2][0:128, 0:128])
                        tw2 = kb.mark("vector", "dv")
                        kb.psum_consumed(2, tw2)
                        kb.wait("tensor", tw2)
                        kb.psum_wait(6)
                        kb.do("tensor", "matmul", pb[6][0:128, 0:128],
                              s_wkT[:], s_state[rb][:, sl],
                              start=True, stop=True)
                        tw3 = kb.mark("tensor", "pe")
                        kb.wait("vector", tw3)
                        kb.do("vector", "tensor_tensor", s_wfl[:],
                              panI[pp][:, h * 256:h * 256 + 128],
                              pb[6][0:128, 0:128], ALU.subtract)
                        kb.do("vector", "tensor_scalar", s_kb_[:],
                              ktI[pp][:, kq * 128:(kq + 1) * 128],
                              s_kbar[:, co + h:co + h + 1], None,
                              ALU.mult)
                        tw4 = kb.mark("vector", "dv")
                        kb.psum_consumed(6, tw4)
                        kb.wait("tensor", tq2)
                        kb.wait("tensor", tw4)
                        kb.psum_wait(3)
                        kb.do("tensor", "matmul", pb[3][0:128, 0:128],
                              s_state[rb][:, sl], s_qt[:],
                              start=True, stop=False)
                        kb.do("tensor", "matmul", pb[3][0:128, 0:128],
                              s_wfl[:],
                              gtI[pp][:, h * 128:(h + 1) * 128],
                              start=False, stop=True)
                        tO = kb.mark("tensor", "pe")
                        kb.wait("scalar", tO)
                        kb.do("scalar", "copy", core[h][:, cs],
                              pb[3][0:128, 0:128])
                        tOe = kb.mark("scalar", "ac")
                        kb.psum_consumed(3, tOe)
                        kb.psum_wait(5)
                        kb.do("tensor", "matmul", pb[5][0:128, 0:128],
                              s_kb_[:], s_wfl[:], start=True, stop=True)
                        tw5 = kb.mark("tensor", "pe")
                        kb.wait("vector", tw5)
                        kb.do("vector", "scalar_tensor_tensor",
                              s_state[wbuf][:, sl], s_state[rb][:, sl],
                              s_acb[:, co + h:co + h + 1],
                              pb[5][0:128, 0:128], ALU.mult, ALU.add)
                        tw6 = kb.mark("vector", "dv")
                        kb.psum_consumed(5, tw6)
                    ld_free[pp] = kb.tok("pe")
                kb.barrier()
                kb.ckpt("p5")
                snap("core_raw0", core[0][:], (128, TOK), BF16)
                snap("core_raw15", core[15][:], (128, TOK), BF16)
            if debug:
                dbg("core_raw0", core[0][:], (128, TOK), BF16)
                dbg("core_raw15", core[15][:], (128, TOK), BF16)

            for g in reversed(sel_guards):
                g.__exit__(None, None, None)

            with nc.sbuf_tensor("zin0", [128, TOK], BF16) as zin0, \
                 nc.sbuf_tensor("zin1", [128, TOK], BF16) as zin1, \
                 nc.sbuf_tensor("gsc", [128, TOK], BF16) as gsc:
                zin = [zin0, zin1]
                zfree = [None, None]
                gsc_free = [None]
                rows6_free = [None]
                kb.do("sync", "dma_start", zin[0][:],
                      z_hbm[0])
                kb.mark("sync", "wdma", 16)
                tz = [None] * HV
                tz[0] = kb.tok("wdma")
                for h in range(HV):
                    if h + 1 < HV:
                        kb.wait("sync", zfree[(h + 1) & 1])
                        kb.do("sync", "dma_start", zin[(h + 1) & 1][:],
                              z_hbm[h + 1])
                        kb.mark("sync", "wdma", 16)
                        tz[h + 1] = kb.tok("wdma")
                    kb.wait("scalar", gsc_free[0])
                    kb.do("scalar", "activation", gsc[:], core[h][:],
                          AF.Square)
                    tq = kb.mark("scalar", "ac")
                    kb.wait("tensor", tq)
                    kb.psum_wait(6)
                    kb.do("tensor", "matmul", pb[6][0:1, 0:512],
                          s_onesb[:], gsc[:, 0:512],
                          start=True, stop=True)
                    kb.psum_wait(5)
                    kb.do("tensor", "matmul", pb[5][0:1, 0:512],
                          s_onesb[:], gsc[:, 512:1024],
                          start=True, stop=True)
                    tsm = kb.mark("tensor", "pe")
                    kb.wait("scalar", tsm)
                    kb.wait("scalar", rows6_free[0])
                    kb.do("scalar", "activation", s_rows[0:1, 0:512],
                          pb[6][0:1, 0:512], AF.Sqrt,
                          scale=s_cst[0:1, 7:8], bias=s_cst[0:1, 8:9])
                    kb.do("scalar", "activation", s_rows[0:1, 512:1024],
                          pb[5][0:1, 0:512], AF.Sqrt,
                          scale=s_cst[0:1, 7:8], bias=s_cst[0:1, 8:9])
                    tsr = kb.mark("scalar", "ac")
                    kb.psum_consumed(6, tsr)
                    kb.psum_consumed(5, tsr)
                    kb.wait("vector", tsr)
                    kb.do("vector", "reciprocal", s_rows[0:1, :],
                          s_rows[0:1, :])
                    kb.do("vector", "tensor_copy", s_rowb[0:1, :],
                          s_rows[0:1, :])
                    trc = kb.mark("vector", "dv")
                    rows6_free[0] = trc
                    kb.wait("tensor", trc)
                    kb.psum_wait(6)
                    kb.do("tensor", "matmul", pb[6][:, 0:512],
                          s_onesrb[:], s_rowb[0:1, 0:512],
                          start=True, stop=True)
                    kb.psum_wait(5)
                    kb.do("tensor", "matmul", pb[5][:, 0:512],
                          s_onesrb[:], s_rowb[0:1, 512:1024],
                          start=True, stop=True)
                    tbc = kb.mark("tensor", "pe")
                    kb.wait("vector", tbc)
                    kb.do("vector", "tensor_tensor", gsc[:, 0:512],
                          core[h][:, 0:512], pb[6][:, 0:512], ALU.mult)
                    kb.do("vector", "tensor_tensor", gsc[:, 512:1024],
                          core[h][:, 512:1024], pb[5][:, 0:512], ALU.mult)
                    tgg = kb.mark("vector", "dv")
                    kb.psum_consumed(6, tgg)
                    kb.psum_consumed(5, tgg)
                    kb.wait_p16("vector", tz[h])
                    kb.do("vector", "scalar_tensor_tensor", gsc[:],
                          gsc[:], s_normw[:, 0:1], zin[h & 1][:],
                          ALU.mult, ALU.mult)
                    kb.do("vector", "tensor_copy", core[h][:], gsc[:])
                    zfree[h & 1] = kb.mark("vector", "dv")
                    gsc_free[0] = zfree[h & 1]
                kb.barrier()
                kb.ckpt("p6")
                snap("core_g0", core[0][:], (128, TOK), BF16)
            if debug:
                dbg("core_g0", core[0][:], (128, TOK), BF16)


        with ExitStack() as sc7:
            E7 = sc7.enter_context
            att_guards = [nc.sbuf_tensor(f"att{j}", [128, TOK], BF16)
                          for j in range(16)]
            att = [g.__enter__() for g in att_guards]
            wtok2 = load_wblk(wout_b[0], 0, False, 0)
            for o in range(16):
                widx = o & 1
                nxt = None
                if o + 1 < 16:
                    nxt = load_wblk(
                        wout_b[o + 1], (o + 1) & 1, False, (o + 1) & 1)
                kb.wait("tensor", wtok2)
                for si in range(2):
                    bank = (2 * o + si) % 2
                    kb.psum_wait(bank)
                    for kk in range(16):
                        kb.do("tensor", "matmul", pb[bank][0:128, 0:512],
                              wb[widx][:, kk * 128:(kk + 1) * 128],
                              core[kk][:, si * 512:(si + 1) * 512],
                              start=(kk == 0), stop=(kk == 15))
                    tpp = kb.mark("tensor", "pe")
                    wb_free[widx] = tpp
                    kb.wait("vector", tpp)
                    kb.do("vector", "tensor_scalar",
                          att[o][:, si * 512:(si + 1) * 512],
                          pb[bank][0:128, 0:512],
                          s_boutc[:, o:o + 1], None, ALU.add)
                    te = kb.mark("vector", "dv")
                    kb.psum_consumed(bank, te)
                wtok2 = nxt
            conv_pump(len(conv_seq))
            kb.barrier()
            kb.ckpt("p7a")
            snap("att0", att[0][:], (128, TOK), BF16)
            snap("att15", att[15][:], (128, TOK), BF16)
            if debug:
                dbg("att0", att[0][:], (128, TOK), BF16)

            with nc.sbuf_tensor("atok", [128, HID], BF16) as atok, \
                 nc.sbuf_tensor("xt", [128, HID], F32) as xt, \
                 nc.sbuf_tensor("x1t", [128, HID], F32) as x1t, \
                 nc.sbuf_tensor("l3t", [128, HID], BF16) as l3t, \
                 nc.sbuf_tensor("l3s", [128, HID], BF16) as l3s, \
                 nc.sbuf_tensor("st7", [128, 8], F32) as st7:
                xt_free = None
                atok_free = None
                x1t_free = []
                l3t_free = None
                l3s_free = None
                for t in range(NTT):
                    kb.wait("sync", xt_free)
                    kb.do("sync", "dma_start", xt[:],
                          x_main[t * TT:(t + 1) * TT, :])
                    kb.mark("sync", "wdma", 16)
                    kb.do("sync", "dma_start", xt[:],
                          x_main[t * TT:(t + 1) * TT, :])
                    kb.mark("sync", "wdma", 16)
                    txx = kb.tok("wdma")
                    kb.wait("scalar", atok_free)
                    for j in range(16):
                        bank = 2 + (j & 1)
                        kb.psum_wait(bank)
                        kb.do("tensor", "matmul", pb[bank][0:128, 0:128],
                              att[j][:, t * TT:(t + 1) * TT], s_i128b[:],
                              start=True, stop=True)
                        tpj = kb.mark("tensor", "pe")
                        kb.wait("scalar", tpj)
                        kb.do("scalar", "copy",
                              atok[:, j * 128:(j + 1) * 128],
                              pb[bank][0:128, 0:128])
                        taj = kb.mark("scalar", "ac")
                        kb.psum_consumed(bank, taj)
                    tat = kb.tok("ac")
                    kb.wait("vector", tat)
                    kb.wait("vector", txx)
                    for tkn in x1t_free:
                        kb.wait("vector", tkn)
                    tl2 = ln_apply(atok, 128, None, st7,
                                   add_ap=xt[:], out_f32=x1t[:])
                    kb.wait("gpsimd", tl2)
                    kb.do("gpsimd", "dma_start",
                          x1_hbm[t * TT:(t + 1) * TT, :], x1t[:])
                    tx1d = kb.mark("gpsimd", "gdma", 16)
                    xt_free = tl2
                    atok_free = tl2
                    kb.wait("vector", l3t_free)
                    tl3 = ln_apply(x1t, 128, l3t[:], st7)
                    x1t_free = [tx1d, kb.tok("ac"), tl3]
                    kb.wait("scalar", l3s_free)
                    for j in range(16):
                        bank = 2 + (j & 1)
                        kb.wait("tensor", tl3)
                        kb.psum_wait(bank)
                        kb.do("tensor", "matmul", pb[bank][0:128, 0:128],
                              l3t[:, j * 128:(j + 1) * 128], s_i128b[:],
                              start=True, stop=True)
                        tpj = kb.mark("tensor", "pe")
                        kb.wait("scalar", tpj)
                        kb.do("scalar", "copy",
                              l3s[:, j * 128:(j + 1) * 128],
                              pb[bank][0:128, 0:128])
                        taj = kb.mark("scalar", "ac")
                        kb.psum_consumed(bank, taj)
                    l3t_free = kb.tok("pe")
                    tl3s = kb.tok("ac")
                    kb.wait("gpsimd", tl3s)
                    for j in range(16):
                        kb.do("gpsimd", "dma_start",
                              ln3_hbm[t // 4, j][:, (t % 4) * 128:
                                                 (t % 4) * 128 + 128],
                              l3s[:, j * 128:(j + 1) * 128])
                        kb.mark("gpsimd", "gdma", 16)
                    l3s_free = kb.tok("gdma")
            kb.barrier()
            snap("x1a", x1_hbm[0:128, :], (128, HID), F32)
            snap("x1b", x1_hbm[TOK - 128:TOK, :], (128, HID), F32)
            snap("ln3a", ln3_hbm[0, 0], (128, 512), BF16)

            if debug:
                dx1 = nc.dram_tensor("dbg_x1", [TOK, HID], F32,
                                     kind="ExternalOutput").ap()
                kb.barrier()
                kb.do("gpsimd", "dma_start", dx1, x1_hbm)
                kb.mark("gpsimd", "gdma", 16)
                dl3 = nc.dram_tensor("dbg_ln3", [128, 512], BF16,
                                     kind="ExternalOutput").ap()
                kb.do("gpsimd", "dma_start", dl3, ln3_hbm[0, 0])
                kb.mark("gpsimd", "gdma", 16)
                kb.barrier()
            for g in reversed(att_guards):
                g.__exit__(None, None, None)
            qkv_guard.__exit__(None, None, None)

            with ExitStack() as sc8:
                E8 = sc8.enter_context
                fuse = E8(nc.sbuf_tensor("fuse", [128, NI * 512], BF16))
                l3in = E8(nc.sbuf_tensor("l3in", [128, 16 * 512], BF16))
                gact = [E8(nc.sbuf_tensor(f"gact{k}", [128, 512], BF16))
                        for k in range(2)]
                ffncm = E8(nc.sbuf_tensor("ffncm", [128, 16 * 512], BF16))
                gub = [E8(nc.sbuf_tensor(f"gub{k}", [128, 2048], BF16))
                       for k in range(4)]
                dnb = [E8(nc.sbuf_tensor(f"dnb{k}", [128, 2048], BF16))
                       for k in range(2)]
                ftok = E8(nc.sbuf_tensor("ftok", [128, HID], BF16))
                x1r = E8(nc.sbuf_tensor("x1r", [128, HID], F32))
                l4t = E8(nc.sbuf_tensor("l4t", [128, HID], F32))
                st8 = E8(nc.sbuf_tensor("st8", [128, 8], F32))
                gact_free = [None, None]
                gub_free = [None] * 4
                dnb_free = [None] * 2
                fuse_free = None
                ffncm_free = None
                ftok_free = None
                x1r_free = None
                l4t_free = None

                def load_gu(ib, which):
                    bi = 2 * which + (ib & 1)
                    kb.wait("sync", gub_free[bi])
                    w = gw_bf if which == 0 else uw_bf
                    for hh in range(2):
                        kb.do("sync", "dma_start",
                              gub[bi][:, hh * 1024:(hh + 1) * 1024],
                              w[ib, hh])
                        kb.mark("sync", "wdma", 16)
                    return kb.tok("wdma")

                def load_dn(ib, slot):
                    kb.wait("sync", dnb_free[slot])
                    for hh in range(2):
                        kb.do("sync", "dma_start",
                              dnb[slot][:, hh * 1024:(hh + 1) * 1024],
                              dn_bf[ib, hh])
                        kb.mark("sync", "wdma", 16)
                    return kb.tok("wdma")

                l3in_free = None
                for half in range(2):
                    kb.wait("sync", l3in_free)
                    for kk in range(16):
                        kb.do("sync", "dma_start",
                              l3in[:, kk * 512:(kk + 1) * 512],
                              ln3_hbm[half, kk])
                        kb.mark("sync", "wdma", 16)
                    tl3in = kb.tok("wdma")
                    kb.wait_p16("tensor", tl3in)
                    tg_tok = load_gu(0, 0)
                    tu_tok = load_gu(0, 1)
                    kb.wait("vector", fuse_free)
                    for ib in range(NI):
                        gb = ib & 1
                        ub = 2 + (ib & 1)
                        kb.wait_p16("tensor", tg_tok)
                        bank = ib % 2
                        kb.psum_wait(bank)
                        for kk in range(16):
                            kb.do("tensor", "matmul", pb[bank][0:128, 0:512],
                                  gub[gb][:, kk * 128:(kk + 1) * 128],
                                  l3in[:, kk * 512:(kk + 1) * 512],
                                  start=(kk == 0), stop=(kk == 15))
                        tg1 = kb.mark("tensor", "pe")
                        gub_free[gb] = tg1
                        if ib + 1 < NI:
                            tg_tok = load_gu(ib + 1, 0)
                        kb.wait("scalar", tg1)
                        kb.wait("scalar", gact_free[ib & 1])
                        kb.do("scalar", "activation", gact[ib & 1][:],
                              pb[bank][0:128, 0:512], AF.Silu,
                              scale=s_gsc[:, ib:ib + 1])
                        tg2 = kb.mark("scalar", "ac")
                        kb.psum_consumed(bank, tg2)
                        kb.wait_p16("tensor", tu_tok)
                        bank2 = 2 + (ib % 2)
                        kb.psum_wait(bank2)
                        for kk in range(16):
                            kb.do("tensor", "matmul",
                                  pb[bank2][0:128, 0:512],
                                  gub[ub][:, kk * 128:(kk + 1) * 128],
                                  l3in[:, kk * 512:(kk + 1) * 512],
                                  start=(kk == 0), stop=(kk == 15))
                        tu1 = kb.mark("tensor", "pe")
                        gub_free[ub] = tu1
                        if ib + 1 < NI:
                            tu_tok = load_gu(ib + 1, 1)
                        kb.wait("vector", tu1)
                        kb.wait("vector", tg2)
                        kb.do("vector", "scalar_tensor_tensor",
                              fuse[:, ib * 512:(ib + 1) * 512],
                              pb[bank2][0:128, 0:512],
                              s_usc[:, ib:ib + 1], gact[ib & 1][:],
                              ALU.mult, ALU.mult)
                        tf = kb.mark("vector", "dv")
                        gact_free[ib & 1] = tf
                        kb.psum_consumed(bank2, tf)
                    tfuse = kb.tok("dv")
                    l3in_free = kb.tok("pe")
                    if debug2:
                        kb.wait("gpsimd", tfuse)
                        snap(f"fuse_h{half}", fuse[:, 0:4096], (128, 4096),
                             BF16)
                        snap(f"l3in_h{half}", l3in[:], (128, 16 * 512), BF16)
                    kb.wait("tensor", tfuse)
                    tdn = [None] * 2
                    for s in range(2):
                        tdn[s] = load_dn(s, s)
                    kb.wait("vector", ffncm_free)
                    for gi in range(2 * NI):
                        og, ib = divmod(gi, NI)
                        slot = gi & 1
                        if ib == 0:
                            for oo in range(8):
                                kb.psum_wait(oo)
                        kb.wait_p16("tensor", tdn[slot])
                        for oo in range(8):
                            o = og * 8 + oo
                            kb.do("tensor", "matmul",
                                  pb[oo][0:128, 0:512],
                                  dnb[slot][:, o * 128:(o + 1) * 128],
                                  fuse[:, ib * 512:(ib + 1) * 512],
                                  start=(ib == 0), stop=(ib == NI - 1))
                        tdd = kb.mark("tensor", "pe")
                        dnb_free[slot] = tdd
                        if gi + 2 < 2 * NI:
                            tdn[slot] = load_dn((gi + 2) % NI, slot)
                        if ib == NI - 1:
                            tdall = kb.tok("pe")
                            kb.wait("vector", tdall)
                            for oo in range(8):
                                o = og * 8 + oo
                                kb.do("vector", "tensor_scalar",
                                      ffncm[:, o * 512:(o + 1) * 512],
                                      pb[oo][0:128, 0:512],
                                      s_dsc[:, o:o + 1], None, ALU.mult)
                                tev = kb.mark("vector", "dv")
                                kb.psum_consumed(oo, tev)
                    tffn = kb.tok("dv")
                    fuse_free = kb.tok("pe")
                    if debug2:
                        kb.wait("gpsimd", tffn)
                        snap(f"ffncm_h{half}", ffncm[:], (128, 16 * 512),
                             BF16)
                    if debug and half == 0:
                        dbg("fuse0", fuse[:, 0:512], (128, 512), BF16)
                        dbg("ffncm0", ffncm[:, 0:512], (128, 512), BF16)
                        dbg("l3in0", l3in[:, 0:512], (128, 512), BF16)
                    for sub in range(4):
                        tglob = half * 512 + sub * 128
                        kb.wait("sync", x1r_free)
                        kb.do("sync", "dma_start", x1r[:],
                              x1_hbm[tglob:tglob + 128, :])
                        kb.mark("sync", "wdma", 16)
                        kb.do("sync", "dma_start", x1r[:],
                              x1_hbm[tglob:tglob + 128, :])
                        kb.mark("sync", "wdma", 16)
                        tx1 = kb.tok("wdma")
                        kb.wait("scalar", ftok_free)
                        for j in range(16):
                            bank = 6 + (j & 1)
                            kb.wait("tensor", tffn)
                            kb.psum_wait(bank)
                            kb.do("tensor", "matmul",
                                  pb[bank][0:128, 0:128],
                                  ffncm[:, j * 512 + sub * 128:
                                        j * 512 + sub * 128 + 128],
                                  s_i128b[:], start=True, stop=True)
                            tpj = kb.mark("tensor", "pe")
                            kb.wait("scalar", tpj)
                            kb.do("scalar", "copy",
                                  ftok[:, j * 128:(j + 1) * 128],
                                  pb[bank][0:128, 0:128])
                            taj = kb.mark("scalar", "ac")
                            kb.psum_consumed(bank, taj)
                        tft = kb.tok("ac")
                        kb.wait("vector", tft)
                        kb.wait("vector", tx1)
                        kb.wait("vector", l4t_free)
                        tl4 = ln_apply(ftok, 128, None, st8,
                                       add_ap=x1r[:], out_f32=l4t[:])
                        ftok_free = tl4
                        x1r_free = tl4
                        kb.wait("gpsimd", tl4)
                        kb.do("gpsimd", "dma_start",
                              out_x[tglob:tglob + 128, :], l4t[:])
                        l4t_free = kb.mark("gpsimd", "gdma", 16)
                    ffncm_free = kb.tok("pe")
                kb.barrier()
        kb.barrier()
        kb.ckpt("full")
        if cut is not None:
            kb.cut_at(cut)
        kb.wait("gpsimd", kb.tok("gdma"))

        if kb_sink is not None:
            kb_sink.append(kb)
        with nc.Block() as block:
            kb.emit(block)
    return nc


_NC_CACHE = {}


def _get_nc(debug=False):
    key = bool(debug)
    if key not in _NC_CACHE:
        _NC_CACHE[key] = build_nc(debug=key)
    return _NC_CACHE[key]


def _host_inputs(inp):
    f32 = np.float32
    bf16 = ml_dtypes.bfloat16
    x = np.ascontiguousarray(inp["L0_attn_hidden_states"], dtype=f32)
    Wqkv = np.asarray(inp["L0_attn_Wqkv"], f32)
    Wz = np.asarray(inp["L0_attn_Wz"], f32)
    Wb = np.asarray(inp["L0_attn_Wb"], f32)
    Wa = np.asarray(inp["L0_attn_Wa"], f32)
    bqkv = np.asarray(inp["L0_attn_bqkv"], f32)
    bz = np.asarray(inp["L0_attn_bz"], f32)
    bb = np.asarray(inp["L0_attn_bb"], f32)
    ba = np.asarray(inp["L0_attn_ba"], f32)
    convw = np.asarray(inp["L0_attn_conv_w"], f32)
    convb = np.asarray(inp["L0_attn_conv_b"], f32)
    dtb = np.asarray(inp["L0_attn_dt_bias"], f32)
    alog = np.asarray(inp["L0_attn_A_log"], f32)
    normw = np.asarray(inp["L0_attn_norm_w"], f32)
    Wout = np.asarray(inp["L0_attn_Wout"], f32)
    bout = np.asarray(inp["L0_attn_bout"], f32)
    gw = np.asarray(inp["L0_ffn_gate_w"], f32)
    uw = np.asarray(inp["L0_ffn_up_w"], f32)
    dw = np.asarray(inp["L0_ffn_down_w"], f32)
    gs = np.asarray(inp["L0_ffn_gate_scale"], f32)
    us = np.asarray(inp["L0_ffn_up_scale"], f32)
    ds = np.asarray(inp["L0_ffn_down_scale"], f32)

    Wcat = np.zeros((NIN * 128, HID), f32)
    Wcat[0:CONVD] = Wqkv
    Wcat[CONVD:CONVD + HID] = Wz
    Wcat[6144:6160] = Wb
    Wcat[6176:6192] = Wa
    WinT = Wcat.T
    w_in = np.ascontiguousarray(
        WinT.reshape(16, 128, NIN, 128).transpose(2, 1, 0, 3)
        .reshape(NIN, 128, 2, 1024).transpose(0, 2, 1, 3))
    b_cat = np.zeros((NIN * 128,), f32)
    b_cat[0:CONVD] = bqkv
    b_cat[CONVD:CONVD + HID] = bz
    b_cat[6144:6160] = bb * 0.0
    b_cat[6176:6192] = ba
    b_in = np.ascontiguousarray(b_cat.reshape(NIN, 128).T)

    conv_w_d = np.ascontiguousarray(
        convw.reshape(32, 128, KC).transpose(1, 0, 2).reshape(128, 32 * KC))
    conv_b_d = np.ascontiguousarray(convb.reshape(32, 128).T)
    small = np.zeros((16, 8), f32)
    small[:, 0] = alog
    small[:, 1] = dtb
    small[:, 2] = bb
    norm_w_d = normw.reshape(128, 1)

    WoutT = np.ascontiguousarray(Wout.T)
    wout_b = np.ascontiguousarray(
        WoutT.reshape(16, 128, 16, 128).transpose(2, 1, 0, 3)
        .reshape(16, 128, 2, 1024).transpose(0, 2, 1, 3))
    bout_c = np.ascontiguousarray(bout.reshape(16, 128).T)

    gate_b = np.ascontiguousarray(
        gw.reshape(16, 128, NI, 128).transpose(2, 1, 0, 3)
        .reshape(NI, 128, 2, 1024).transpose(0, 2, 1, 3))
    up_b = np.ascontiguousarray(
        uw.reshape(16, 128, NI, 128).transpose(2, 1, 0, 3)
        .reshape(NI, 128, 2, 1024).transpose(0, 2, 1, 3))
    down_b = np.ascontiguousarray(
        dw.reshape(NI, 128, 2, 1024).transpose(0, 2, 1, 3))
    g_sc = np.ascontiguousarray(gs.reshape(NI, 128).T)
    u_sc = np.ascontiguousarray(us.reshape(NI, 128).T)
    d_sc = np.ascontiguousarray(ds.reshape(16, 128).T)

    i128b = np.eye(128, dtype=bf16)
    i16f = np.eye(16, dtype=f32)
    selh = np.zeros((16, 16 * 128), f32)
    for h in range(16):
        selh[h, h * 128:(h + 1) * 128] = 1.0
    selhb = selh.astype(bf16)
    ones16f = np.ones((16, 128), f32)
    onesb = np.ones((128, 1), bf16)
    onesrb = np.ones((1, 128), bf16)
    onesrf = np.ones((1, 128), f32)
    jj, ff = np.mgrid[0:128, 0:128]
    mask_lo = (ff < jj).astype(bf16)
    mask_ue = (ff >= jj).astype(bf16)
    rmask = np.ones((16, TOK), f32)
    rmask[:, ::C] = 0.0
    e127 = np.zeros((128, 1), f32)
    e127[127, 0] = 1.0
    csts = np.zeros((1, 16), f32)
    csts[0, 0] = LN_EPS
    csts[0, 1] = 128.0
    csts[0, 2] = 1.0
    csts[0, 3] = 128.0 * L2_EPS
    csts[0, 4] = L2_EPS
    csts[0, 5] = -1.0
    csts[0, 6] = 1.0
    csts[0, 7] = 1.0 / 128.0
    csts[0, 8] = RMS_EPS
    cstc = np.broadcast_to(csts, (128, 16)).copy()

    shared = dict(w_in=w_in, b_in=b_in, conv_w=conv_w_d, conv_b=conv_b_d,
                  small=small, norm_w=norm_w_d, wout_b=wout_b, bout_c=bout_c,
                  gate_w=gate_b, up_w=up_b, down_in=down_b, g_scale=g_sc,
                  u_scale=u_sc, d_scale=d_sc, i128b=i128b, i16f=i16f,
                  selhb=selhb, selhf=selh, ones16f=ones16f, onesb=onesb,
                  onesrb=onesrb, onesrf=onesrf, mask_lo=mask_lo,
                  mask_ue=mask_ue, rmask=rmask, e127=e127, csts=csts, cstc=cstc)

    in_maps = []
    for c in range(8):
        b, half = c // 2, c % 2
        xm = np.ascontiguousarray(x[b, half * TOK:(half + 1) * TOK])
        if half == 1:
            xh = np.ascontiguousarray(x[b, TOK - HALO:TOK])
            fl = 1.0
        else:
            xh = np.zeros((HALO, HID), np.float32)
            fl = 0.0
        sel = np.zeros((1, 8), np.float32)
        if half == 1:
            sel[0, c - 1] = 1.0
        m = dict(shared)
        m.update(x_main=xm, x_halo=xh,
                 flag=np.full((1, 1), fl, np.float32), sel8=sel)
        in_maps.append(m)
    return in_maps


def kernel(**inputs):
    nc = _get_nc(debug=False)
    in_maps = _host_inputs(inputs)
    res = run_bass_kernel_spmd(nc, in_maps, core_ids=list(range(8)))
    out = np.empty((4, 2048, HID), np.float32)
    for c in range(8):
        b, half = c // 2, c % 2
        out[b, half * TOK:(half + 1) * TOK] = res.results[c]["out_x"]
    return out


if __name__ == "__main__":
    print("kernel module OK")
